# version 12
# baseline (speedup 1.0000x reference)
import numpy as np
import ml_dtypes
import concourse.bass as bass
import concourse.mybir as mybir
from concourse.bass_utils import run_bass_kernel_spmd

F32 = mybir.dt.float32
BF16 = mybir.dt.bfloat16
AF = mybir.ActivationFunctionType
ALU = mybir.AluOpType
AX = mybir.AxisListType
NPBF = ml_dtypes.bfloat16
ds = bass.ds

FUSED = True
C_RND = float(np.float32(1.5 * 2 ** 23))
EPS = 1e-5
CLIP_W = 1.4999999

EV_SEMS = ("ev", "eva1", "eva2", "evb", "evb2", "evb3", "evw", "sc")

def _ru(x, a=32):
    return (x + a - 1) // a * a

class SMap:
    def __init__(self):
        self.off = 0
        self.items = {}

    def add(self, name, shape, dt, at=None):
        nbytes = int(np.prod(shape[1:])) * mybir.dt.size(dt)
        o = self.off if at is None else at
        self.items[name] = (shape, dt, o)
        if at is None:
            self.off = _ru(o + nbytes)
        return o + nbytes

SBUF_BASE = 16512

def build_smap():
    m = SMap()
    m.off = SBUF_BASE
    m.add("dmat", [64, 1024], BF16)
    m.add("wxpT", [128, 8, 64], BF16)
    m.add("binp", [128, 8], F32)
    m.add("bxp", [64, 1], F32)
    m.add("gob", [128, 1024], F32)
    m.add("bob", [128, 1024], F32)
    m.add("outb", [128, 1024], F32)
    m.add("bzh", [128, 1024], F32)
    m.add("abc", [64, 512], F32)
    m.add("stats", [128, 96], F32)
    m.add("mws", [128, 8], F32)
    m.add("hcar", [64, 2], F32)
    m.add("onesr", [1, 128], F32)
    m.add("onec", [128, 1], F32)
    m.add("conek", [128, 1], BF16)
    m.add("ident128", [128, 128], BF16)
    m.add("ceps", [128, 1], F32)
    m.add("x", [128, 4, 1024], F32)
    m.add("scrx", [128, 2, 1024], F32)
    m.add("xs", [128, 1024], F32)
    m.add("q1xn", [128, 2, 1024], BF16)
    m.add("actfm", [128, 8, 512], BF16)
    m.add("silu", [128, 2, 512], F32)
    m.add("wgb", [128, 2, 8, 512], BF16)
    m.add("wub", [128, 2, 8, 512], BF16)
    m.add("wdb", [128, 3, 2, 1024], BF16)
    goff = m.off
    m.add("g", [128, 2, 4096], F32)
    q2off = m.add("q2tm", [128, 4096], BF16) - 8192
    fmoff = m.add("q2fmA", [128, 32, 256], BF16) - 16384
    m.add("q2fmB", [128, 32, 256], BF16)
    aend = m.off
    m.add("xs2", [128, 1024], F32, at=q2off)
    m.add("q2d", [128, 1024], BF16, at=q2off + 4096)
    m.add("q2t2", [128, 2, 4096], BF16, at=goff)
    m.add("woutb", [128, 8, 1024], BF16)
    m.add("yzfm", [128, 8, 512], BF16)
    m.budget = m.off
    m.add("winPb", [128, 8, 8, 128], BF16, at=goff)
    m.add("winZb", [128, 8, 1024], BF16, at=goff + 16384)
    m.add("xp", [128, 8, 512], BF16, at=goff + 32768)
    m.add("th", [128, 4, 1024], F32, at=goff + 40960)
    m.add("x1ts", [128, 2, 1024], F32, at=goff + 40960)
    m.add("ebuf", [128, 1024], F32, at=goff + 49152)
    m.add("u_s", [64, 512], F32, at=goff + 57344)
    m.add("H", [64, 512], F32, at=goff + 59392)
    m.add("Hb", [64, 512], BF16, at=goff + 61440)
    m.add("yztm", [128, 4, 1024], BF16, at=goff + 62464)
    assert goff + 62464 + 8192 <= aend
    xoff = m.items["x"][2]
    goff = m.items["g"][2]
    m.add("wstage", [128, 32, 1024], F32, at=xoff)
    po = xoff + 131072
    assert po <= goff + 45056 + 1
    m.add("pp_t1", [128, 1024], F32, at=po)
    m.add("pp_t2", [128, 1024], F32, at=po + 4096)
    m.add("rowst", [1, 2048], F32, at=po)
    m.add("pp_q4", [128, 4096], BF16, at=po + 8192)
    m.add("pp_abs", [128, 1024], BF16, at=po + 16384)
    m.add("pp_img", [128, 8, 512], BF16, at=po + 18432)
    assert po + 18432 + 8192 <= goff + 73728
    assert m.budget <= SBUF_BASE + 212800, m.budget
    return m

ENGS = ("sync", "tensor", "vector", "scalar", "gpsimd")

class B:
    def __init__(self):
        self.ops = {e: [] for e in ENGS}
        self.ctr = {}
        self.t = {}
        self.d = {}
        self.ps = None
        self.sems = {}

    def inc(self, name, amt=1):
        self.ctr[name] = self.ctr.get(name, 0) + amt
        return self.ctr[name]

    def val(self, name):
        return self.ctr.get(name, 0)

    def ejoin(self):
        return [(s, self.val(s)) for s in EV_SEMS if self.val(s) > 0]

    def emit(self, eng, fn, waits=None, inc=None, drain=True):
        post = []
        if inc is not None:
            incs = inc if isinstance(inc, list) else [inc]
            for (s, amt) in incs:
                post.append((s, amt, self.inc(s, amt)))
        self.ops[eng].append((fn, [(s, v) for (s, v) in (waits or []) if v > 0], post,
                              drain))

    PLAIN_DMA_SEMS = ("ld", "wd", "st", "out")

    def dma(self, out, in_, waits=None, sem="ld", transpose=False, eng="sync", slow=False):
        waits = list(waits or [])

        def fn(e, out=out, in_=in_, transpose=transpose, slow=slow):
            if transpose:
                return e.dma_start_transpose(out=out, in_=in_)
            if slow:
                return e.dma_start(out=out, in_=in_, allow_slow_non_contiguous=True)
            return e.dma_start(out=out, in_=in_)
        fn._is_dma = True
        self.emit(eng, fn, waits, (sem, 16))
        return self.ctr[sem]

    def v_op(self, fn, waits=None, sem="ev", drain=True):
        self.emit("vector", fn, waits, (sem, 1), drain=drain)
        return self.ctr[sem]

    def v_recip(self, out, in_, waits=None, sem="ev"):
        return self.v_op(lambda e, out=out, in_=in_: e.reciprocal(out, in_), waits, sem)

    def s_op(self, fn, waits=None, sem="sc", drain=True):
        self.emit("scalar", fn, waits, (sem, 1), drain=drain)
        return self.ctr[sem]

    def mm(self, out, lhsT, rhs, start, stop, waits=None, sem="mm", do_inc=None,
           extra_inc=None):
        def fn(e, out=out, lhsT=lhsT, rhs=rhs, start=start, stop=stop):
            return e.matmul(out, lhsT, rhs, start=start, stop=stop)
        if do_inc is None:
            do_inc = stop
        incs = [(sem, 1)] if do_inc else []
        if extra_inc is not None:
            incs.append(extra_inc)
        self.emit("tensor", fn, waits, incs if incs else None)
        return self.ctr.get(sem, 0)

def finalize(b, nc, sem_names):
    import contextlib
    stack = contextlib.ExitStack()
    for s in sem_names:
        b.sems[s] = stack.enter_context(nc.semaphore(s))
    block = stack.enter_context(nc.Block())

    drain_engs = ("vector", "scalar")

    def mk(eng_name):
        def body(eng):
            for fn, waits, post, drain_f in b.ops[eng_name]:
                for (s, v) in waits:
                    eng.wait_ge(b.sems[s], v)
                inst = fn(eng)
                if (eng_name in drain_engs and drain_f
                        and not getattr(fn, "_is_dma", False)):
                    tgt = eng.drain() if post else None
                    if tgt is None:
                        eng.drain()
                    for (s, amt, _) in post:
                        tgt.then_inc(b.sems[s], amt)
                else:
                    for (s, amt, _) in post:
                        inst.then_inc(b.sems[s], amt)
        return body
    block.sync(mk("sync"))
    block.tensor(mk("tensor"))
    block.vector(mk("vector"))
    block.scalar(mk("scalar"))
    block.gpsimd(mk("gpsimd"))
    stack.close()

SEMS = ["ld", "ld2", "tp", "st", "st2", "wd", "wdk", "mm", "mma1", "mma2", "mmb", "mmw",
        "ev", "eva1", "eva2", "evb", "evb2", "evb3", "evw", "sc", "cc", "out"]

def prepass_consts(b):
    t = b.t
    b.dma(t["ident128"][:], b.d["ident"])
    b.v_op(lambda e: e.memset(t["onesr"][:], 1.0))
    b.v_op(lambda e: e.memset(t["onec"][:], 1.0))
    b.v_op(lambda e: e.memset(t["conek"][:], 1.0))
    b.v_op(lambda e: e.memset(t["ceps"][:], EPS))
    b.v_op(lambda e: e.memset(t["hcar"][:], 0.0))

def prepass_ffn(b, w_ap, O, I, wt_s, mcol, gate=None):
    t = b.t
    is_gu = (I == 1024)
    if is_gu:
        src = w_ap.rearrange("(m p) i -> p m i", p=128)
        dst = t["wstage"][:]
    else:
        src = w_ap.rearrange("(m p) (c i) -> p m c i", p=128, i=1024)
        dst = t["wstage"][:].rearrange("p (m c) i -> p m c i", c=4)
    ld_sub = []
    for s in range(8):
        w = b.ejoin() if (gate is None and s == 0) else \
            ([("ev", gate[0][4 * s + 3])] if gate is not None else [])
        if is_gu:
            o, i_ = dst[:, 4 * s:4 * s + 4, :], src[:, 4 * s:4 * s + 4, :]
        else:
            o, i_ = dst[:, s, :, :], src[:, s, :, :]
        eng, sem = ("sync", "ld") if s % 2 == 0 else ("gpsimd", "ld2")
        v = b.dma(o, i_, waits=w, sem=sem, eng=eng)
        ld_sub.append((sem, v))
    for c in range(32):
        w = [ld_sub[c // 4]] if c % 4 == 0 else None
        if c == 0 and gate is not None:
            w = w + [("ev", gate[1])]
        b.s_op(lambda e, c=c: e.activation(t["pp_abs"][:], t["wstage"][:, c, :], AF.Abs,
                                           accum_out=t["stats"][:, 64 + c:65 + c]),
               waits=w, drain=(c == 31))
    sc_abs = b.val("sc")
    ev_statsfree = b.v_op(lambda e: e.reduce_sum(t["stats"][:, 60:61], t["stats"][:, 64:96], AX.X),
                          waits=[("sc", sc_abs)])
    ev = ev_statsfree
    mm0 = b.mm(b.ps[0:1, 0, 0:1], t["stats"][:, 60:61], t["onec"][:], True, True,
               waits=[("ev", ev)] + b.ejoin())
    ev = b.v_op(lambda e: e.tensor_scalar(t["rowst"][0:1, 0:1], b.ps[0:1, 0, 0:1],
                                          1.0 / (O * I), 1e-5, ALU.mult, ALU.max),
                waits=[("mm", mm0)])
    ev = b.v_recip(t["rowst"][0:1, 1:2], t["rowst"][0:1, 0:1])
    mm1 = b.mm(b.ps[:, 1, 0:2], t["onesr"][:], t["rowst"][0:1, 0:2], True, True,
               waits=[("ev", ev)])
    evm = b.v_op(lambda e: e.tensor_copy(t["mws"][:, mcol:mcol + 2], b.ps[:, 1, 0:2]),
                 waits=[("mm", mm1)])
    scol = t["mws"][:, mcol + 1:mcol + 2]
    ev = evm
    chunkfree = [None] * 32
    for m in range(O // 128):
        npc = I // 1024
        for pc in range(npc):
            c = m * npc + pc
            w1 = [("ev", evm), ("tp", b.val("tp"))] if (m == 0 and pc == 0) else None
            chunkfree[c] = b.v_op(
                lambda e, c=c: e.tensor_scalar(t["pp_t1"][:], t["wstage"][:, c, :],
                                               scol, CLIP_W, ALU.mult, ALU.min), waits=w1,
                drain=False)
            b.v_op(lambda e: e.tensor_scalar(t["pp_t2"][:], t["pp_t1"][:],
                                             -CLIP_W, C_RND, ALU.max, ALU.add), drain=False)
            ev = b.v_op(lambda e, pc=pc: e.tensor_single_scalar(
                t["pp_q4"][:, pc * 1024:(pc + 1) * 1024], t["pp_t2"][:], C_RND, ALU.subtract),
                waits=[("ev", b.val("ev")), ("st2", b.val("st2"))] if pc == 0 and m > 0 else None)
        if is_gu:
            toff = (m % 4) * 128
            evt = pe_transpose(b, lambda blk, toff=toff: t["pp_img"][:, blk, toff:toff + 128],
                               t["pp_q4"][:, 0:1024], 8, waits=[("ev", ev)])
            if m % 4 == 3:
                b.dma(wt_s[m // 4], t["pp_img"][:], waits=[("ev", evt)], sem="st2",
                      eng="scalar")
        else:
            img = bass.AP(t["pp_img"], 0, [[4096, 128], [128, 32], [1, 128]])
            evt = pe_transpose(b, lambda blk: bass.AP(t["pp_img"], blk * 128,
                                                      [[4096, 128], [1, 128]]),
                               t["pp_q4"][:], 32, waits=[("ev", ev)])
            b.dma(wt_s[:, :, m * 128:(m + 1) * 128].transpose([1, 0, 2]), img,
                  waits=[("ev", evt)], sem="st2", eng="scalar")
    return chunkfree, ev_statsfree

def prepass_attn(b, aps, need_producer, need_consumer):
    t = b.t
    if need_producer:
        for grp in range(2):
            for m0 in range(grp * 8, grp * 8 + 8, 2):
                ld = b.dma(t["wstage"][:, 0:2, :],
                           aps["W_in"][m0 * 128:(m0 + 2) * 128, :].rearrange("(m p) i -> p m i", p=128),
                           waits=[("ev", b.val("ev"))])
                evc = b.v_op(lambda e: e.tensor_copy(
                    t["pp_q4"][:].rearrange("p (m f) -> p m f", m=4)[:, 0:2, :],
                    t["wstage"][:, 0:2, :]), waits=[("ld", ld), ("tp", b.val("tp"))])
                for mm_ in range(2):
                    m = m0 + mm_
                    if grp == 0:
                        outap = b.d["winP_s"][:, m, :, :]
                    else:
                        outap = b.d["winZ_s"][:, :, (m - 8) * 128:(m - 7) * 128]
                    evt = pe_transpose(b, lambda blk: t["pp_img"][:, blk, 0:128],
                                       t["pp_q4"][:, mm_ * 1024:(mm_ + 1) * 1024], 8,
                                       waits=[("ev", evc), ("st", b.val("st"))])
                    b.dma(outap, t["pp_img"][:, :, 0:128], waits=[("ev", evt)], sem="st")
        ld = b.dma(t["wstage"][0:64, 0, :], aps["W_xp"], waits=[("ev", b.val("ev"))])
        evc = b.v_op(lambda e: e.tensor_copy(t["pp_q4"][0:64, 0:1024], t["wstage"][0:64, 0, :]),
                     waits=[("ld", ld), ("tp", b.val("tp"))])
        pe_transpose(b, lambda blk: t["wxpT"][:, blk, :],
                     t["pp_q4"][0:64, 0:1024], 8, waits=[("ev", evc)])
        ld = b.dma(t["wstage"][0:64, 1, :], aps["Dmat"])
        b.v_op(lambda e: e.tensor_single_scalar(t["dmat"][:], t["wstage"][0:64, 1, :],
                                                0.5, ALU.mult), waits=[("ld", ld)])
        b.dma(t["binp"][:], aps["b_in"][0:1024].rearrange("(m p) -> m p", p=128).rearrange("m p -> p m"), slow=True)
        b.dma(t["bxp"][:], aps["b_xp"][:, None], slow=True)
        ld = b.dma(t["stats"][0:64, 32:33], aps["A_log"].rearrange("a b p -> (a b) p").rearrange("a p -> p a"), slow=True)
        sc = b.s_op(lambda e: e.activation(t["stats"][0:64, 33:34], t["stats"][0:64, 32:33], AF.Exp),
                    waits=[("ld", ld)])
        ev = b.v_op(lambda e: e.tensor_single_scalar(t["stats"][0:64, 34:35],
                                                     t["stats"][0:64, 33:34], -1.0, ALU.mult),
                    waits=[("sc", sc)])
        sc = b.s_op(lambda e: e.activation(t["stats"][0:64, 35:36], t["stats"][0:64, 34:35], AF.Exp),
                    waits=[("ev", ev)])
        b.v_op(lambda e: e.memset(t["xs"][0:64, 0:512], 1.0))
        b.v_op(lambda e: e.tensor_scalar(t["abc"][:], t["xs"][0:64, 0:512],
                                         t["stats"][0:64, 35:36], None, ALU.mult),
               waits=[("sc", sc)])
    bcasts = []
    if need_producer:
        bcasts += [("gob", aps["gamma"], None), ("bob", aps["beta"], None),
                   ("bzh", aps["b_in"][1024:2048], 0.5)]
    if need_consumer:
        bcasts += [("outb", aps["b_out"], None)]
    for dstt, srcap, scale in bcasts:
        ld = b.dma(t["rowst"][0:1, 0:1024], srcap[None, :],
                   waits=[("mm", b.val("mm")), ("ev", b.val("ev"))])
        if scale is not None:
            ev = b.v_op(lambda e, scale=scale: e.tensor_single_scalar(
                t["rowst"][0:1, 1024:2048], t["rowst"][0:1, 0:1024], scale, ALU.mult),
                waits=[("ld", ld)])
            row = t["rowst"][0:1, 1024:2048]
        else:
            ev = b.val("ev")
            row = t["rowst"][0:1, 0:1024]
        for h in range(2):
            mmv = b.mm(b.ps[:, 2 + h, :], t["onesr"][:], row[:, h * 512:(h + 1) * 512], True, True,
                       waits=[("ld", ld), ("ev", ev), ("ev", b.val("ev"))])
            b.v_op(lambda e, dstt=dstt, h=h: e.tensor_copy(t[dstt][:, h * 512:(h + 1) * 512],
                                                           b.ps[:, 2 + h, :]),
                   waits=[("mm", mmv)])

def ln_stats_tc(b, S, tc, sq_scs, sc0, waits=None):
    t = b.t
    src = t["x"][:, tc, :]
    xst = t["xs2"] if tc % 2 else t["xs"]
    b.v_op(lambda e, src=src, tc=tc: e.reduce_sum(t["stats"][:, S + tc:S + tc + 1], src, AX.X),
           waits=list(waits or []))
    b.v_op(lambda e, tc=tc: e.tensor_single_scalar(
        t["stats"][:, S + 4 + tc:S + 5 + tc], t["stats"][:, S + tc:S + tc + 1],
        1.0 / 1024.0, ALU.mult))
    wxs = [("sc", sq_scs[-2])] if len(sq_scs) >= 2 else [("sc", sc0)]
    evx = b.v_op(lambda e, src=src, tc=tc, xst=xst: e.tensor_scalar(
        xst[:], src, t["stats"][:, S + 4 + tc:S + 5 + tc], None, ALU.subtract),
        waits=wxs)
    sq = b.s_op(lambda e, tc=tc, xst=xst: e.activation(t["q2d"][:], xst[:], AF.Square,
                                                       accum_out=t["stats"][:, S + 8 + tc:S + 9 + tc]),
                waits=[("ev", evx)])
    sq_scs.append(sq)
    b.v_op(lambda e, tc=tc, xst=xst: e.reduce_max(t["stats"][:, S + 12 + tc:S + 13 + tc], xst[:],
                                                  AX.X, apply_absolute_value=True))

def ln_quant(b, S, extra_w=None, out_guard=None):
    sq_scs = []
    sc0 = b.val("sc")
    for tc in range(4):
        ln_stats_tc(b, S, tc, sq_scs, sc0, waits=(extra_w if tc == 0 else None))
    return ln_quant_rest(b, S, out_guard=out_guard)

def ln_quant_rest(b, S, out_guard=None):
    t = b.t
    tps = []
    sc = b.s_op(lambda e: e.activation(t["stats"][:, S + 44:S + 48], t["stats"][:, S + 8:S + 12],
                                       AF.Ln, bias=t["ceps"][:, 0:1], scale=1.0 / 1024.0),
                waits=[("ev", b.val("ev"))])
    sc = b.s_op(lambda e: e.activation(t["stats"][:, S + 16:S + 20], t["stats"][:, S + 44:S + 48],
                                       AF.Exp, scale=-0.5))
    b.v_op(lambda e: e.tensor_tensor(t["stats"][:, S + 44:S + 48], t["stats"][:, S + 16:S + 20],
                                     t["stats"][:, S + 12:S + 16], ALU.mult), waits=[("sc", sc)])
    b.v_op(lambda e: e.tensor_single_scalar(t["stats"][:, S + 20:S + 24],
                                            t["stats"][:, S + 44:S + 48], 1e-5, ALU.max))
    b.v_recip(t["stats"][:, S + 24:S + 28], t["stats"][:, S + 20:S + 24])
    b.v_op(lambda e: e.tensor_tensor(t["stats"][:, S + 44:S + 48], t["stats"][:, S + 16:S + 20],
                                     t["stats"][:, S + 24:S + 28], ALU.mult))
    b.v_op(lambda e: e.tensor_single_scalar(t["stats"][:, S + 28:S + 32],
                                            t["stats"][:, S + 44:S + 48], 127.0, ALU.mult))
    b.v_op(lambda e: e.tensor_scalar(t["stats"][:, S + 32:S + 36], t["stats"][:, S + 20:S + 24],
                                     1.0 / 127.0, None, ALU.mult))
    b.v_op(lambda e: e.tensor_scalar(t["stats"][:, S + 36:S + 40], t["stats"][:, S + 32:S + 36],
                                     t["mws"][:, 0:1], None, ALU.mult))
    b.v_op(lambda e: e.tensor_scalar(t["stats"][:, S + 40:S + 44], t["stats"][:, S + 32:S + 36],
                                     t["mws"][:, 2:3], None, ALU.mult))
    evs = b.val("ev")
    for tc in range(4):
        src = t["x"][:, tc, :]
        w = [("ev", evs)] if tc == 0 else []
        if out_guard is not None and tc < 2:
            w = w + [out_guard]
        if len(tps) >= 2:
            w.append(("tp", tps[-2]))
        b.v_op(lambda e, src=src, tc=tc: e.tensor_scalar(
            t["xs"][:], src, t["stats"][:, S + 4 + tc:S + 5 + tc],
            t["stats"][:, S + 28 + tc:S + 29 + tc], ALU.subtract, ALU.mult), waits=w,
            drain=False)
        b.v_op(lambda e, tc=tc: e.tensor_single_scalar(
            t["scrx"][:, tc % 2, :], t["xs"][:], C_RND, ALU.add), drain=False)
        evq = b.v_op(lambda e, tc=tc: e.tensor_single_scalar(
            t["q1xn"][:, tc % 2, :], t["scrx"][:, tc % 2, :], C_RND, ALU.subtract))
        tp = b.dma(t["actfm"][:, :, tc * 128:(tc + 1) * 128], t["q1xn"][:, tc % 2, :],
                   waits=[("ev", evq)], sem="tp", transpose=True)
        tps.append(tp)
    return tps

def ffn_preload(b):
    t = b.t
    pre = {}
    pg = getattr(b, "prev_a1g", None)
    for gi in range(2):
        w = [("mma1", pg[14 + gi])] if pg is not None else \
            [("st2", getattr(b, "st2_gate", 0))]
        b.dma(t["wgb"][:, gi, :, :].rearrange("p k t -> p (k t)"),
              b.d["wgT_s"][gi].rearrange("p k t -> p (k t)"), waits=w)
        pre[gi] = b.dma(t["wub"][:, gi, :, :].rearrange("p k t -> p (k t)"),
                        b.d["wuT_s"][gi].rearrange("p k t -> p (k t)"))
    return pre

def ffn_phase(b, S, q1_tps, pre=None, after_pr0=None):
    t = b.t
    wgT_s, wuT_s, wdT_s = b.d["wgT_s"], b.d["wuT_s"], b.d["wdT_s"]
    q2tp = {}
    evr_tc = {}
    a1_groups = {}
    for pair in range(2):
        for n in range(8):
            par = n % 2
            gi = pair * 8 + n
            if pre is not None and gi < 2:
                ldu = pre[gi]
            else:
                w_ld = [("mma1", a1_groups[gi - 2])] if gi >= 2 else \
                    [("st2", getattr(b, "st2_gate", 0))]
                b.dma(t["wgb"][:, par, :, :].rearrange("p k t -> p (k t)"),
                      wgT_s[n].rearrange("p k t -> p (k t)"), waits=w_ld)
                ldu = b.dma(t["wub"][:, par, :, :].rearrange("p k t -> p (k t)"),
                            wuT_s[n].rearrange("p k t -> p (k t)"))
            for tcl in range(2):
                tc = pair * 2 + tcl
                pb = 2 * (gi * 2 + tcl) % 6
                w = [("ld", ldu)]
                if gi == 0:
                    w += [("tp", q1_tps[3])] + b.ejoin()
                prev_ev = b.val("eva1") - 2
                if prev_ev > 0:
                    w.append(("eva1", prev_ev))
                for k in range(8):
                    b.mm(b.ps[:, pb, :], t["actfm"][:, k, tc * 128:(tc + 1) * 128],
                         t["wgb"][:, par, k, :], k == 0, k == 7,
                         waits=w if k == 0 else None, sem="mma1")
                for k in range(8):
                    b.mm(b.ps[:, pb + 1, :], t["actfm"][:, k, tc * 128:(tc + 1) * 128],
                         t["wub"][:, par, k, :], k == 0, k == 7, sem="mma1")
                mmv = b.val("mma1")
                scv = b.s_op(lambda e, tcl=tcl, tc=tc, pb=pb: e.activation(
                    t["silu"][:, tcl, :], b.ps[:, pb, :], AF.Silu,
                    scale=t["stats"][:, S + 36 + tc:S + 37 + tc]),
                    waits=[("mma1", mmv - 1)], drain=False)
                b.v_op(lambda e, tcl=tcl, tc=tc, n=n, pb=pb: e.scalar_tensor_tensor(
                    t["g"][:, tcl, n * 512:(n + 1) * 512], b.ps[:, pb + 1, :],
                    t["stats"][:, S + 40 + tc:S + 41 + tc], t["silu"][:, tcl, :],
                    ALU.mult, ALU.mult),
                    waits=[("mma1", mmv), ("sc", scv)], sem="eva1", drain=False)
                a1_groups[gi] = mmv
        for tcl in range(2):
            tc = pair * 2 + tcl
            b.v_op(lambda e, tcl=tcl, tc=tc: e.reduce_max(
                t["stats"][:, S + 56 + tc:S + 57 + tc], t["g"][:, tcl, :], AX.X,
                apply_absolute_value=True))
            b.v_op(lambda e, tc=tc: e.tensor_single_scalar(
                t["stats"][:, S + 64 + tc:S + 65 + tc], t["stats"][:, S + 56 + tc:S + 57 + tc],
                1e-5, ALU.max))
            b.v_recip(t["stats"][:, S + 60 + tc:S + 61 + tc],
                      t["stats"][:, S + 64 + tc:S + 65 + tc])
            b.v_op(lambda e, tc=tc: e.tensor_single_scalar(
                t["stats"][:, S + 68 + tc:S + 69 + tc], t["stats"][:, S + 60 + tc:S + 61 + tc],
                127.0, ALU.mult))
            evsm = b.v_op(lambda e, tc=tc: e.tensor_scalar(
                t["stats"][:, S + 72 + tc:S + 73 + tc], t["stats"][:, S + 64 + tc:S + 65 + tc],
                t["mws"][:, 4:5], 0.5 / 127.0, ALU.mult, ALU.mult))
            evr = None
            src2 = t["q2tm"][:] if pair == 0 else t["q2t2"][:, tcl, :]
            for qq in range(4):
                sl = slice(qq * 1024, (qq + 1) * 1024)
                w = None
                if qq == 0:
                    w = [("ev", evsm)] + ([("tp", b.val("tp"))] if pair == 0 else [])
                b.v_op(lambda e, tcl=tcl, tc=tc, sl=sl: e.tensor_scalar(
                    t["xs"][:], t["g"][:, tcl, sl], t["stats"][:, S + 68 + tc:S + 69 + tc],
                    C_RND, ALU.mult, ALU.add), waits=w, drain=False)
                evr = b.v_op(lambda e, sl=sl, src2=src2: e.tensor_single_scalar(
                    src2[:, sl], t["xs"][:], C_RND, ALU.subtract))
            evr_tc[tc] = evr
            if pair == 0:
                q2tp[tc] = b.dma(t["q2fmA"][:, :, tc * 128:(tc + 1) * 128], src2,
                                 waits=[("ev", evr)], sem="tp", transpose=True)
    evA2 = {}
    wd_load0 = b.val("wdk")
    for pr in range(2):
        tc_e, tc_o = 2 * pr, 2 * pr + 1
        if pr == 1:
            for tc in (2, 3):
                q2tp[tc] = b.dma(t["q2fmB"][:, :, (tc - 2) * 128:(tc - 1) * 128],
                                 t["q2t2"][:, tc - 2, :],
                                 waits=[("ev", evr_tc[tc])], sem="tp", transpose=True)
        for ld_i in range(16):
            gl = b.val("wdk") - wd_load0
            w = []
            if pr == 0 and ld_i == 0:
                w = [("st2", getattr(b, "st2_gate", 0))]
            wdk_prev = b.val("wdk") - 2
            if wdk_prev > 0:
                w.append(("wdk", wdk_prev))
            buf = gl % 3
            b.dma(t["wdb"][:, buf, :, :], wdT_s[ld_i * 2:ld_i * 2 + 2].transpose([1, 0, 2]),
                  waits=w, sem="wd")
            if getattr(b, "wd_log", None) is not None:
                b.dma(b.wd_log[gl], t["wdb"][:, buf, :, :], sem="st")
            wdv = b.val("wd")
            for kk in range(2):
                k = ld_i * 2 + kk
                w2 = [("wd", wdv)] if kk == 0 else []
                if k == 0:
                    w2 += [("eva2", b.val("eva2")), ("tp", q2tp[2 * pr + 1])] + \
                        (b.ejoin() if pr == 0 else [])
                for ci, tc in enumerate((tc_e, tc_o)):
                    pb = 4 + 2 * ci
                    qfm = t["q2fmA"] if tc < 2 else t["q2fmB"]
                    for h in range(2):
                        b.mm(b.ps[:, pb + h, :], qfm[:, k, (tc % 2) * 128:(tc % 2 + 1) * 128],
                             t["wdb"][:, buf, kk, h * 512:(h + 1) * 512],
                             k == 0, k == 31,
                             waits=w2 if (ci == 0 and h == 0) else None, sem="mma2",
                             do_inc=(k == 31 and ci == 1 and h == 1))
            b.mm(b.ps[0:1, 0, 0:1], t["conek"][:], t["conek"][:], True, True,
                 sem="wdk", do_inc=True)
        mmv = b.val("mma2")
        for ci, tc in enumerate((tc_e, tc_o)):
            pb = 4 + 2 * ci
            for h in range(2):
                b.v_op(lambda e, tc=tc, h=h, pb=pb: e.scalar_tensor_tensor(
                    t["x"][:, tc, h * 512:(h + 1) * 512], b.ps[:, pb + h, :],
                    t["stats"][:, S + 72 + tc:S + 73 + tc],
                    t["x"][:, tc, h * 512:(h + 1) * 512], ALU.mult, ALU.add),
                    waits=[("mma2", mmv)], sem="eva2")
            evA2[tc] = b.val("eva2")
        if pr == 0 and after_pr0 is not None:
            after_pr0(evA2)
    b.prev_a1g = a1_groups
    return evA2

def lnb_front(b, S, tc, evA2_tc, sq_scs, sc0):
    t = b.t
    src = t["x"][:, tc, :]
    xst = t["xs2"] if tc % 2 else t["xs"]
    b.v_op(lambda e, src=src, tc=tc: e.reduce_sum(
        t["stats"][:, S + tc:S + tc + 1], src, AX.X), waits=[("eva2", evA2_tc)])
    b.v_op(lambda e, tc=tc: e.tensor_single_scalar(
        t["stats"][:, S + 4 + tc:S + 5 + tc], t["stats"][:, S + tc:S + tc + 1],
        1.0 / 1024.0, ALU.mult))
    wxs = [("sc", sq_scs[-2])] if len(sq_scs) >= 2 else [("sc", sc0)]
    evx = b.v_op(lambda e, src=src, tc=tc, xst=xst: e.tensor_scalar(
        xst[:], src, t["stats"][:, S + 4 + tc:S + 5 + tc], None, ALU.subtract),
        waits=wxs)
    sq = b.s_op(lambda e, tc=tc, xst=xst: e.activation(
        t["q2d"][:], xst[:], AF.Square,
        accum_out=t["stats"][:, S + 8 + tc:S + 9 + tc]), waits=[("ev", evx)])
    sq_scs.append(sq)
    b.s_op(lambda e, tc=tc: e.activation(t["stats"][:, S + 12 + tc:S + 13 + tc],
                                         t["stats"][:, S + 8 + tc:S + 9 + tc],
                                         AF.Ln, bias=t["ceps"][:, 0:1], scale=1.0 / 1024.0))
    return b.s_op(lambda e, tc=tc: e.activation(t["stats"][:, S + 16 + tc:S + 17 + tc],
                                                t["stats"][:, S + 12 + tc:S + 13 + tc],
                                                AF.Exp, scale=-0.5))

def lnb_backc(b, S, tc, sc_exp, w_tp):
    t = b.t
    src = t["x"][:, tc, :]
    xst = t["xs2"] if tc % 2 else t["xs"]
    ev_st = b.v_op(lambda e, src=src, tc=tc, xst=xst: e.scalar_tensor_tensor(
        src, xst[:], t["stats"][:, S + 16 + tc:S + 17 + tc], src,
        ALU.mult, ALU.add), waits=[("sc", sc_exp)])
    xn1 = t["scrx"][:, tc % 2, :]
    b.v_op(lambda e, xn1=xn1, tc=tc, xst=xst: e.scalar_tensor_tensor(
        xn1, xst[:], t["stats"][:, S + 16 + tc:S + 17 + tc], t["gob"][:],
        ALU.mult, ALU.mult), drain=False)
    evq = b.v_op(lambda e, xn1=xn1, tc=tc: e.tensor_tensor(
        t["q1xn"][:, tc % 2, :], xn1, t["bob"][:], ALU.add), waits=[w_tp])
    return ev_st, evq

def phase_b(b, S, tile, evA2, x1t_dst, yz_dst, lnb=None, mid_hook=None):
    t = b.t
    j = tile % 2
    if not lnb:
        lnb = {"sq_scs": [], "sc0": b.val("sc"), "exps": {}, "backs": {}}
        for tc in (0, 1):
            lnb["exps"][tc] = lnb_front(b, S, tc, evA2[tc], lnb["sq_scs"], lnb["sc0"])
        for tc in (0, 1):
            lnb["backs"][tc] = lnb_backc(b, S, tc, lnb["exps"][tc], ("tp", b.val("tp")))
    sq_scs, exps, backs = lnb["sq_scs"], lnb["exps"], lnb["backs"]
    xn_tps = []
    for tc in (0, 1):
        ev_st, evq = backs[tc]
        b.dma(x1t_dst(tc), t["x"][:, tc, :], waits=[("ev", ev_st)], sem="st")
        xn_tps.append(b.dma(t["actfm"][:, :, tc * 128:(tc + 1) * 128],
                            t["q1xn"][:, tc % 2, :],
                            waits=[("ev", evq)], sem="tp", transpose=True))
    ldp = b.dma(t["winPb"][:].rearrange("p m k c -> p (m k c)"),
                b.d["winP_s"][:].rearrange("p m k c -> p (m k c)"),
                waits=[("ev", b.val("ev")), ("eva1", b.val("eva1")),
                       ("tp", b.val("tp"))])
    for tc in (2, 3):
        exps[tc] = lnb_front(b, S, tc, evA2[tc], sq_scs, lnb["sc0"])
    for tc in (2, 3):
        ev_st, evq = lnb_backc(b, S, tc, exps[tc], ("tp", xn_tps[tc - 2]))
        b.dma(x1t_dst(tc), t["x"][:, tc, :], waits=[("ev", ev_st)], sem="st")
        xn_tps.append(b.dma(t["actfm"][:, :, tc * 128:(tc + 1) * 128],
                            t["q1xn"][:, tc % 2, :],
                            waits=[("ev", evq)], sem="tp", transpose=True))
    ldz = b.dma(t["winZb"][:].rearrange("p k z -> p (k z)"),
                b.d["winZ_s"][:].rearrange("p k z -> p (k z)"))
    if mid_hook is not None:
        mid_hook()
    for m in range(8):
        pb = m % 2
        if m == 0:
            w = [("ld", ldp), ("tp", xn_tps[3])] + b.ejoin()
        else:
            w = [("evb", max(0, b.val("evb") - 1))]
        for k in range(8):
            b.mm(b.ps[:, pb, :], t["winPb"][:, m, k, :], t["actfm"][:, k, :],
                 k == 0, k == 7, waits=w if k == 0 else None, sem="mmb")
        mmv = b.val("mmb")
        b.s_op(lambda e, m=m, pb=pb: e.activation(
            t["xp"][:, m, :], b.ps[:, pb, :], AF.Silu, bias=t["binp"][:, m:m + 1]),
            waits=[("mmb", mmv)], sem="evb")
    evb_xp = b.val("evb")
    for k in range(8):
        b.mm(b.ps[0:64, 2, :], t["wxpT"][:, k, :], t["xp"][:, k, :], k == 0, k == 7,
             waits=[("evb", evb_xp)] if k == 0 else None, sem="mmb")
    mmv = b.val("mmb")
    b.v_op(lambda e: e.tensor_scalar(t["u_s"][:], b.ps[0:64, 2, :], t["bxp"][:, 0:1],
                                     None, ALU.add), waits=[("mmb", mmv)], sem="evb")
    init = 0.0 if tile < 2 else t["hcar"][:, j:j + 1]
    b.v_op(lambda e, init=init: e.tensor_tensor_scan(
        t["H"][:], t["abc"][:], t["u_s"][:], init, ALU.mult, ALU.add), sem="evb")
    b.v_op(lambda e: e.tensor_copy(t["hcar"][:, j:j + 1], t["H"][:, 511:512]), sem="evb")
    ev_hb = b.v_op(lambda e: e.tensor_copy(t["Hb"][:], t["H"][:]), sem="evb")
    for tc in range(4):
        pbz = 3 + (tc % 2) * 2
        for k in range(8):
            w = []
            if k == 0:
                w = [("evb2", max(0, b.val("evb2") - 4))]
                if tc == 0:
                    w += [("ld", ldz), ("eva2", evA2[3]), ("eva1", b.val("eva1"))]
            b.mm(b.ps[:, pbz, :], t["actfm"][:, k, tc * 128:(tc + 1) * 128],
                 t["winZb"][:, k, 0:512], k == 0, k == 7,
                 waits=w if k == 0 else None, sem="mmb")
            b.mm(b.ps[:, pbz + 1, :], t["actfm"][:, k, tc * 128:(tc + 1) * 128],
                 t["winZb"][:, k, 512:1024], k == 0, k == 7, sem="mmb")
        mmz = b.val("mmb")
        for h in range(2):
            ev = b.v_op(lambda e, h=h, pbz=pbz: e.scalar_tensor_tensor(
                t["silu"][:, h, :], b.ps[:, pbz + h, :], 0.5,
                t["bzh"][:, h * 512:(h + 1) * 512], ALU.mult, ALU.add),
                waits=[("mmb", mmz)], sem="evb2", drain=False)
            b.s_op(lambda e, tc=tc, h=h: e.activation(
                t["th"][:, tc, h * 512:(h + 1) * 512], t["silu"][:, h, :], AF.Tanh),
                waits=[("evb2", ev)], sem="evb2")
    sc_th = b.val("evb2")
    ev3 = b.val("evb3")
    for tc in range(4):
        pby = tc % 2
        w = [("evb", ev_hb)]
        if tc == 0:
            w.append(("evb", ev_hb))
        if tc >= 2:
            w.append(("evb3", b.val("evb3") - 2))
        b.mm(b.ps[:, pby, :], t["Hb"][:, tc * 128:(tc + 1) * 128], t["dmat"][:, 0:512],
             True, True, waits=w, sem="mmb")
        mmy = b.val("mmb")
        b.v_op(lambda e, tc=tc, pby=pby: e.scalar_tensor_tensor(
            t["yztm"][:, tc, 0:512], t["th"][:, tc, 0:512], 1.0, b.ps[:, pby, :],
            ALU.add, ALU.mult), waits=[("mmb", mmy), ("evb2", sc_th)], sem="evb3",
            drain=False)
        b.mm(b.ps[:, pby, :], t["Hb"][:, tc * 128:(tc + 1) * 128], t["dmat"][:, 512:1024],
             True, True, waits=[("evb3", b.val("evb3"))], sem="mmb")
        mmy = b.val("mmb")
        b.v_op(lambda e, tc=tc, pby=pby: e.scalar_tensor_tensor(
            t["yztm"][:, tc, 512:1024], t["th"][:, tc, 512:1024], 1.0, b.ps[:, pby, :],
            ALU.add, ALU.mult), waits=[("mmb", mmy)], sem="evb3")
    st = b.dma(yz_dst, t["yztm"][:], waits=[("evb3", b.val("evb3"))], sem="st")
    return st

def pass2_wout(b, S, u, yz_srcs, x1t_src, first, tc_hook=None):
    t = b.t
    out_guard = b.val("out")
    if first:
        b.dma(t["woutb"][:].rearrange("p k o -> p (k o)"),
              b.d["woutT_s"][:].rearrange("p k o -> p (k o)"), sem="ld",
              waits=[("tp", b.val("tp")), ("ev", b.val("ev"))])
    w_wout = b.val("ld")
    if callable(yz_srcs):
        tp_yz = yz_srcs()
    else:
        tp_yz = None
        for jj in range(2):
            w = [("mmw", b.val("mmw"))]
            tp_yz = b.dma(t["yzfm"][:, :, jj * 256:(jj + 1) * 256], yz_srcs[jj],
                          waits=w, sem="tp", transpose=True)
    evw = {}
    for tc in range(4):
        par = tc % 2
        ld_x1 = b.dma(t["x1ts"][:, par, :], x1t_src(tc),
                      waits=[("evw", max(0, b.val("evw") - 4)), ("st", b.val("st")),
                             ("evb3", b.val("evb3")), ("eva2", b.val("eva2"))], sem="ld")
        pb = par * 2
        for k in range(8):
            w = []
            if k == 0:
                w = [("tp", tp_yz), ("ld", w_wout)]
                if tc == 0:
                    w += b.ejoin()
                prev = b.val("evw") - 4
                if prev > 0:
                    w.append(("evw", prev))
            b.mm(b.ps[:, pb, :], t["yzfm"][:, k, tc * 128:(tc + 1) * 128],
                 t["woutb"][:, k, 0:512], k == 0, k == 7, waits=w if k == 0 else None, sem="mmw")
            b.mm(b.ps[:, pb + 1, :], t["yzfm"][:, k, tc * 128:(tc + 1) * 128],
                 t["woutb"][:, k, 512:1024], k == 0, k == 7, sem="mmw")
        mmv = b.val("mmw")
        for h in range(2):
            b.v_op(lambda e, h=h, pb=pb: e.tensor_tensor(
                t["ebuf"][:, h * 512:(h + 1) * 512], b.ps[:, pb + h, :],
                t["outb"][:, h * 512:(h + 1) * 512], ALU.add),
                waits=[("mmw", mmv), ("ld", ld_x1)], sem="evw", drain=False)
            b.v_op(lambda e, tc=tc, h=h, par=par: e.tensor_tensor(
                t["x"][:, tc, h * 512:(h + 1) * 512], t["x1ts"][:, par, h * 512:(h + 1) * 512],
                t["ebuf"][:, h * 512:(h + 1) * 512], ALU.add),
                waits=[("out", out_guard)], sem="evw")
        evw[tc] = b.val("evw")
        if tc_hook is not None:
            tc_hook(tc)
    return evw

def weight_aps(nc):
    aps = {}
    names = [("w_gate", [4096, 1024]), ("w_up", [4096, 1024]), ("w_down", [1024, 4096]),
             ("W_in", [2048, 1024]), ("W_xp", [64, 1024]), ("W_out", [1024, 1024]),
             ("Dmat", [64, 1024]), ("A_log", [1, 1, 64]), ("b_in", [2048]), ("b_xp", [64]),
             ("b_out", [1024]), ("gamma", [1024]), ("beta", [1024])]
    for nm, shp in names:
        aps[nm] = nc.dram_tensor(nm, shp, F32, kind="ExternalInput").ap()
    return aps

def dram_scratch(b, nc):
    b.d["wgT_s"] = nc.dram_tensor("wgT_s", [8, 128, 8, 512], BF16).ap()
    b.d["wuT_s"] = nc.dram_tensor("wuT_s", [8, 128, 8, 512], BF16).ap()
    b.d["wdT_s"] = nc.dram_tensor("wdT_s", [32, 128, 1024], BF16).ap()
    b.d["winP_s"] = nc.dram_tensor("winP_s", [128, 8, 8, 128], BF16).ap()
    b.d["winZ_s"] = nc.dram_tensor("winZ_s", [128, 8, 1024], BF16).ap()
    b.d["woutT_s"] = nc.dram_tensor("woutT_s", [128, 8, 1024], BF16).ap()

def setup_b(nc):
    smap = build_smap()
    b = B()
    b.t = {nm: nc.alloc_sbuf_tensor_at(nm, shp, dt, offset=off)
           for nm, (shp, dt, off) in smap.items.items()}
    b.ps = nc.alloc_psum_tensor("ps", [128, 8, 512], F32)
    ident = nc.inline_tensor(np.eye(128, dtype=NPBF), name="ident128")
    b.d["ident"] = ident.ap()
    dram_scratch(b, nc)
    return b

def pe_transpose(b, dst, src_ap, nblk, waits=None, sem="ev"):
    t = b.t
    ev = None
    for blk in range(nblk):
        pb = 2 + blk % 2
        w = list(waits or []) if blk == 0 else []
        prev = b.val("ev") - 1
        if blk >= 2:
            w.append(("ev", b.val("ev") - 1))
        b.mm(b.ps[:, pb, 0:128], src_ap[:, blk * 128:(blk + 1) * 128], t["ident128"][:],
             True, True, waits=w, sem="mm") if False else None
        P = src_ap.shape[0]
        def fn(e, pb=pb, blk=blk, P=P):
            return e.transpose(b.ps[:, pb, 0:P // 2].bitcast(BF16),
                               src_ap[:, blk * 128:(blk + 1) * 128],
                               t["ident128"][0:P, 0:P])
        b.emit("tensor", fn, w, ("mm", 1))
        mmv = b.val("mm")
        ev = b.v_op(lambda e, pb=pb, blk=blk, P=P: e.tensor_copy(
            dst(blk), b.ps[:, pb, 0:P // 2].bitcast(BF16)), waits=[("mm", mmv)], drain=False)
    return ev

def emit_pass1(b, x_in, x1t_out, yz_out, quarter_hook=None):
    t = b.t

    def emit_ldx(tile, w):
        q, j = tile // 2, tile % 2
        return b.dma(t["x"][:], x_in[j, q * 512:q * 512 + 512, :]
                     .rearrange("(c p) f -> p c f", p=128), waits=w)

    nxt = {}
    for tile in range(8):
        q, j = tile // 2, tile % 2
        l0 = q * 512
        if tile == 0:
            ldx = emit_ldx(0, [("tp", b.val("tp")), ("ev", b.val("ev")), ("sc", b.val("sc"))])
            pre = ffn_preload(b)
            sq_scs, sc0 = [], b.val("sc")
            for tc in range(4):
                ln_stats_tc(b, 0, tc, sq_scs, sc0, waits=([("ld", ldx)] if tc == 0 else None))
        else:
            pre = nxt["pre"]
        x1v = x1t_out[j].rearrange("(c p) f -> p c f", p=128)
        q1_tps = ln_quant_rest(b, 0)
        lnb = {}

        def after_pr0(evA2, lnb=lnb):
            lnb.update(sq_scs=[], sc0=b.val("sc"), exps={}, backs={})
            for tc in (0, 1):
                lnb["exps"][tc] = lnb_front(b, 0, tc, evA2[tc], lnb["sq_scs"], lnb["sc0"])
            for tc in (0, 1):
                lnb["backs"][tc] = lnb_backc(b, 0, tc, lnb["exps"][tc],
                                             ("tp", b.val("tp")))
        evA2 = ffn_phase(b, 0, q1_tps, pre=pre, after_pr0=after_pr0)

        def mid_hook(tile=tile):
            if tile == 7:
                return
            ldx2 = emit_ldx(tile + 1, [("st", b.val("st"))])
            nxt["pre"] = ffn_preload(b)
            sq_scs, sc0 = [], b.val("sc")
            for tc in range(4):
                ln_stats_tc(b, 0, tc, sq_scs, sc0, waits=([("ld", ldx2)] if tc == 0 else None))
        phase_b(b, 0, tile, evA2,
                x1t_dst=lambda tc, x1v=x1v, l0=l0: x1v[:, (l0 // 128) + tc, :],
                yz_dst=yz_out[q, j].rearrange("(c p) f -> p c f", p=128), lnb=lnb,
                mid_hook=mid_hook)
        if quarter_hook is not None and tile % 2 == 1:
            quarter_hook(tile // 2)
    b.emit("sync", lambda e: e.nop(), [("st", b.val("st")), ("out", b.val("out"))], None)

def emit_pass2(b, yz_in_fn, x1t_src_fn, y_out_fn):
    t = b.t
    for u in range(8):
        sq_scs, sc0 = [], b.val("sc")

        def tc_hook(tc, sq_scs=sq_scs, sc0=sc0):
            ln_stats_tc(b, 0, tc, sq_scs, sc0)
        evw = pass2_wout(b, 0, u, yz_in_fn(u), lambda tc, u=u: x1t_src_fn(u, tc),
                         first=(u == 0), tc_hook=tc_hook)
        pre = ffn_preload(b)
        q1_tps = ln_quant_rest(b, 0, out_guard=("out", b.val("out")))
        evA2 = ffn_phase(b, 0, q1_tps, pre=pre)
        for tc in range(4):
            b.dma(y_out_fn(u, tc), t["x"][:, tc, :], waits=[("eva2", evA2[tc])], sem="out")
    b.emit("sync", lambda e: e.nop(), [("out", b.val("out"))], None)

def _build_p1():
    nc = bass.Bass("TRN2", target_bir_lowering=False)
    b = setup_b(nc)
    aps = weight_aps(nc)
    x_in = nc.dram_tensor("x", [2, 2048, 1024], F32, kind="ExternalInput").ap()
    x1t_out = nc.dram_tensor("x1t", [2, 2048, 1024], F32, kind="ExternalOutput").ap()
    yz_out = nc.dram_tensor("yz", [4, 2, 512, 1024], BF16, kind="ExternalOutput").ap()
    prepass_consts(b)
    g1 = prepass_ffn(b, aps["w_gate"], 4096, 1024, b.d["wgT_s"], 0)
    g2 = prepass_ffn(b, aps["w_up"], 4096, 1024, b.d["wuT_s"], 2, gate=g1)
    prepass_ffn(b, aps["w_down"], 1024, 4096, b.d["wdT_s"], 4, gate=g2)
    b.st2_gate = b.val("st2")
    prepass_attn(b, aps, need_producer=True, need_consumer=False)
    emit_pass1(b, x_in, x1t_out, yz_out)
    finalize(b, nc, SEMS)
    return nc

def _prepass_wout(b, aps):
    t = b.t
    for mp in range(0, 8, 2):
        ld = b.dma(t["wstage"][:, 0:2, :],
                   aps["W_out"][mp * 128:(mp + 2) * 128, :].rearrange("(m p) i -> p m i", p=128),
                   waits=[("ev", b.val("ev"))])
        evc = b.v_op(lambda e: e.tensor_copy(
            t["pp_q4"][:].rearrange("p (m f) -> p m f", m=4)[:, 0:2, :],
            t["wstage"][:, 0:2, :]), waits=[("ld", ld), ("tp", b.val("tp"))])
        for mm_ in range(2):
            m = mp + mm_
            evt = pe_transpose(b, lambda blk: t["pp_img"][:, blk, 0:128],
                               t["pp_q4"][:, mm_ * 1024:(mm_ + 1) * 1024], 8,
                               waits=[("ev", evc), ("st", b.val("st"))])
            b.dma(b.d["woutT_s"][:, :, m * 128:(m + 1) * 128], t["pp_img"][:, :, 0:128],
                  waits=[("ev", evt)], sem="st")

def _build_p2():
    nc = bass.Bass("TRN2", target_bir_lowering=False)
    b = setup_b(nc)
    aps = weight_aps(nc)
    x1t_in = nc.dram_tensor("x1t", [2, 2048, 1024], F32, kind="ExternalInput").ap()
    yz_in = nc.dram_tensor("yzin", [8, 2, 256, 1024], BF16, kind="ExternalInput").ap()
    y_out = nc.dram_tensor("y", [2, 2048, 1024], F32, kind="ExternalOutput").ap()
    prepass_consts(b)
    g1 = prepass_ffn(b, aps["w_gate"], 4096, 1024, b.d["wgT_s"], 0)
    g2 = prepass_ffn(b, aps["w_up"], 4096, 1024, b.d["wuT_s"], 2, gate=g1)
    prepass_ffn(b, aps["w_down"], 1024, 4096, b.d["wdT_s"], 4, gate=g2)
    b.st2_gate = b.val("st2")
    prepass_attn(b, aps, need_producer=False, need_consumer=True)
    _prepass_wout(b, aps)

    def yz_src(u):
        return [yz_in[u, jj] for jj in range(2)]

    def x1t_src(u, tc):
        jj, ih = tc // 2, tc % 2
        b2 = 2 * u + jj
        return bass.AP(x1t_in.tensor, (ih * 2048 + b2) * 1024, [[16 * 1024, 128], [1, 1024]])

    def y_dst(u, tc):
        jj, ih = tc // 2, tc % 2
        b2 = 2 * u + jj
        return bass.AP(y_out.tensor, (ih * 2048 + b2) * 1024, [[16 * 1024, 128], [1, 1024]])

    emit_pass2(b, yz_src, x1t_src, y_dst)
    finalize(b, nc, SEMS)
    return nc

def _build_fused():
    nc = bass.Bass("TRN2", target_bir_lowering=False, num_devices=8)
    b = setup_b(nc)
    aps = weight_aps(nc)
    t = b.t
    x_in = nc.dram_tensor("x", [2, 2048, 1024], F32, kind="ExternalInput").ap()
    y_out = nc.dram_tensor("y", [2, 2048, 1024], F32, kind="ExternalOutput").ap()
    x1t_s = nc.dram_tensor("x1t_s", [2, 2048, 1024], F32, kind="Internal").ap()
    yz_send = nc.dram_tensor("yz_send", [4, 2, 512, 1024], BF16, kind="Internal").ap()
    gat = nc.dram_tensor("gat", [4, 8, 2, 512, 1024], BF16, kind="Internal").ap()
    prepass_consts(b)
    g1 = prepass_ffn(b, aps["w_gate"], 4096, 1024, b.d["wgT_s"], 0)
    g2 = prepass_ffn(b, aps["w_up"], 4096, 1024, b.d["wuT_s"], 2, gate=g1)
    prepass_ffn(b, aps["w_down"], 1024, 4096, b.d["wdT_s"], 4, gate=g2)
    b.st2_gate = b.val("st2")
    prepass_attn(b, aps, need_producer=True, need_consumer=True)
    _prepass_wout(b, aps)

    emit_pass1(b, x_in, x1t_s, yz_send, quarter_hook=None)
    for q in range(4):
        def ag_fn(g, q=q):
            return g.collective_compute(
                "AllGather", mybir.AluOpType.bypass,
                replica_groups=[list(range(NCORE))],
                ins=[yz_send[q]], outs=[gat[q]])
        b.emit("gpsimd", ag_fn,
               [("st", b.val("st")), ("tp", b.val("tp")), ("evb3", b.val("evb3"))],
               ("cc", 1))

    b.dyn = {}

    def rank_fn(g):
        rank = g.compute_val(g.partition_id())
        b.dyn["rq"] = nc.s_assert_within(rank // 2, 0, 3)
        b.dyn["rm256"] = nc.s_assert_within((rank % 2) * 256, 0, 256)
        return g.engine_nop()
    b.emit("gpsimd", rank_fn, [("cc", 4)], ("cc", 1))

    def yz_loader(u):
        def load():
            for jj in range(2):
                def fn(g, u=u, jj=jj):
                    in_ = gat[ds(b.dyn["rq"], 1), u, jj, ds(b.dyn["rm256"], 256), :]
                    in_ = in_.squeeze(0).rearrange("(c p) f -> p c f", p=128)
                    return g.dma_start(out=t["yztm"][:, jj * 2:jj * 2 + 2, :], in_=in_)
                w = [("cc", 5), ("tp", b.val("tp")), ("evb3", b.val("evb3")),
                     ("st", b.val("st")), ("mmw", b.val("mmw")),
                     ("mma2", b.val("mma2")), ("eva2", b.val("eva2"))]
                fn._is_dma = True
                b.emit("gpsimd", fn, w, ("ld", 16))
            ldv = b.ctr["ld"]
            tp = None
            for tc in range(4):
                tp = b.dma(t["yzfm"][:, :, tc * 128:(tc + 1) * 128], t["yztm"][:, tc, :],
                           waits=[("ld", ldv)], sem="tp", transpose=True)
            return tp
        return load

    def x1t_src(u, tc):
        jj, ih = tc // 2, tc % 2
        b2 = 2 * u + jj
        return bass.AP(x1t_s.tensor, (ih * 2048 + b2) * 1024, [[16 * 1024, 128], [1, 1024]])

    def y_dst(u, tc):
        jj, ih = tc // 2, tc % 2
        b2 = 2 * u + jj
        return bass.AP(y_out.tensor, (ih * 2048 + b2) * 1024, [[16 * 1024, 128], [1, 1024]])

    emit_pass2(b, yz_loader, x1t_src, y_dst)
    finalize(b, nc, SEMS)
    return nc

NCORE = 8

def kernel(L0_attn_x, L0_attn_A_log, L0_attn_Dmat, L0_attn_W_in, L0_attn_b_in,
           L0_attn_W_xp, L0_attn_b_xp, L0_attn_W_out, L0_attn_b_out, L0_attn_gamma,
           L0_attn_beta, L0_ffn_w_gate, L0_ffn_w_up, L0_ffn_w_down):
    wmap = dict(
        w_gate=np.ascontiguousarray(L0_ffn_w_gate, dtype=np.float32),
        w_up=np.ascontiguousarray(L0_ffn_w_up, dtype=np.float32),
        w_down=np.ascontiguousarray(L0_ffn_w_down, dtype=np.float32),
        W_in=np.ascontiguousarray(L0_attn_W_in, dtype=np.float32),
        W_xp=np.ascontiguousarray(L0_attn_W_xp, dtype=np.float32),
        W_out=np.ascontiguousarray(L0_attn_W_out, dtype=np.float32),
        Dmat=np.ascontiguousarray(L0_attn_Dmat, dtype=np.float32),
        A_log=np.ascontiguousarray(L0_attn_A_log, dtype=np.float32),
        b_in=np.ascontiguousarray(L0_attn_b_in, dtype=np.float32),
        b_xp=np.ascontiguousarray(L0_attn_b_xp, dtype=np.float32),
        b_out=np.ascontiguousarray(L0_attn_b_out, dtype=np.float32),
        gamma=np.ascontiguousarray(L0_attn_gamma, dtype=np.float32),
        beta=np.ascontiguousarray(L0_attn_beta, dtype=np.float32),
    )
    x = np.ascontiguousarray(L0_attn_x, dtype=np.float32)
    if FUSED:
        nc = _build_fused()
        in_maps = [dict(wmap, x=np.ascontiguousarray(x[2 * c:2 * c + 2])) for c in range(8)]
        res = run_bass_kernel_spmd(nc, in_maps, core_ids=list(range(8)))
        out = np.stack([np.asarray(res.results[c]["y"]) for c in range(8)])
        return out.reshape(16, 2048, 1024)
    nc1 = _build_p1()
    in_maps = [dict(wmap, x=np.ascontiguousarray(x[2 * c:2 * c + 2])) for c in range(8)]
    res1 = run_bass_kernel_spmd(nc1, in_maps, core_ids=list(range(8)))
    x1t = [res1.results[c]["x1t"] for c in range(8)]
    yz = np.stack([np.asarray(res1.results[c]["yz"]).view(NPBF) if res1.results[c]["yz"].dtype != NPBF
                   else res1.results[c]["yz"] for c in range(8)])
    nc2 = _build_p2()
    in_maps2 = []
    for c in range(8):
        yzin = np.ascontiguousarray(yz[:, c // 2, :, (c % 2) * 256:(c % 2) * 256 + 256, :])
        in_maps2.append(dict(wmap, x1t=np.ascontiguousarray(x1t[c]), yzin=yzin))
    res2 = run_bass_kernel_spmd(nc2, in_maps2, core_ids=list(range(8)))
    out = np.concatenate([np.asarray(res2.results[c]["y"])[None] for c in range(8)], axis=0)
    return out.reshape(16, 2048, 1024)


# version 13
# speedup vs baseline: 1.0461x; 1.0461x over previous
import numpy as np
import ml_dtypes
import concourse.bass as bass
import concourse.mybir as mybir
from concourse.bass_utils import run_bass_kernel_spmd

F32 = mybir.dt.float32
BF16 = mybir.dt.bfloat16
AF = mybir.ActivationFunctionType
ALU = mybir.AluOpType
AX = mybir.AxisListType
NPBF = ml_dtypes.bfloat16
ds = bass.ds

FUSED = True
C_RND = float(np.float32(1.5 * 2 ** 23))
EPS = 1e-5
CLIP_W = 1.4999999

EV_SEMS = ("ev", "eva1", "eva2", "evb", "evb2", "evb3", "evw", "sc")

def _ru(x, a=32):
    return (x + a - 1) // a * a

class SMap:
    def __init__(self):
        self.off = 0
        self.items = {}

    def add(self, name, shape, dt, at=None):
        nbytes = int(np.prod(shape[1:])) * mybir.dt.size(dt)
        o = self.off if at is None else at
        self.items[name] = (shape, dt, o)
        if at is None:
            self.off = _ru(o + nbytes)
        return o + nbytes

SBUF_BASE = 16512

def build_smap():
    m = SMap()
    m.off = SBUF_BASE
    m.add("dmat", [64, 1024], BF16)
    m.add("wxpT", [128, 8, 64], BF16)
    m.add("binp", [128, 8], F32)
    m.add("bxp", [64, 1], F32)
    m.add("gob", [128, 1024], F32)
    m.add("bob", [128, 1024], F32)
    m.add("outb", [128, 1024], F32)
    m.add("bzh", [128, 1024], F32)
    m.add("abc", [64, 512], F32)
    m.add("stats", [128, 96], F32)
    m.add("mws", [128, 8], F32)
    m.add("hcar", [64, 2], F32)
    m.add("onesr", [1, 128], F32)
    m.add("onec", [128, 1], F32)
    m.add("conek", [128, 1], BF16)
    m.add("ident128", [128, 128], BF16)
    m.add("ceps", [128, 1], F32)
    m.add("x", [128, 4, 1024], F32)
    m.add("scrx", [128, 2, 1024], F32)
    m.add("xs", [128, 1024], F32)
    m.add("q1xn", [128, 2, 1024], BF16)
    m.add("actfm", [128, 8, 512], BF16)
    m.add("silu", [128, 2, 512], F32)
    m.add("wgb", [128, 2, 8, 512], BF16)
    m.add("wub", [128, 2, 8, 512], BF16)
    m.add("wdb", [128, 3, 2, 1024], BF16)
    goff = m.off
    m.add("g", [128, 2, 4096], F32)
    q2off = m.add("q2tm", [128, 4096], BF16) - 8192
    fmoff = m.add("q2fmA", [128, 32, 256], BF16) - 16384
    m.add("q2fmB", [128, 32, 256], BF16)
    aend = m.off
    m.add("xs2", [128, 1024], F32, at=q2off)
    m.add("q2d", [128, 1024], BF16, at=q2off + 4096)
    m.add("q2t2", [128, 2, 4096], BF16, at=goff)
    m.add("woutb", [128, 8, 1024], BF16)
    m.add("yzfm", [128, 8, 512], BF16)
    m.budget = m.off
    m.add("winPb", [128, 8, 8, 128], BF16, at=goff)
    m.add("winZb", [128, 8, 1024], BF16, at=goff + 16384)
    m.add("xp", [128, 8, 512], BF16, at=goff + 32768)
    m.add("th", [128, 4, 1024], F32, at=goff + 40960)
    m.add("x1ts", [128, 2, 1024], F32, at=goff + 40960)
    m.add("ebuf", [128, 1024], F32, at=goff + 49152)
    m.add("u_s", [64, 512], F32, at=goff + 57344)
    m.add("H", [64, 512], F32, at=goff + 59392)
    m.add("Hb", [64, 512], BF16, at=goff + 61440)
    m.add("yztm", [128, 4, 1024], BF16, at=goff + 62464)
    assert goff + 62464 + 8192 <= aend
    xoff = m.items["x"][2]
    goff = m.items["g"][2]
    m.add("wstage", [128, 32, 1024], F32, at=xoff)
    po = xoff + 131072
    assert po <= goff + 45056 + 1
    m.add("pp_t1", [128, 1024], F32, at=po)
    m.add("pp_t2", [128, 1024], F32, at=po + 4096)
    m.add("rowst", [1, 2048], F32, at=po)
    m.add("pp_q4", [128, 4096], BF16, at=po + 8192)
    m.add("pp_abs", [128, 1024], BF16, at=po + 16384)
    m.add("pp_img", [128, 8, 512], BF16, at=po + 18432)
    assert po + 18432 + 8192 <= goff + 73728
    assert m.budget <= SBUF_BASE + 212800, m.budget
    return m

ENGS = ("sync", "tensor", "vector", "scalar", "gpsimd")

class B:
    def __init__(self):
        self.ops = {e: [] for e in ENGS}
        self.ctr = {}
        self.t = {}
        self.d = {}
        self.ps = None
        self.sems = {}

    def inc(self, name, amt=1):
        self.ctr[name] = self.ctr.get(name, 0) + amt
        return self.ctr[name]

    def val(self, name):
        return self.ctr.get(name, 0)

    def ejoin(self):
        return [(s, self.val(s)) for s in EV_SEMS if self.val(s) > 0]

    def emit(self, eng, fn, waits=None, inc=None, drain=True):
        post = []
        if inc is not None:
            incs = inc if isinstance(inc, list) else [inc]
            for (s, amt) in incs:
                post.append((s, amt, self.inc(s, amt)))
        self.ops[eng].append((fn, [(s, v) for (s, v) in (waits or []) if v > 0], post,
                              drain))

    PLAIN_DMA_SEMS = ("ld", "wd", "st", "out")

    def dma(self, out, in_, waits=None, sem="ld", transpose=False, eng="sync", slow=False):
        waits = list(waits or [])

        def fn(e, out=out, in_=in_, transpose=transpose, slow=slow):
            if transpose:
                return e.dma_start_transpose(out=out, in_=in_)
            if slow:
                return e.dma_start(out=out, in_=in_, allow_slow_non_contiguous=True)
            return e.dma_start(out=out, in_=in_)
        fn._is_dma = True
        self.emit(eng, fn, waits, (sem, 16))
        return self.ctr[sem]

    def v_op(self, fn, waits=None, sem="ev", drain=True):
        self.emit("vector", fn, waits, (sem, 1), drain=drain)
        return self.ctr[sem]

    def v_recip(self, out, in_, waits=None, sem="ev"):
        return self.v_op(lambda e, out=out, in_=in_: e.reciprocal(out, in_), waits, sem)

    def s_op(self, fn, waits=None, sem="sc", drain=True):
        self.emit("scalar", fn, waits, (sem, 1), drain=drain)
        return self.ctr[sem]

    def mm(self, out, lhsT, rhs, start, stop, waits=None, sem="mm", do_inc=None,
           extra_inc=None):
        def fn(e, out=out, lhsT=lhsT, rhs=rhs, start=start, stop=stop):
            return e.matmul(out, lhsT, rhs, start=start, stop=stop)
        if do_inc is None:
            do_inc = stop
        incs = [(sem, 1)] if do_inc else []
        if extra_inc is not None:
            incs.append(extra_inc)
        self.emit("tensor", fn, waits, incs if incs else None)
        return self.ctr.get(sem, 0)

def finalize(b, nc, sem_names):
    import contextlib
    stack = contextlib.ExitStack()
    for s in sem_names:
        b.sems[s] = stack.enter_context(nc.semaphore(s))
    block = stack.enter_context(nc.Block())

    drain_engs = ("vector", "scalar")

    def mk(eng_name):
        def body(eng):
            for fn, waits, post, drain_f in b.ops[eng_name]:
                for (s, v) in waits:
                    eng.wait_ge(b.sems[s], v)
                inst = fn(eng)
                if (eng_name in drain_engs and drain_f
                        and not getattr(fn, "_is_dma", False)):
                    tgt = eng.drain() if post else None
                    if tgt is None:
                        eng.drain()
                    for (s, amt, _) in post:
                        tgt.then_inc(b.sems[s], amt)
                else:
                    for (s, amt, _) in post:
                        inst.then_inc(b.sems[s], amt)
        return body
    block.sync(mk("sync"))
    block.tensor(mk("tensor"))
    block.vector(mk("vector"))
    block.scalar(mk("scalar"))
    block.gpsimd(mk("gpsimd"))
    stack.close()

SEMS = ["ld", "ld2", "tp", "st", "st2", "wd", "wdk", "mm", "mma1", "mma2", "mmb", "mmw",
        "ev", "eva1", "eva2", "evb", "evb2", "evb3", "evw", "sc", "cc", "out"]

def prepass_consts(b):
    t = b.t
    b.dma(t["ident128"][:], b.d["ident"])
    b.v_op(lambda e: e.memset(t["onesr"][:], 1.0))
    b.v_op(lambda e: e.memset(t["onec"][:], 1.0))
    b.v_op(lambda e: e.memset(t["conek"][:], 1.0))
    b.v_op(lambda e: e.memset(t["ceps"][:], EPS))
    b.v_op(lambda e: e.memset(t["hcar"][:], 0.0))

def prepass_ffn(b, w_ap, O, I, wt_s, mcol, gate=None):
    t = b.t
    is_gu = (I == 1024)
    if is_gu:
        src = w_ap.rearrange("(m p) i -> p m i", p=128)
        dst = t["wstage"][:]
    else:
        src = w_ap.rearrange("(m p) (c i) -> p m c i", p=128, i=1024)
        dst = t["wstage"][:].rearrange("p (m c) i -> p m c i", c=4)
    ld_sub = []
    for s in range(8):
        w = b.ejoin() if (gate is None and s == 0) else \
            ([("ev", gate[0][4 * s + 3])] if gate is not None else [])
        if is_gu:
            o, i_ = dst[:, 4 * s:4 * s + 4, :], src[:, 4 * s:4 * s + 4, :]
        else:
            o, i_ = dst[:, s, :, :], src[:, s, :, :]
        eng, sem = ("sync", "ld") if s % 2 == 0 else ("gpsimd", "ld2")
        v = b.dma(o, i_, waits=w, sem=sem, eng=eng)
        ld_sub.append((sem, v))
    for c in range(32):
        w = [ld_sub[c // 4]] if c % 4 == 0 else None
        if c == 0 and gate is not None:
            w = w + [("ev", gate[1])]
        b.s_op(lambda e, c=c: e.activation(t["pp_abs"][:], t["wstage"][:, c, :], AF.Abs,
                                           accum_out=t["stats"][:, 64 + c:65 + c]),
               waits=w, drain=(c == 31))
    sc_abs = b.val("sc")
    ev_statsfree = b.v_op(lambda e: e.reduce_sum(t["stats"][:, 60:61], t["stats"][:, 64:96], AX.X),
                          waits=[("sc", sc_abs)])
    ev = ev_statsfree
    mm0 = b.mm(b.ps[0:1, 0, 0:1], t["stats"][:, 60:61], t["onec"][:], True, True,
               waits=[("ev", ev)] + b.ejoin())
    ev = b.v_op(lambda e: e.tensor_scalar(t["rowst"][0:1, 0:1], b.ps[0:1, 0, 0:1],
                                          1.0 / (O * I), 1e-5, ALU.mult, ALU.max),
                waits=[("mm", mm0)])
    ev = b.v_recip(t["rowst"][0:1, 1:2], t["rowst"][0:1, 0:1])
    mm1 = b.mm(b.ps[:, 1, 0:2], t["onesr"][:], t["rowst"][0:1, 0:2], True, True,
               waits=[("ev", ev)])
    evm = b.v_op(lambda e: e.tensor_copy(t["mws"][:, mcol:mcol + 2], b.ps[:, 1, 0:2]),
                 waits=[("mm", mm1)])
    scol = t["mws"][:, mcol + 1:mcol + 2]
    ev = evm
    chunkfree = [None] * 32
    for m in range(O // 128):
        npc = I // 1024
        for pc in range(npc):
            c = m * npc + pc
            w1 = [("ev", evm), ("tp", b.val("tp"))] if (m == 0 and pc == 0) else None
            chunkfree[c] = b.v_op(
                lambda e, c=c: e.tensor_scalar(t["pp_t1"][:], t["wstage"][:, c, :],
                                               scol, CLIP_W, ALU.mult, ALU.min), waits=w1,
                drain=False)
            b.v_op(lambda e: e.tensor_scalar(t["pp_t2"][:], t["pp_t1"][:],
                                             -CLIP_W, C_RND, ALU.max, ALU.add), drain=False)
            ev = b.v_op(lambda e, pc=pc: e.tensor_single_scalar(
                t["pp_q4"][:, pc * 1024:(pc + 1) * 1024], t["pp_t2"][:], C_RND, ALU.subtract),
                waits=[("ev", b.val("ev")), ("st2", b.val("st2"))] if pc == 0 and m > 0 else None)
        if is_gu:
            toff = (m % 4) * 128
            evt = pe_transpose(b, lambda blk, toff=toff: t["pp_img"][:, blk, toff:toff + 128],
                               t["pp_q4"][:, 0:1024], 8, waits=[("ev", ev)])
            if m % 4 == 3:
                b.dma(wt_s[m // 4], t["pp_img"][:], waits=[("ev", evt)], sem="st2",
                      eng="scalar")
        else:
            img = bass.AP(t["pp_img"], 0, [[4096, 128], [128, 32], [1, 128]])
            evt = pe_transpose(b, lambda blk: bass.AP(t["pp_img"], blk * 128,
                                                      [[4096, 128], [1, 128]]),
                               t["pp_q4"][:], 32, waits=[("ev", ev)])
            b.dma(wt_s[:, :, m * 128:(m + 1) * 128].transpose([1, 0, 2]), img,
                  waits=[("ev", evt)], sem="st2", eng="scalar")
    return chunkfree, ev_statsfree

def prepass_attn(b, aps, need_producer, need_consumer):
    t = b.t
    if need_producer:
        for grp in range(2):
            for m0 in range(grp * 8, grp * 8 + 8, 2):
                ld = b.dma(t["wstage"][:, 0:2, :],
                           aps["W_in"][m0 * 128:(m0 + 2) * 128, :].rearrange("(m p) i -> p m i", p=128),
                           waits=[("ev", b.val("ev"))])
                evc = b.v_op(lambda e: e.tensor_copy(
                    t["pp_q4"][:].rearrange("p (m f) -> p m f", m=4)[:, 0:2, :],
                    t["wstage"][:, 0:2, :]), waits=[("ld", ld), ("tp", b.val("tp"))])
                for mm_ in range(2):
                    m = m0 + mm_
                    if grp == 0:
                        outap = b.d["winP_s"][:, m, :, :]
                    else:
                        outap = b.d["winZ_s"][:, :, (m - 8) * 128:(m - 7) * 128]
                    evt = pe_transpose(b, lambda blk: t["pp_img"][:, blk, 0:128],
                                       t["pp_q4"][:, mm_ * 1024:(mm_ + 1) * 1024], 8,
                                       waits=[("ev", evc), ("st", b.val("st"))])
                    b.dma(outap, t["pp_img"][:, :, 0:128], waits=[("ev", evt)], sem="st")
        ld = b.dma(t["wstage"][0:64, 0, :], aps["W_xp"], waits=[("ev", b.val("ev"))])
        evc = b.v_op(lambda e: e.tensor_copy(t["pp_q4"][0:64, 0:1024], t["wstage"][0:64, 0, :]),
                     waits=[("ld", ld), ("tp", b.val("tp"))])
        pe_transpose(b, lambda blk: t["wxpT"][:, blk, :],
                     t["pp_q4"][0:64, 0:1024], 8, waits=[("ev", evc)])
        ld = b.dma(t["wstage"][0:64, 1, :], aps["Dmat"])
        b.v_op(lambda e: e.tensor_single_scalar(t["dmat"][:], t["wstage"][0:64, 1, :],
                                                0.5, ALU.mult), waits=[("ld", ld)])
        b.dma(t["binp"][:], aps["b_in"][0:1024].rearrange("(m p) -> m p", p=128).rearrange("m p -> p m"), slow=True)
        b.dma(t["bxp"][:], aps["b_xp"][:, None], slow=True)
        ld = b.dma(t["stats"][0:64, 32:33], aps["A_log"].rearrange("a b p -> (a b) p").rearrange("a p -> p a"), slow=True)
        sc = b.s_op(lambda e: e.activation(t["stats"][0:64, 33:34], t["stats"][0:64, 32:33], AF.Exp),
                    waits=[("ld", ld)])
        ev = b.v_op(lambda e: e.tensor_single_scalar(t["stats"][0:64, 34:35],
                                                     t["stats"][0:64, 33:34], -1.0, ALU.mult),
                    waits=[("sc", sc)])
        sc = b.s_op(lambda e: e.activation(t["stats"][0:64, 35:36], t["stats"][0:64, 34:35], AF.Exp),
                    waits=[("ev", ev)])
        b.v_op(lambda e: e.memset(t["xs"][0:64, 0:512], 1.0))
        b.v_op(lambda e: e.tensor_scalar(t["abc"][:], t["xs"][0:64, 0:512],
                                         t["stats"][0:64, 35:36], None, ALU.mult),
               waits=[("sc", sc)])
    bcasts = []
    if need_producer:
        bcasts += [("gob", aps["gamma"], None), ("bob", aps["beta"], None),
                   ("bzh", aps["b_in"][1024:2048], 0.5)]
    if need_consumer:
        bcasts += [("outb", aps["b_out"], None)]
    for dstt, srcap, scale in bcasts:
        ld = b.dma(t["rowst"][0:1, 0:1024], srcap[None, :],
                   waits=[("mm", b.val("mm")), ("ev", b.val("ev"))])
        if scale is not None:
            ev = b.v_op(lambda e, scale=scale: e.tensor_single_scalar(
                t["rowst"][0:1, 1024:2048], t["rowst"][0:1, 0:1024], scale, ALU.mult),
                waits=[("ld", ld)])
            row = t["rowst"][0:1, 1024:2048]
        else:
            ev = b.val("ev")
            row = t["rowst"][0:1, 0:1024]
        for h in range(2):
            mmv = b.mm(b.ps[:, 2 + h, :], t["onesr"][:], row[:, h * 512:(h + 1) * 512], True, True,
                       waits=[("ld", ld), ("ev", ev), ("ev", b.val("ev"))])
            b.v_op(lambda e, dstt=dstt, h=h: e.tensor_copy(t[dstt][:, h * 512:(h + 1) * 512],
                                                           b.ps[:, 2 + h, :]),
                   waits=[("mm", mmv)])

def ln_stats_tc(b, S, tc, sq_scs, sc0, waits=None):
    t = b.t
    src = t["x"][:, tc, :]
    xst = t["xs2"] if tc % 2 else t["xs"]
    b.v_op(lambda e, src=src, tc=tc: e.reduce_sum(t["stats"][:, S + tc:S + tc + 1], src, AX.X),
           waits=list(waits or []))
    b.v_op(lambda e, tc=tc: e.tensor_single_scalar(
        t["stats"][:, S + 4 + tc:S + 5 + tc], t["stats"][:, S + tc:S + tc + 1],
        1.0 / 1024.0, ALU.mult))
    wxs = [("sc", sq_scs[-2])] if len(sq_scs) >= 2 else [("sc", sc0)]
    evx = b.v_op(lambda e, src=src, tc=tc, xst=xst: e.tensor_scalar(
        xst[:], src, t["stats"][:, S + 4 + tc:S + 5 + tc], None, ALU.subtract),
        waits=wxs)
    sq = b.s_op(lambda e, tc=tc, xst=xst: e.activation(t["q2d"][:], xst[:], AF.Square,
                                                       accum_out=t["stats"][:, S + 8 + tc:S + 9 + tc]),
                waits=[("ev", evx)])
    sq_scs.append(sq)
    b.v_op(lambda e, tc=tc, xst=xst: e.reduce_max(t["stats"][:, S + 12 + tc:S + 13 + tc], xst[:],
                                                  AX.X, apply_absolute_value=True))

def ln_quant(b, S, extra_w=None, out_guard=None):
    sq_scs = []
    sc0 = b.val("sc")
    for tc in range(4):
        ln_stats_tc(b, S, tc, sq_scs, sc0, waits=(extra_w if tc == 0 else None))
    return ln_quant_rest(b, S, out_guard=out_guard)

def ln_quant_rest(b, S, out_guard=None):
    t = b.t
    tps = []
    sc = b.s_op(lambda e: e.activation(t["stats"][:, S + 44:S + 48], t["stats"][:, S + 8:S + 12],
                                       AF.Ln, bias=t["ceps"][:, 0:1], scale=1.0 / 1024.0),
                waits=[("ev", b.val("ev"))])
    sc = b.s_op(lambda e: e.activation(t["stats"][:, S + 16:S + 20], t["stats"][:, S + 44:S + 48],
                                       AF.Exp, scale=-0.5))
    b.v_op(lambda e: e.tensor_tensor(t["stats"][:, S + 44:S + 48], t["stats"][:, S + 16:S + 20],
                                     t["stats"][:, S + 12:S + 16], ALU.mult), waits=[("sc", sc)])
    b.v_op(lambda e: e.tensor_single_scalar(t["stats"][:, S + 20:S + 24],
                                            t["stats"][:, S + 44:S + 48], 1e-5, ALU.max))
    b.v_recip(t["stats"][:, S + 24:S + 28], t["stats"][:, S + 20:S + 24])
    b.v_op(lambda e: e.tensor_tensor(t["stats"][:, S + 44:S + 48], t["stats"][:, S + 16:S + 20],
                                     t["stats"][:, S + 24:S + 28], ALU.mult))
    b.v_op(lambda e: e.tensor_single_scalar(t["stats"][:, S + 28:S + 32],
                                            t["stats"][:, S + 44:S + 48], 127.0, ALU.mult))
    b.v_op(lambda e: e.tensor_scalar(t["stats"][:, S + 32:S + 36], t["stats"][:, S + 20:S + 24],
                                     1.0 / 127.0, None, ALU.mult))
    b.v_op(lambda e: e.tensor_scalar(t["stats"][:, S + 36:S + 40], t["stats"][:, S + 32:S + 36],
                                     t["mws"][:, 0:1], None, ALU.mult))
    b.v_op(lambda e: e.tensor_scalar(t["stats"][:, S + 40:S + 44], t["stats"][:, S + 32:S + 36],
                                     t["mws"][:, 2:3], None, ALU.mult))
    evs = b.val("ev")
    for tc in range(4):
        src = t["x"][:, tc, :]
        w = [("ev", evs)] if tc == 0 else []
        if out_guard is not None and tc < 2:
            w = w + [out_guard]
        if len(tps) >= 2:
            w.append(("tp", tps[-2]))
        b.v_op(lambda e, src=src, tc=tc: e.tensor_scalar(
            t["xs"][:], src, t["stats"][:, S + 4 + tc:S + 5 + tc],
            t["stats"][:, S + 28 + tc:S + 29 + tc], ALU.subtract, ALU.mult), waits=w,
            drain=False)
        b.v_op(lambda e, tc=tc: e.tensor_single_scalar(
            t["scrx"][:, tc % 2, :], t["xs"][:], C_RND, ALU.add), drain=False)
        evq = b.v_op(lambda e, tc=tc: e.tensor_single_scalar(
            t["q1xn"][:, tc % 2, :], t["scrx"][:, tc % 2, :], C_RND, ALU.subtract))
        tp = b.dma(t["actfm"][:, :, tc * 128:(tc + 1) * 128], t["q1xn"][:, tc % 2, :],
                   waits=[("ev", evq)], sem="tp", transpose=True)
        tps.append(tp)
    return tps

def ffn_preload(b):
    t = b.t
    pre = {}
    pg = getattr(b, "prev_a1g", None)
    for gi in range(2):
        w = [("mma1", pg[14 + gi])] if pg is not None else \
            [("st2", getattr(b, "st2_gate", 0))]
        b.dma(t["wgb"][:, gi, :, :].rearrange("p k t -> p (k t)"),
              b.d["wgT_s"][gi].rearrange("p k t -> p (k t)"), waits=w)
        pre[gi] = b.dma(t["wub"][:, gi, :, :].rearrange("p k t -> p (k t)"),
                        b.d["wuT_s"][gi].rearrange("p k t -> p (k t)"))
    return pre

def ffn_phase(b, S, q1_tps, pre=None, after_pr0=None):
    t = b.t
    wgT_s, wuT_s, wdT_s = b.d["wgT_s"], b.d["wuT_s"], b.d["wdT_s"]
    q2tp = {}
    evr_tc = {}
    a1_groups = {}
    for pair in range(2):
        for n in range(8):
            par = n % 2
            gi = pair * 8 + n
            if pre is not None and gi < 2:
                ldu = pre[gi]
            else:
                w_ld = [("mma1", a1_groups[gi - 2])] if gi >= 2 else \
                    [("st2", getattr(b, "st2_gate", 0))]
                b.dma(t["wgb"][:, par, :, :].rearrange("p k t -> p (k t)"),
                      wgT_s[n].rearrange("p k t -> p (k t)"), waits=w_ld)
                ldu = b.dma(t["wub"][:, par, :, :].rearrange("p k t -> p (k t)"),
                            wuT_s[n].rearrange("p k t -> p (k t)"))
            for tcl in range(2):
                tc = pair * 2 + tcl
                pb = 2 * (gi * 2 + tcl) % 6
                w = [("ld", ldu)]
                if gi == 0:
                    w += [("tp", q1_tps[3])] + b.ejoin()
                prev_ev = b.val("eva1") - 2
                if prev_ev > 0:
                    w.append(("eva1", prev_ev))
                for k in range(8):
                    b.mm(b.ps[:, pb, :], t["actfm"][:, k, tc * 128:(tc + 1) * 128],
                         t["wgb"][:, par, k, :], k == 0, k == 7,
                         waits=w if k == 0 else None, sem="mma1")
                for k in range(8):
                    b.mm(b.ps[:, pb + 1, :], t["actfm"][:, k, tc * 128:(tc + 1) * 128],
                         t["wub"][:, par, k, :], k == 0, k == 7, sem="mma1")
                mmv = b.val("mma1")
                scv = b.s_op(lambda e, tcl=tcl, tc=tc, pb=pb: e.activation(
                    t["silu"][:, tcl, :], b.ps[:, pb, :], AF.Silu,
                    scale=t["stats"][:, S + 36 + tc:S + 37 + tc]),
                    waits=[("mma1", mmv - 1)], drain=False)
                b.v_op(lambda e, tcl=tcl, tc=tc, n=n, pb=pb: e.scalar_tensor_tensor(
                    t["g"][:, tcl, n * 512:(n + 1) * 512], b.ps[:, pb + 1, :],
                    t["stats"][:, S + 40 + tc:S + 41 + tc], t["silu"][:, tcl, :],
                    ALU.mult, ALU.mult),
                    waits=[("mma1", mmv), ("sc", scv)], sem="eva1", drain=False)
                a1_groups[gi] = mmv
        for tcl in range(2):
            tc = pair * 2 + tcl
            b.v_op(lambda e, tcl=tcl, tc=tc: e.reduce_max(
                t["stats"][:, S + 56 + tc:S + 57 + tc], t["g"][:, tcl, :], AX.X,
                apply_absolute_value=True))
            b.v_op(lambda e, tc=tc: e.tensor_single_scalar(
                t["stats"][:, S + 64 + tc:S + 65 + tc], t["stats"][:, S + 56 + tc:S + 57 + tc],
                1e-5, ALU.max))
            b.v_recip(t["stats"][:, S + 60 + tc:S + 61 + tc],
                      t["stats"][:, S + 64 + tc:S + 65 + tc])
            b.v_op(lambda e, tc=tc: e.tensor_single_scalar(
                t["stats"][:, S + 68 + tc:S + 69 + tc], t["stats"][:, S + 60 + tc:S + 61 + tc],
                127.0, ALU.mult))
            evsm = b.v_op(lambda e, tc=tc: e.tensor_scalar(
                t["stats"][:, S + 72 + tc:S + 73 + tc], t["stats"][:, S + 64 + tc:S + 65 + tc],
                t["mws"][:, 4:5], 0.5 / 127.0, ALU.mult, ALU.mult))
            evr = None
            src2 = t["q2tm"][:] if pair == 0 else t["q2t2"][:, tcl, :]
            for qq in range(4):
                sl = slice(qq * 1024, (qq + 1) * 1024)
                w = None
                if qq == 0:
                    w = [("ev", evsm)] + ([("tp", b.val("tp"))] if pair == 0 else [])
                b.v_op(lambda e, tcl=tcl, tc=tc, sl=sl: e.tensor_scalar(
                    t["xs"][:], t["g"][:, tcl, sl], t["stats"][:, S + 68 + tc:S + 69 + tc],
                    C_RND, ALU.mult, ALU.add), waits=w, drain=False)
                evr = b.v_op(lambda e, sl=sl, src2=src2: e.tensor_single_scalar(
                    src2[:, sl], t["xs"][:], C_RND, ALU.subtract))
            evr_tc[tc] = evr
            if pair == 0:
                q2tp[tc] = b.dma(t["q2fmA"][:, :, tc * 128:(tc + 1) * 128], src2,
                                 waits=[("ev", evr)], sem="tp", transpose=True)
    evA2 = {}
    wd_load0 = b.val("wdk")
    for pr in range(2):
        tc_e, tc_o = 2 * pr, 2 * pr + 1
        if pr == 1:
            for tc in (2, 3):
                q2tp[tc] = b.dma(t["q2fmB"][:, :, (tc - 2) * 128:(tc - 1) * 128],
                                 t["q2t2"][:, tc - 2, :],
                                 waits=[("ev", evr_tc[tc])], sem="tp", transpose=True)
        for ld_i in range(16):
            gl = b.val("wdk") - wd_load0
            w = []
            if pr == 0 and ld_i == 0:
                w = [("st2", getattr(b, "st2_gate", 0))]
            wdk_prev = b.val("wdk") - 2
            if wdk_prev > 0:
                w.append(("wdk", wdk_prev))
            buf = gl % 3
            b.dma(t["wdb"][:, buf, :, :], wdT_s[ld_i * 2:ld_i * 2 + 2].transpose([1, 0, 2]),
                  waits=w, sem="wd")
            if getattr(b, "wd_log", None) is not None:
                b.dma(b.wd_log[gl], t["wdb"][:, buf, :, :], sem="st")
            wdv = b.val("wd")
            for kk in range(2):
                k = ld_i * 2 + kk
                w2 = [("wd", wdv)] if kk == 0 else []
                if k == 0:
                    w2 += [("eva2", b.val("eva2")), ("tp", q2tp[2 * pr + 1])]
                    if pr == 0:
                        w2 += [("eva1", b.val("eva1") - 2), ("evb2", b.val("evb2"))]
                for ci, tc in enumerate((tc_e, tc_o)):
                    pb = 4 + 2 * ci
                    qfm = t["q2fmA"] if tc < 2 else t["q2fmB"]
                    for h in range(2):
                        b.mm(b.ps[:, pb + h, :], qfm[:, k, (tc % 2) * 128:(tc % 2 + 1) * 128],
                             t["wdb"][:, buf, kk, h * 512:(h + 1) * 512],
                             k == 0, k == 31,
                             waits=w2 if (ci == 0 and h == 0) else None, sem="mma2",
                             do_inc=(k == 31 and ci == 1 and h == 1))
            b.mm(b.ps[0:1, 0, 0:1], t["conek"][:], t["conek"][:], True, True,
                 sem="wdk", do_inc=True)
        mmv = b.val("mma2")
        for ci, tc in enumerate((tc_e, tc_o)):
            pb = 4 + 2 * ci
            for h in range(2):
                b.v_op(lambda e, tc=tc, h=h, pb=pb: e.scalar_tensor_tensor(
                    t["x"][:, tc, h * 512:(h + 1) * 512], b.ps[:, pb + h, :],
                    t["stats"][:, S + 72 + tc:S + 73 + tc],
                    t["x"][:, tc, h * 512:(h + 1) * 512], ALU.mult, ALU.add),
                    waits=[("mma2", mmv)], sem="eva2")
            evA2[tc] = b.val("eva2")
        if pr == 0 and after_pr0 is not None:
            after_pr0(evA2)
    b.prev_a1g = a1_groups
    return evA2

def lnb_front(b, S, tc, evA2_tc, sq_scs, sc0):
    t = b.t
    src = t["x"][:, tc, :]
    xst = t["xs2"] if tc % 2 else t["xs"]
    b.v_op(lambda e, src=src, tc=tc: e.reduce_sum(
        t["stats"][:, S + tc:S + tc + 1], src, AX.X), waits=[("eva2", evA2_tc)])
    b.v_op(lambda e, tc=tc: e.tensor_single_scalar(
        t["stats"][:, S + 4 + tc:S + 5 + tc], t["stats"][:, S + tc:S + tc + 1],
        1.0 / 1024.0, ALU.mult))
    wxs = [("sc", sq_scs[-2])] if len(sq_scs) >= 2 else [("sc", sc0)]
    evx = b.v_op(lambda e, src=src, tc=tc, xst=xst: e.tensor_scalar(
        xst[:], src, t["stats"][:, S + 4 + tc:S + 5 + tc], None, ALU.subtract),
        waits=wxs)
    sq = b.s_op(lambda e, tc=tc, xst=xst: e.activation(
        t["q2d"][:], xst[:], AF.Square,
        accum_out=t["stats"][:, S + 8 + tc:S + 9 + tc]), waits=[("ev", evx)])
    sq_scs.append(sq)
    b.s_op(lambda e, tc=tc: e.activation(t["stats"][:, S + 12 + tc:S + 13 + tc],
                                         t["stats"][:, S + 8 + tc:S + 9 + tc],
                                         AF.Ln, bias=t["ceps"][:, 0:1], scale=1.0 / 1024.0))
    return b.s_op(lambda e, tc=tc: e.activation(t["stats"][:, S + 16 + tc:S + 17 + tc],
                                                t["stats"][:, S + 12 + tc:S + 13 + tc],
                                                AF.Exp, scale=-0.5))

def lnb_backc(b, S, tc, sc_exp, w_tp):
    t = b.t
    src = t["x"][:, tc, :]
    xst = t["xs2"] if tc % 2 else t["xs"]
    ev_st = b.v_op(lambda e, src=src, tc=tc, xst=xst: e.scalar_tensor_tensor(
        src, xst[:], t["stats"][:, S + 16 + tc:S + 17 + tc], src,
        ALU.mult, ALU.add), waits=[("sc", sc_exp)])
    xn1 = t["scrx"][:, tc % 2, :]
    b.v_op(lambda e, xn1=xn1, tc=tc, xst=xst: e.scalar_tensor_tensor(
        xn1, xst[:], t["stats"][:, S + 16 + tc:S + 17 + tc], t["gob"][:],
        ALU.mult, ALU.mult), drain=False)
    evq = b.v_op(lambda e, xn1=xn1, tc=tc: e.tensor_tensor(
        t["q1xn"][:, tc % 2, :], xn1, t["bob"][:], ALU.add), waits=[w_tp])
    return ev_st, evq

def phase_b(b, S, tile, evA2, x1t_dst, yz_dst, lnb=None, mid_hook=None):
    t = b.t
    j = tile % 2
    if not lnb:
        lnb = {"sq_scs": [], "sc0": b.val("sc"), "exps": {}, "backs": {}}
        for tc in (0, 1):
            lnb["exps"][tc] = lnb_front(b, S, tc, evA2[tc], lnb["sq_scs"], lnb["sc0"])
        for tc in (0, 1):
            lnb["backs"][tc] = lnb_backc(b, S, tc, lnb["exps"][tc], ("tp", b.val("tp")))
    sq_scs, exps, backs = lnb["sq_scs"], lnb["exps"], lnb["backs"]
    ldp = b.dma(t["winPb"][:].rearrange("p m k c -> p (m k c)"),
                b.d["winP_s"][:].rearrange("p m k c -> p (m k c)"),
                waits=[("ev", b.val("ev")), ("eva1", b.val("eva1")),
                       ("tp", b.val("tp"))])
    xn_tps = []
    for tc in (0, 1):
        ev_st, evq = backs[tc]
        b.dma(x1t_dst(tc), t["x"][:, tc, :], waits=[("ev", ev_st)], sem="st")
        xn_tps.append(b.dma(t["actfm"][:, :, tc * 128:(tc + 1) * 128],
                            t["q1xn"][:, tc % 2, :],
                            waits=[("ev", evq)], sem="tp", transpose=True))
    ldz = b.dma(t["winZb"][:].rearrange("p k z -> p (k z)"),
                b.d["winZ_s"][:].rearrange("p k z -> p (k z)"))
    for tc in (2, 3):
        exps[tc] = lnb_front(b, S, tc, evA2[tc], sq_scs, lnb["sc0"])
    for tc in (2, 3):
        ev_st, evq = lnb_backc(b, S, tc, exps[tc], ("tp", xn_tps[tc - 2]))
        b.dma(x1t_dst(tc), t["x"][:, tc, :], waits=[("ev", ev_st)], sem="st")
        xn_tps.append(b.dma(t["actfm"][:, :, tc * 128:(tc + 1) * 128],
                            t["q1xn"][:, tc % 2, :],
                            waits=[("ev", evq)], sem="tp", transpose=True))
    if mid_hook is not None:
        mid_hook()
    for m in range(8):
        pb = m % 2
        if m == 0:
            w = [("ld", ldp), ("tp", xn_tps[3])] + b.ejoin()
        else:
            w = [("evb", max(0, b.val("evb") - 1))]
        for k in range(8):
            b.mm(b.ps[:, pb, :], t["winPb"][:, m, k, :], t["actfm"][:, k, :],
                 k == 0, k == 7, waits=w if k == 0 else None, sem="mmb")
        mmv = b.val("mmb")
        b.s_op(lambda e, m=m, pb=pb: e.activation(
            t["xp"][:, m, :], b.ps[:, pb, :], AF.Silu, bias=t["binp"][:, m:m + 1]),
            waits=[("mmb", mmv)], sem="evb")
    evb_xp = b.val("evb")
    for k in range(8):
        b.mm(b.ps[0:64, 2, :], t["wxpT"][:, k, :], t["xp"][:, k, :], k == 0, k == 7,
             waits=[("evb", evb_xp)] if k == 0 else None, sem="mmb")
    mmv = b.val("mmb")
    b.v_op(lambda e: e.tensor_scalar(t["u_s"][:], b.ps[0:64, 2, :], t["bxp"][:, 0:1],
                                     None, ALU.add), waits=[("mmb", mmv)], sem="evb")
    init = 0.0 if tile < 2 else t["hcar"][:, j:j + 1]
    b.v_op(lambda e, init=init: e.tensor_tensor_scan(
        t["H"][:], t["abc"][:], t["u_s"][:], init, ALU.mult, ALU.add), sem="evb")
    b.v_op(lambda e: e.tensor_copy(t["hcar"][:, j:j + 1], t["H"][:, 511:512]), sem="evb")
    ev_hb = b.v_op(lambda e: e.tensor_copy(t["Hb"][:], t["H"][:]), sem="evb")
    th_done = {}
    for tc in range(4):
        pbz = 3 + (tc % 2) * 2
        for k in range(8):
            w = []
            if k == 0:
                w = [("evb2", max(0, b.val("evb2") - 4))]
                if tc == 0:
                    w += [("ld", ldz), ("eva2", evA2[3]), ("eva1", b.val("eva1"))]
            b.mm(b.ps[:, pbz, :], t["actfm"][:, k, tc * 128:(tc + 1) * 128],
                 t["winZb"][:, k, 0:512], k == 0, k == 7,
                 waits=w if k == 0 else None, sem="mmb")
            b.mm(b.ps[:, pbz + 1, :], t["actfm"][:, k, tc * 128:(tc + 1) * 128],
                 t["winZb"][:, k, 512:1024], k == 0, k == 7, sem="mmb")
        mmz = b.val("mmb")
        for h in range(2):
            ev = b.v_op(lambda e, h=h, pbz=pbz: e.scalar_tensor_tensor(
                t["silu"][:, h, :], b.ps[:, pbz + h, :], 0.5,
                t["bzh"][:, h * 512:(h + 1) * 512], ALU.mult, ALU.add),
                waits=[("mmb", mmz)], sem="evb2", drain=False)
            b.s_op(lambda e, tc=tc, h=h: e.activation(
                t["th"][:, tc, h * 512:(h + 1) * 512], t["silu"][:, h, :], AF.Tanh),
                waits=[("evb2", ev)], sem="evb2")
        th_done[tc] = b.val("evb2")
    mm_h0 = {}
    for tc in range(4):
        w = [("evb", ev_hb)]
        if tc == 3:
            w.append(("evb2", th_done[2]))
        b.mm(b.ps[:, tc, :], t["Hb"][:, tc * 128:(tc + 1) * 128], t["dmat"][:, 0:512],
             True, True, waits=w, sem="mmb")
        mm_h0[tc] = b.val("mmb")
    for tc in range(4):
        b.v_op(lambda e, tc=tc: e.scalar_tensor_tensor(
            t["yztm"][:, tc, 0:512], t["th"][:, tc, 0:512], 1.0, b.ps[:, tc, :],
            ALU.add, ALU.mult), waits=[("mmb", mm_h0[tc]), ("evb2", th_done[tc])],
            sem="evb3", drain=False)
        b.mm(b.ps[:, tc, :], t["Hb"][:, tc * 128:(tc + 1) * 128], t["dmat"][:, 512:1024],
             True, True, waits=[("evb3", b.val("evb3"))], sem="mmb")
        mm_h0[tc] = b.val("mmb")
    for tc in range(4):
        b.v_op(lambda e, tc=tc: e.scalar_tensor_tensor(
            t["yztm"][:, tc, 512:1024], t["th"][:, tc, 512:1024], 1.0, b.ps[:, tc, :],
            ALU.add, ALU.mult), waits=[("mmb", mm_h0[tc])], sem="evb3",
            drain=(tc == 3))
    st = b.dma(yz_dst, t["yztm"][:], waits=[("evb3", b.val("evb3"))], sem="st")
    return st

def pass2_wout(b, S, u, yz_srcs, x1t_src, first, tc_hook=None):
    t = b.t
    out_guard = b.val("out")
    if first:
        b.dma(t["woutb"][:].rearrange("p k o -> p (k o)"),
              b.d["woutT_s"][:].rearrange("p k o -> p (k o)"), sem="ld",
              waits=[("tp", b.val("tp")), ("ev", b.val("ev"))])
    w_wout = b.val("ld")
    if callable(yz_srcs):
        tp_yz = yz_srcs()
    else:
        tp_yz = None
        for jj in range(2):
            w = [("mmw", b.val("mmw"))]
            tp_yz = b.dma(t["yzfm"][:, :, jj * 256:(jj + 1) * 256], yz_srcs[jj],
                          waits=w, sem="tp", transpose=True)
    evw = {}
    for tc in range(4):
        par = tc % 2
        ld_x1 = b.dma(t["x1ts"][:, par, :], x1t_src(tc),
                      waits=[("evw", max(0, b.val("evw") - 4)), ("st", b.val("st")),
                             ("evb3", b.val("evb3")), ("eva2", b.val("eva2"))], sem="ld")
        pb = par * 2
        for k in range(8):
            w = []
            if k == 0:
                w = [("tp", tp_yz), ("ld", w_wout)]
                if tc == 0:
                    w += b.ejoin()
                prev = b.val("evw") - 4
                if prev > 0:
                    w.append(("evw", prev))
            b.mm(b.ps[:, pb, :], t["yzfm"][:, k, tc * 128:(tc + 1) * 128],
                 t["woutb"][:, k, 0:512], k == 0, k == 7, waits=w if k == 0 else None, sem="mmw")
            b.mm(b.ps[:, pb + 1, :], t["yzfm"][:, k, tc * 128:(tc + 1) * 128],
                 t["woutb"][:, k, 512:1024], k == 0, k == 7, sem="mmw")
        mmv = b.val("mmw")
        for h in range(2):
            b.v_op(lambda e, h=h, pb=pb: e.tensor_tensor(
                t["ebuf"][:, h * 512:(h + 1) * 512], b.ps[:, pb + h, :],
                t["outb"][:, h * 512:(h + 1) * 512], ALU.add),
                waits=[("mmw", mmv), ("ld", ld_x1)], sem="evw", drain=False)
            b.v_op(lambda e, tc=tc, h=h, par=par: e.tensor_tensor(
                t["x"][:, tc, h * 512:(h + 1) * 512], t["x1ts"][:, par, h * 512:(h + 1) * 512],
                t["ebuf"][:, h * 512:(h + 1) * 512], ALU.add),
                waits=[("out", out_guard)], sem="evw")
        evw[tc] = b.val("evw")
        if tc_hook is not None:
            tc_hook(tc)
    return evw

def weight_aps(nc):
    aps = {}
    names = [("w_gate", [4096, 1024]), ("w_up", [4096, 1024]), ("w_down", [1024, 4096]),
             ("W_in", [2048, 1024]), ("W_xp", [64, 1024]), ("W_out", [1024, 1024]),
             ("Dmat", [64, 1024]), ("A_log", [1, 1, 64]), ("b_in", [2048]), ("b_xp", [64]),
             ("b_out", [1024]), ("gamma", [1024]), ("beta", [1024])]
    for nm, shp in names:
        aps[nm] = nc.dram_tensor(nm, shp, F32, kind="ExternalInput").ap()
    return aps

def dram_scratch(b, nc):
    b.d["wgT_s"] = nc.dram_tensor("wgT_s", [8, 128, 8, 512], BF16).ap()
    b.d["wuT_s"] = nc.dram_tensor("wuT_s", [8, 128, 8, 512], BF16).ap()
    b.d["wdT_s"] = nc.dram_tensor("wdT_s", [32, 128, 1024], BF16).ap()
    b.d["winP_s"] = nc.dram_tensor("winP_s", [128, 8, 8, 128], BF16).ap()
    b.d["winZ_s"] = nc.dram_tensor("winZ_s", [128, 8, 1024], BF16).ap()
    b.d["woutT_s"] = nc.dram_tensor("woutT_s", [128, 8, 1024], BF16).ap()

def setup_b(nc):
    smap = build_smap()
    b = B()
    b.t = {nm: nc.alloc_sbuf_tensor_at(nm, shp, dt, offset=off)
           for nm, (shp, dt, off) in smap.items.items()}
    b.ps = nc.alloc_psum_tensor("ps", [128, 8, 512], F32)
    ident = nc.inline_tensor(np.eye(128, dtype=NPBF), name="ident128")
    b.d["ident"] = ident.ap()
    dram_scratch(b, nc)
    return b

def pe_transpose(b, dst, src_ap, nblk, waits=None, sem="ev"):
    t = b.t
    ev = None
    for blk in range(nblk):
        pb = 2 + blk % 2
        w = list(waits or []) if blk == 0 else []
        prev = b.val("ev") - 1
        if blk >= 2:
            w.append(("ev", b.val("ev") - 1))
        b.mm(b.ps[:, pb, 0:128], src_ap[:, blk * 128:(blk + 1) * 128], t["ident128"][:],
             True, True, waits=w, sem="mm") if False else None
        P = src_ap.shape[0]
        def fn(e, pb=pb, blk=blk, P=P):
            return e.transpose(b.ps[:, pb, 0:P // 2].bitcast(BF16),
                               src_ap[:, blk * 128:(blk + 1) * 128],
                               t["ident128"][0:P, 0:P])
        b.emit("tensor", fn, w, ("mm", 1))
        mmv = b.val("mm")
        ev = b.v_op(lambda e, pb=pb, blk=blk, P=P: e.tensor_copy(
            dst(blk), b.ps[:, pb, 0:P // 2].bitcast(BF16)), waits=[("mm", mmv)], drain=False)
    return ev

def emit_pass1(b, x_in, x1t_out, yz_out, quarter_hook=None):
    t = b.t

    def emit_ldx(tile, w):
        q, j = tile // 2, tile % 2
        return b.dma(t["x"][:], x_in[j, q * 512:q * 512 + 512, :]
                     .rearrange("(c p) f -> p c f", p=128), waits=w)

    nxt = {}
    for tile in range(8):
        q, j = tile // 2, tile % 2
        l0 = q * 512
        if tile == 0:
            ldx = emit_ldx(0, [("tp", b.val("tp")), ("ev", b.val("ev")), ("sc", b.val("sc"))])
            pre = ffn_preload(b)
            sq_scs, sc0 = [], b.val("sc")
            for tc in range(4):
                ln_stats_tc(b, 0, tc, sq_scs, sc0, waits=([("ld", ldx)] if tc == 0 else None))
        else:
            pre = nxt["pre"]
        x1v = x1t_out[j].rearrange("(c p) f -> p c f", p=128)
        q1_tps = ln_quant_rest(b, 0)
        lnb = {}

        def after_pr0(evA2, lnb=lnb):
            lnb.update(sq_scs=[], sc0=b.val("sc"), exps={}, backs={})
            for tc in (0, 1):
                lnb["exps"][tc] = lnb_front(b, 0, tc, evA2[tc], lnb["sq_scs"], lnb["sc0"])
            for tc in (0, 1):
                lnb["backs"][tc] = lnb_backc(b, 0, tc, lnb["exps"][tc],
                                             ("tp", b.val("tp")))
        evA2 = ffn_phase(b, 0, q1_tps, pre=pre, after_pr0=after_pr0)

        def mid_hook(tile=tile):
            if tile == 7:
                return
            ldx2 = emit_ldx(tile + 1, [("st", b.val("st"))])
            nxt["pre"] = ffn_preload(b)
            sq_scs, sc0 = [], b.val("sc")
            for tc in range(4):
                ln_stats_tc(b, 0, tc, sq_scs, sc0, waits=([("ld", ldx2)] if tc == 0 else None))
        phase_b(b, 0, tile, evA2,
                x1t_dst=lambda tc, x1v=x1v, l0=l0: x1v[:, (l0 // 128) + tc, :],
                yz_dst=yz_out[q, j].rearrange("(c p) f -> p c f", p=128), lnb=lnb,
                mid_hook=mid_hook)
        if quarter_hook is not None and tile % 2 == 1:
            quarter_hook(tile // 2)
    b.emit("sync", lambda e: e.nop(), [("st", b.val("st")), ("out", b.val("out"))], None)

def emit_pass2(b, yz_in_fn, x1t_src_fn, y_out_fn):
    t = b.t
    for u in range(8):
        sq_scs, sc0 = [], b.val("sc")

        def tc_hook(tc, sq_scs=sq_scs, sc0=sc0):
            ln_stats_tc(b, 0, tc, sq_scs, sc0)
        evw = pass2_wout(b, 0, u, yz_in_fn(u), lambda tc, u=u: x1t_src_fn(u, tc),
                         first=(u == 0), tc_hook=tc_hook)
        pre = ffn_preload(b)
        q1_tps = ln_quant_rest(b, 0, out_guard=("out", b.val("out")))
        evA2 = ffn_phase(b, 0, q1_tps, pre=pre)
        for tc in range(4):
            b.dma(y_out_fn(u, tc), t["x"][:, tc, :], waits=[("eva2", evA2[tc])], sem="out")
    b.emit("sync", lambda e: e.nop(), [("out", b.val("out"))], None)

def _build_p1():
    nc = bass.Bass("TRN2", target_bir_lowering=False)
    b = setup_b(nc)
    aps = weight_aps(nc)
    x_in = nc.dram_tensor("x", [2, 2048, 1024], F32, kind="ExternalInput").ap()
    x1t_out = nc.dram_tensor("x1t", [2, 2048, 1024], F32, kind="ExternalOutput").ap()
    yz_out = nc.dram_tensor("yz", [4, 2, 512, 1024], BF16, kind="ExternalOutput").ap()
    prepass_consts(b)
    g1 = prepass_ffn(b, aps["w_gate"], 4096, 1024, b.d["wgT_s"], 0)
    g2 = prepass_ffn(b, aps["w_up"], 4096, 1024, b.d["wuT_s"], 2, gate=g1)
    prepass_ffn(b, aps["w_down"], 1024, 4096, b.d["wdT_s"], 4, gate=g2)
    b.st2_gate = b.val("st2")
    prepass_attn(b, aps, need_producer=True, need_consumer=False)
    emit_pass1(b, x_in, x1t_out, yz_out)
    finalize(b, nc, SEMS)
    return nc

def _prepass_wout(b, aps):
    t = b.t
    for mp in range(0, 8, 2):
        ld = b.dma(t["wstage"][:, 0:2, :],
                   aps["W_out"][mp * 128:(mp + 2) * 128, :].rearrange("(m p) i -> p m i", p=128),
                   waits=[("ev", b.val("ev"))])
        evc = b.v_op(lambda e: e.tensor_copy(
            t["pp_q4"][:].rearrange("p (m f) -> p m f", m=4)[:, 0:2, :],
            t["wstage"][:, 0:2, :]), waits=[("ld", ld), ("tp", b.val("tp"))])
        for mm_ in range(2):
            m = mp + mm_
            evt = pe_transpose(b, lambda blk: t["pp_img"][:, blk, 0:128],
                               t["pp_q4"][:, mm_ * 1024:(mm_ + 1) * 1024], 8,
                               waits=[("ev", evc), ("st", b.val("st"))])
            b.dma(b.d["woutT_s"][:, :, m * 128:(m + 1) * 128], t["pp_img"][:, :, 0:128],
                  waits=[("ev", evt)], sem="st")

def _build_p2():
    nc = bass.Bass("TRN2", target_bir_lowering=False)
    b = setup_b(nc)
    aps = weight_aps(nc)
    x1t_in = nc.dram_tensor("x1t", [2, 2048, 1024], F32, kind="ExternalInput").ap()
    yz_in = nc.dram_tensor("yzin", [8, 2, 256, 1024], BF16, kind="ExternalInput").ap()
    y_out = nc.dram_tensor("y", [2, 2048, 1024], F32, kind="ExternalOutput").ap()
    prepass_consts(b)
    g1 = prepass_ffn(b, aps["w_gate"], 4096, 1024, b.d["wgT_s"], 0)
    g2 = prepass_ffn(b, aps["w_up"], 4096, 1024, b.d["wuT_s"], 2, gate=g1)
    prepass_ffn(b, aps["w_down"], 1024, 4096, b.d["wdT_s"], 4, gate=g2)
    b.st2_gate = b.val("st2")
    prepass_attn(b, aps, need_producer=False, need_consumer=True)
    _prepass_wout(b, aps)

    def yz_src(u):
        return [yz_in[u, jj] for jj in range(2)]

    def x1t_src(u, tc):
        jj, ih = tc // 2, tc % 2
        b2 = 2 * u + jj
        return bass.AP(x1t_in.tensor, (ih * 2048 + b2) * 1024, [[16 * 1024, 128], [1, 1024]])

    def y_dst(u, tc):
        jj, ih = tc // 2, tc % 2
        b2 = 2 * u + jj
        return bass.AP(y_out.tensor, (ih * 2048 + b2) * 1024, [[16 * 1024, 128], [1, 1024]])

    emit_pass2(b, yz_src, x1t_src, y_dst)
    finalize(b, nc, SEMS)
    return nc

def _build_fused():
    nc = bass.Bass("TRN2", target_bir_lowering=False, num_devices=8)
    b = setup_b(nc)
    aps = weight_aps(nc)
    t = b.t
    x_in = nc.dram_tensor("x", [2, 2048, 1024], F32, kind="ExternalInput").ap()
    y_out = nc.dram_tensor("y", [2, 2048, 1024], F32, kind="ExternalOutput").ap()
    x1t_s = nc.dram_tensor("x1t_s", [2, 2048, 1024], F32, kind="Internal").ap()
    yz_send = nc.dram_tensor("yz_send", [4, 2, 512, 1024], BF16, kind="Internal").ap()
    gat = nc.dram_tensor("gat", [4, 8, 2, 512, 1024], BF16, kind="Internal").ap()
    prepass_consts(b)
    g1 = prepass_ffn(b, aps["w_gate"], 4096, 1024, b.d["wgT_s"], 0)
    g2 = prepass_ffn(b, aps["w_up"], 4096, 1024, b.d["wuT_s"], 2, gate=g1)
    prepass_ffn(b, aps["w_down"], 1024, 4096, b.d["wdT_s"], 4, gate=g2)
    b.st2_gate = b.val("st2")
    prepass_attn(b, aps, need_producer=True, need_consumer=True)
    _prepass_wout(b, aps)

    emit_pass1(b, x_in, x1t_s, yz_send, quarter_hook=None)
    for q in range(4):
        def ag_fn(g, q=q):
            return g.collective_compute(
                "AllGather", mybir.AluOpType.bypass,
                replica_groups=[list(range(NCORE))],
                ins=[yz_send[q]], outs=[gat[q]])
        b.emit("gpsimd", ag_fn,
               [("st", b.val("st")), ("tp", b.val("tp")), ("evb3", b.val("evb3"))],
               ("cc", 1))

    b.dyn = {}

    def rank_fn(g):
        rank = g.compute_val(g.partition_id())
        b.dyn["rq"] = nc.s_assert_within(rank // 2, 0, 3)
        b.dyn["rm256"] = nc.s_assert_within((rank % 2) * 256, 0, 256)
        return g.engine_nop()
    b.emit("gpsimd", rank_fn, [("cc", 4)], ("cc", 1))

    def yz_loader(u):
        def load():
            for jj in range(2):
                def fn(g, u=u, jj=jj):
                    in_ = gat[ds(b.dyn["rq"], 1), u, jj, ds(b.dyn["rm256"], 256), :]
                    in_ = in_.squeeze(0).rearrange("(c p) f -> p c f", p=128)
                    return g.dma_start(out=t["yztm"][:, jj * 2:jj * 2 + 2, :], in_=in_)
                w = [("cc", 5), ("tp", b.val("tp")), ("evb3", b.val("evb3")),
                     ("st", b.val("st")), ("mmw", b.val("mmw")),
                     ("mma2", b.val("mma2")), ("eva2", b.val("eva2"))]
                fn._is_dma = True
                b.emit("gpsimd", fn, w, ("ld", 16))
            ldv = b.ctr["ld"]
            tp = None
            for tc in range(4):
                tp = b.dma(t["yzfm"][:, :, tc * 128:(tc + 1) * 128], t["yztm"][:, tc, :],
                           waits=[("ld", ldv)], sem="tp", transpose=True)
            return tp
        return load

    def x1t_src(u, tc):
        jj, ih = tc // 2, tc % 2
        b2 = 2 * u + jj
        return bass.AP(x1t_s.tensor, (ih * 2048 + b2) * 1024, [[16 * 1024, 128], [1, 1024]])

    def y_dst(u, tc):
        jj, ih = tc // 2, tc % 2
        b2 = 2 * u + jj
        return bass.AP(y_out.tensor, (ih * 2048 + b2) * 1024, [[16 * 1024, 128], [1, 1024]])

    emit_pass2(b, yz_loader, x1t_src, y_dst)
    finalize(b, nc, SEMS)
    return nc

NCORE = 8

def kernel(L0_attn_x, L0_attn_A_log, L0_attn_Dmat, L0_attn_W_in, L0_attn_b_in,
           L0_attn_W_xp, L0_attn_b_xp, L0_attn_W_out, L0_attn_b_out, L0_attn_gamma,
           L0_attn_beta, L0_ffn_w_gate, L0_ffn_w_up, L0_ffn_w_down):
    wmap = dict(
        w_gate=np.ascontiguousarray(L0_ffn_w_gate, dtype=np.float32),
        w_up=np.ascontiguousarray(L0_ffn_w_up, dtype=np.float32),
        w_down=np.ascontiguousarray(L0_ffn_w_down, dtype=np.float32),
        W_in=np.ascontiguousarray(L0_attn_W_in, dtype=np.float32),
        W_xp=np.ascontiguousarray(L0_attn_W_xp, dtype=np.float32),
        W_out=np.ascontiguousarray(L0_attn_W_out, dtype=np.float32),
        Dmat=np.ascontiguousarray(L0_attn_Dmat, dtype=np.float32),
        A_log=np.ascontiguousarray(L0_attn_A_log, dtype=np.float32),
        b_in=np.ascontiguousarray(L0_attn_b_in, dtype=np.float32),
        b_xp=np.ascontiguousarray(L0_attn_b_xp, dtype=np.float32),
        b_out=np.ascontiguousarray(L0_attn_b_out, dtype=np.float32),
        gamma=np.ascontiguousarray(L0_attn_gamma, dtype=np.float32),
        beta=np.ascontiguousarray(L0_attn_beta, dtype=np.float32),
    )
    x = np.ascontiguousarray(L0_attn_x, dtype=np.float32)
    if FUSED:
        nc = _build_fused()
        in_maps = [dict(wmap, x=np.ascontiguousarray(x[2 * c:2 * c + 2])) for c in range(8)]
        res = run_bass_kernel_spmd(nc, in_maps, core_ids=list(range(8)))
        out = np.stack([np.asarray(res.results[c]["y"]) for c in range(8)])
        return out.reshape(16, 2048, 1024)
    nc1 = _build_p1()
    in_maps = [dict(wmap, x=np.ascontiguousarray(x[2 * c:2 * c + 2])) for c in range(8)]
    res1 = run_bass_kernel_spmd(nc1, in_maps, core_ids=list(range(8)))
    x1t = [res1.results[c]["x1t"] for c in range(8)]
    yz = np.stack([np.asarray(res1.results[c]["yz"]).view(NPBF) if res1.results[c]["yz"].dtype != NPBF
                   else res1.results[c]["yz"] for c in range(8)])
    nc2 = _build_p2()
    in_maps2 = []
    for c in range(8):
        yzin = np.ascontiguousarray(yz[:, c // 2, :, (c % 2) * 256:(c % 2) * 256 + 256, :])
        in_maps2.append(dict(wmap, x1t=np.ascontiguousarray(x1t[c]), yzin=yzin))
    res2 = run_bass_kernel_spmd(nc2, in_maps2, core_ids=list(range(8)))
    out = np.concatenate([np.asarray(res2.results[c]["y"])[None] for c in range(8)], axis=0)
    return out.reshape(16, 2048, 1024)


# version 15
# speedup vs baseline: 1.0600x; 1.0134x over previous
import numpy as np
import ml_dtypes
import concourse.bass as bass
import concourse.mybir as mybir
from concourse.bass_utils import run_bass_kernel_spmd

F32 = mybir.dt.float32
BF16 = mybir.dt.bfloat16
AF = mybir.ActivationFunctionType
ALU = mybir.AluOpType
AX = mybir.AxisListType
NPBF = ml_dtypes.bfloat16
ds = bass.ds

FUSED = True
PREFETCH = False
C_RND = float(np.float32(1.5 * 2 ** 23))
EPS = 1e-5
CLIP_W = 1.4999999

EV_SEMS = ("ev", "eva1", "eva2", "evb", "evb2", "evb3", "evw", "sc")

def _ru(x, a=32):
    return (x + a - 1) // a * a

class SMap:
    def __init__(self):
        self.off = 0
        self.items = {}

    def add(self, name, shape, dt, at=None):
        nbytes = int(np.prod(shape[1:])) * mybir.dt.size(dt)
        o = self.off if at is None else at
        self.items[name] = (shape, dt, o)
        if at is None:
            self.off = _ru(o + nbytes)
        return o + nbytes

SBUF_BASE = 16512

def build_smap():
    m = SMap()
    m.off = SBUF_BASE
    m.add("dmat", [64, 1024], BF16)
    m.add("wxpT", [128, 8, 64], BF16)
    m.add("binp", [128, 8], F32)
    m.add("bxp", [64, 1], F32)
    m.add("gob", [128, 1024], F32)
    m.add("bob", [128, 1024], F32)
    m.add("outb", [128, 1024], F32)
    m.add("bzh", [128, 1024], F32)
    m.add("abc", [64, 512], F32)
    m.add("stats", [128, 96], F32)
    m.add("mws", [128, 8], F32)
    m.add("hcar", [64, 2], F32)
    m.add("onesr", [1, 128], F32)
    m.add("onec", [128, 1], F32)
    m.add("conek", [128, 1], BF16)
    m.add("ident128", [128, 128], BF16)
    m.add("ceps", [128, 1], F32)
    m.add("x", [128, 4, 1024], F32)
    m.add("scrx", [128, 2, 1024], F32)
    m.add("xs", [128, 1024], F32)
    m.add("q1xn", [128, 2, 1024], BF16)
    m.add("actfm", [128, 8, 512], BF16)
    m.add("silu", [128, 2, 512], F32)
    m.add("wgb", [128, 2, 8, 512], BF16)
    m.add("wub", [128, 2, 8, 512], BF16)
    m.add("wdb", [128, 3, 2, 1024], BF16)
    goff = m.off
    m.add("g", [128, 2, 4096], F32)
    q2off = m.add("q2tm", [128, 4096], BF16) - 8192
    fmoff = m.add("q2fmA", [128, 32, 256], BF16) - 16384
    m.add("q2fmB", [128, 32, 256], BF16)
    aend = m.off
    m.add("xs2", [128, 1024], F32, at=q2off)
    m.add("q2d", [128, 1024], BF16, at=q2off + 4096)
    m.add("q2t2", [128, 2, 4096], BF16, at=goff)
    m.add("yztm2", [128, 4, 1024], BF16, at=goff + 16384)
    m.add("woutb", [128, 8, 1024], BF16)
    m.add("yzfm", [128, 8, 512], BF16)
    m.budget = m.off
    m.add("winPb", [128, 8, 8, 128], BF16, at=goff)
    m.add("winZb", [128, 8, 1024], BF16, at=goff + 16384)
    m.add("xp", [128, 8, 512], BF16, at=goff + 32768)
    m.add("th", [128, 4, 1024], F32, at=goff + 40960)
    m.add("x1ts", [128, 2, 1024], F32, at=goff + 40960)
    m.add("ebuf", [128, 1024], F32, at=goff + 49152)
    m.add("u_s", [64, 512], F32, at=goff + 57344)
    m.add("H", [64, 512], F32, at=goff + 59392)
    m.add("Hb", [64, 512], BF16, at=goff + 61440)
    m.add("yztm", [128, 4, 1024], BF16, at=goff + 62464)
    assert goff + 62464 + 8192 <= aend
    xoff = m.items["x"][2]
    goff = m.items["g"][2]
    m.add("wstage", [128, 32, 1024], F32, at=xoff)
    po = xoff + 131072
    assert po <= goff + 45056 + 1
    m.add("pp_t1", [128, 1024], F32, at=po)
    m.add("pp_t2", [128, 1024], F32, at=po + 4096)
    m.add("rowst", [1, 2048], F32, at=po)
    m.add("pp_q4", [128, 4096], BF16, at=po + 8192)
    m.add("pp_abs", [128, 1024], BF16, at=po + 16384)
    m.add("pp_img", [128, 8, 512], BF16, at=po + 18432)
    assert po + 18432 + 8192 <= goff + 73728
    assert m.budget <= SBUF_BASE + 212800, m.budget
    return m

ENGS = ("sync", "tensor", "vector", "scalar", "gpsimd")

class B:
    def __init__(self):
        self.ops = {e: [] for e in ENGS}
        self.ctr = {}
        self.t = {}
        self.d = {}
        self.ps = None
        self.sems = {}

    def inc(self, name, amt=1):
        self.ctr[name] = self.ctr.get(name, 0) + amt
        return self.ctr[name]

    def val(self, name):
        return self.ctr.get(name, 0)

    def ejoin(self):
        return [(s, self.val(s)) for s in EV_SEMS if self.val(s) > 0]

    def emit(self, eng, fn, waits=None, inc=None, drain=True):
        post = []
        if inc is not None:
            incs = inc if isinstance(inc, list) else [inc]
            for (s, amt) in incs:
                post.append((s, amt, self.inc(s, amt)))
        self.ops[eng].append((fn, [(s, v) for (s, v) in (waits or []) if v > 0], post,
                              drain))

    PLAIN_DMA_SEMS = ("ld", "wd", "st", "out")

    def dma(self, out, in_, waits=None, sem="ld", transpose=False, eng="sync", slow=False):
        waits = list(waits or [])

        def fn(e, out=out, in_=in_, transpose=transpose, slow=slow):
            if transpose:
                return e.dma_start_transpose(out=out, in_=in_)
            if slow:
                return e.dma_start(out=out, in_=in_, allow_slow_non_contiguous=True)
            return e.dma_start(out=out, in_=in_)
        fn._is_dma = True
        self.emit(eng, fn, waits, (sem, 16))
        return self.ctr[sem]

    def v_op(self, fn, waits=None, sem="ev", drain=True):
        self.emit("vector", fn, waits, (sem, 1), drain=drain)
        return self.ctr[sem]

    def v_recip(self, out, in_, waits=None, sem="ev"):
        return self.v_op(lambda e, out=out, in_=in_: e.reciprocal(out, in_), waits, sem)

    def s_op(self, fn, waits=None, sem="sc", drain=True):
        self.emit("scalar", fn, waits, (sem, 1), drain=drain)
        return self.ctr[sem]

    def mm(self, out, lhsT, rhs, start, stop, waits=None, sem="mm", do_inc=None,
           extra_inc=None):
        def fn(e, out=out, lhsT=lhsT, rhs=rhs, start=start, stop=stop):
            return e.matmul(out, lhsT, rhs, start=start, stop=stop)
        if do_inc is None:
            do_inc = stop
        incs = [(sem, 1)] if do_inc else []
        if extra_inc is not None:
            incs.append(extra_inc)
        self.emit("tensor", fn, waits, incs if incs else None)
        return self.ctr.get(sem, 0)

def finalize(b, nc, sem_names):
    import contextlib
    stack = contextlib.ExitStack()
    for s in sem_names:
        b.sems[s] = stack.enter_context(nc.semaphore(s))
    block = stack.enter_context(nc.Block())

    drain_engs = ("vector", "scalar")

    def mk(eng_name):
        def body(eng):
            for fn, waits, post, drain_f in b.ops[eng_name]:
                for (s, v) in waits:
                    eng.wait_ge(b.sems[s], v)
                inst = fn(eng)
                if (eng_name in drain_engs and drain_f
                        and not getattr(fn, "_is_dma", False)):
                    tgt = eng.drain() if post else None
                    if tgt is None:
                        eng.drain()
                    for (s, amt, _) in post:
                        tgt.then_inc(b.sems[s], amt)
                else:
                    for (s, amt, _) in post:
                        inst.then_inc(b.sems[s], amt)
        return body
    block.sync(mk("sync"))
    block.tensor(mk("tensor"))
    block.vector(mk("vector"))
    block.scalar(mk("scalar"))
    block.gpsimd(mk("gpsimd"))
    stack.close()

SEMS = ["ld", "ld2", "tp", "st", "st2", "wd", "wdk", "mm", "mma1", "mma2", "mmb", "mmw",
        "ev", "eva1", "eva2", "evb", "evb2", "evb3", "evw", "sc", "cc", "out"]

def prepass_consts(b):
    t = b.t
    b.dma(t["ident128"][:], b.d["ident"])
    b.v_op(lambda e: e.memset(t["onesr"][:], 1.0))
    b.v_op(lambda e: e.memset(t["onec"][:], 1.0))
    b.v_op(lambda e: e.memset(t["conek"][:], 1.0))
    b.v_op(lambda e: e.memset(t["ceps"][:], EPS))
    b.v_op(lambda e: e.memset(t["hcar"][:], 0.0))

def prepass_ffn(b, w_ap, O, I, wt_s, mcol, gate=None):
    t = b.t
    is_gu = (I == 1024)
    if is_gu:
        src = w_ap.rearrange("(m p) i -> p m i", p=128)
        dst = t["wstage"][:]
    else:
        src = w_ap.rearrange("(m p) (c i) -> p m c i", p=128, i=1024)
        dst = t["wstage"][:].rearrange("p (m c) i -> p m c i", c=4)
    ld_sub = []
    for s in range(8):
        w = b.ejoin() if (gate is None and s == 0) else \
            ([("ev", gate[0][4 * s + 3])] if gate is not None else [])
        if is_gu:
            o, i_ = dst[:, 4 * s:4 * s + 4, :], src[:, 4 * s:4 * s + 4, :]
        else:
            o, i_ = dst[:, s, :, :], src[:, s, :, :]
        eng, sem = ("sync", "ld") if s % 2 == 0 else ("gpsimd", "ld2")
        v = b.dma(o, i_, waits=w, sem=sem, eng=eng)
        ld_sub.append((sem, v))
    for c in range(32):
        w = [ld_sub[c // 4]] if c % 4 == 0 else None
        if c == 0 and gate is not None:
            w = w + [("ev", gate[1])]
        b.s_op(lambda e, c=c: e.activation(t["pp_abs"][:], t["wstage"][:, c, :], AF.Abs,
                                           accum_out=t["stats"][:, 64 + c:65 + c]),
               waits=w, drain=(c == 31))
    sc_abs = b.val("sc")
    ev_statsfree = b.v_op(lambda e: e.reduce_sum(t["stats"][:, 60:61], t["stats"][:, 64:96], AX.X),
                          waits=[("sc", sc_abs)])
    ev = ev_statsfree
    mm0 = b.mm(b.ps[0:1, 0, 0:1], t["stats"][:, 60:61], t["onec"][:], True, True,
               waits=[("ev", ev)] + b.ejoin())
    ev = b.v_op(lambda e: e.tensor_scalar(t["rowst"][0:1, 0:1], b.ps[0:1, 0, 0:1],
                                          1.0 / (O * I), 1e-5, ALU.mult, ALU.max),
                waits=[("mm", mm0)])
    ev = b.v_recip(t["rowst"][0:1, 1:2], t["rowst"][0:1, 0:1])
    mm1 = b.mm(b.ps[:, 1, 0:2], t["onesr"][:], t["rowst"][0:1, 0:2], True, True,
               waits=[("ev", ev)])
    evm = b.v_op(lambda e: e.tensor_copy(t["mws"][:, mcol:mcol + 2], b.ps[:, 1, 0:2]),
                 waits=[("mm", mm1)])
    scol = t["mws"][:, mcol + 1:mcol + 2]
    ev = evm
    chunkfree = [None] * 32
    for m in range(O // 128):
        npc = I // 1024
        for pc in range(npc):
            c = m * npc + pc
            w1 = [("ev", evm), ("tp", b.val("tp"))] if (m == 0 and pc == 0) else None
            chunkfree[c] = b.v_op(
                lambda e, c=c: e.tensor_scalar(t["pp_t1"][:], t["wstage"][:, c, :],
                                               scol, CLIP_W, ALU.mult, ALU.min), waits=w1,
                drain=False)
            b.v_op(lambda e: e.tensor_scalar(t["pp_t2"][:], t["pp_t1"][:],
                                             -CLIP_W, C_RND, ALU.max, ALU.add), drain=False)
            ev = b.v_op(lambda e, pc=pc: e.tensor_single_scalar(
                t["pp_q4"][:, pc * 1024:(pc + 1) * 1024], t["pp_t2"][:], C_RND, ALU.subtract),
                waits=[("ev", b.val("ev")), ("st2", b.val("st2"))] if pc == 0 and m > 0 else None)
        if is_gu:
            toff = (m % 4) * 128
            evt = pe_transpose(b, lambda blk, toff=toff: t["pp_img"][:, blk, toff:toff + 128],
                               t["pp_q4"][:, 0:1024], 8, waits=[("ev", ev)])
            if m % 4 == 3:
                b.dma(wt_s[m // 4], t["pp_img"][:], waits=[("ev", evt)], sem="st2",
                      eng="scalar")
        else:
            img = bass.AP(t["pp_img"], 0, [[4096, 128], [128, 32], [1, 128]])
            evt = pe_transpose(b, lambda blk: bass.AP(t["pp_img"], blk * 128,
                                                      [[4096, 128], [1, 128]]),
                               t["pp_q4"][:], 32, waits=[("ev", ev)])
            b.dma(wt_s[:, :, m * 128:(m + 1) * 128].transpose([1, 0, 2]), img,
                  waits=[("ev", evt)], sem="st2", eng="scalar")
    return chunkfree, ev_statsfree

def prepass_attn(b, aps, need_producer, need_consumer):
    t = b.t
    if need_producer:
        for grp in range(2):
            for m0 in range(grp * 8, grp * 8 + 8, 2):
                ld = b.dma(t["wstage"][:, 0:2, :],
                           aps["W_in"][m0 * 128:(m0 + 2) * 128, :].rearrange("(m p) i -> p m i", p=128),
                           waits=[("ev", b.val("ev"))])
                evc = b.v_op(lambda e: e.tensor_copy(
                    t["pp_q4"][:].rearrange("p (m f) -> p m f", m=4)[:, 0:2, :],
                    t["wstage"][:, 0:2, :]), waits=[("ld", ld), ("tp", b.val("tp"))])
                for mm_ in range(2):
                    m = m0 + mm_
                    if grp == 0:
                        outap = b.d["winP_s"][:, m, :, :]
                    else:
                        outap = b.d["winZ_s"][:, :, (m - 8) * 128:(m - 7) * 128]
                    evt = pe_transpose(b, lambda blk: t["pp_img"][:, blk, 0:128],
                                       t["pp_q4"][:, mm_ * 1024:(mm_ + 1) * 1024], 8,
                                       waits=[("ev", evc), ("st", b.val("st"))])
                    b.dma(outap, t["pp_img"][:, :, 0:128], waits=[("ev", evt)], sem="st")
        ld = b.dma(t["wstage"][0:64, 0, :], aps["W_xp"], waits=[("ev", b.val("ev"))])
        evc = b.v_op(lambda e: e.tensor_copy(t["pp_q4"][0:64, 0:1024], t["wstage"][0:64, 0, :]),
                     waits=[("ld", ld), ("tp", b.val("tp"))])
        pe_transpose(b, lambda blk: t["wxpT"][:, blk, :],
                     t["pp_q4"][0:64, 0:1024], 8, waits=[("ev", evc)])
        ld = b.dma(t["wstage"][0:64, 1, :], aps["Dmat"])
        b.v_op(lambda e: e.tensor_single_scalar(t["dmat"][:], t["wstage"][0:64, 1, :],
                                                0.5, ALU.mult), waits=[("ld", ld)])
        b.dma(t["binp"][:], aps["b_in"][0:1024].rearrange("(m p) -> m p", p=128).rearrange("m p -> p m"), slow=True)
        b.dma(t["bxp"][:], aps["b_xp"][:, None], slow=True)
        ld = b.dma(t["stats"][0:64, 32:33], aps["A_log"].rearrange("a b p -> (a b) p").rearrange("a p -> p a"), slow=True)
        sc = b.s_op(lambda e: e.activation(t["stats"][0:64, 33:34], t["stats"][0:64, 32:33], AF.Exp),
                    waits=[("ld", ld)])
        ev = b.v_op(lambda e: e.tensor_single_scalar(t["stats"][0:64, 34:35],
                                                     t["stats"][0:64, 33:34], -1.0, ALU.mult),
                    waits=[("sc", sc)])
        sc = b.s_op(lambda e: e.activation(t["stats"][0:64, 35:36], t["stats"][0:64, 34:35], AF.Exp),
                    waits=[("ev", ev)])
        b.v_op(lambda e: e.memset(t["xs"][0:64, 0:512], 1.0))
        b.v_op(lambda e: e.tensor_scalar(t["abc"][:], t["xs"][0:64, 0:512],
                                         t["stats"][0:64, 35:36], None, ALU.mult),
               waits=[("sc", sc)])
    bcasts = []
    if need_producer:
        bcasts += [("gob", aps["gamma"], None), ("bob", aps["beta"], None),
                   ("bzh", aps["b_in"][1024:2048], 0.5)]
    if need_consumer:
        bcasts += [("outb", aps["b_out"], None)]
    for dstt, srcap, scale in bcasts:
        ld = b.dma(t["rowst"][0:1, 0:1024], srcap[None, :],
                   waits=[("mm", b.val("mm")), ("ev", b.val("ev"))])
        if scale is not None:
            ev = b.v_op(lambda e, scale=scale: e.tensor_single_scalar(
                t["rowst"][0:1, 1024:2048], t["rowst"][0:1, 0:1024], scale, ALU.mult),
                waits=[("ld", ld)])
            row = t["rowst"][0:1, 1024:2048]
        else:
            ev = b.val("ev")
            row = t["rowst"][0:1, 0:1024]
        for h in range(2):
            mmv = b.mm(b.ps[:, 2 + h, :], t["onesr"][:], row[:, h * 512:(h + 1) * 512], True, True,
                       waits=[("ld", ld), ("ev", ev), ("ev", b.val("ev"))])
            b.v_op(lambda e, dstt=dstt, h=h: e.tensor_copy(t[dstt][:, h * 512:(h + 1) * 512],
                                                           b.ps[:, 2 + h, :]),
                   waits=[("mm", mmv)])

def ln_stats_tc(b, S, tc, sq_scs, sc0, waits=None):
    t = b.t
    src = t["x"][:, tc, :]
    xst = t["xs2"] if tc % 2 else t["xs"]
    b.v_op(lambda e, src=src, tc=tc: e.reduce_sum(t["stats"][:, S + tc:S + tc + 1], src, AX.X),
           waits=list(waits or []))
    b.v_op(lambda e, tc=tc: e.tensor_single_scalar(
        t["stats"][:, S + 4 + tc:S + 5 + tc], t["stats"][:, S + tc:S + tc + 1],
        1.0 / 1024.0, ALU.mult))
    wxs = [("sc", sq_scs[-2])] if len(sq_scs) >= 2 else [("sc", sc0)]
    evx = b.v_op(lambda e, src=src, tc=tc, xst=xst: e.tensor_scalar(
        xst[:], src, t["stats"][:, S + 4 + tc:S + 5 + tc], None, ALU.subtract),
        waits=wxs)
    sq = b.s_op(lambda e, tc=tc, xst=xst: e.activation(t["q2d"][:], xst[:], AF.Square,
                                                       accum_out=t["stats"][:, S + 8 + tc:S + 9 + tc]),
                waits=[("ev", evx)])
    sq_scs.append(sq)
    b.v_op(lambda e, tc=tc, xst=xst: e.reduce_max(t["stats"][:, S + 12 + tc:S + 13 + tc], xst[:],
                                                  AX.X, apply_absolute_value=True))

def ln_quant(b, S, extra_w=None, out_guard=None):
    sq_scs = []
    sc0 = b.val("sc")
    for tc in range(4):
        ln_stats_tc(b, S, tc, sq_scs, sc0, waits=(extra_w if tc == 0 else None))
    return ln_quant_rest(b, S, out_guard=out_guard)

def ln_quant_rest(b, S, out_guard=None):
    t = b.t
    tps = []
    sc = b.s_op(lambda e: e.activation(t["stats"][:, S + 44:S + 48], t["stats"][:, S + 8:S + 12],
                                       AF.Ln, bias=t["ceps"][:, 0:1], scale=1.0 / 1024.0),
                waits=[("ev", b.val("ev"))])
    sc = b.s_op(lambda e: e.activation(t["stats"][:, S + 16:S + 20], t["stats"][:, S + 44:S + 48],
                                       AF.Exp, scale=-0.5))
    b.v_op(lambda e: e.tensor_tensor(t["stats"][:, S + 44:S + 48], t["stats"][:, S + 16:S + 20],
                                     t["stats"][:, S + 12:S + 16], ALU.mult), waits=[("sc", sc)])
    b.v_op(lambda e: e.tensor_single_scalar(t["stats"][:, S + 20:S + 24],
                                            t["stats"][:, S + 44:S + 48], 1e-5, ALU.max))
    b.v_recip(t["stats"][:, S + 24:S + 28], t["stats"][:, S + 20:S + 24])
    b.v_op(lambda e: e.tensor_tensor(t["stats"][:, S + 44:S + 48], t["stats"][:, S + 16:S + 20],
                                     t["stats"][:, S + 24:S + 28], ALU.mult))
    b.v_op(lambda e: e.tensor_single_scalar(t["stats"][:, S + 28:S + 32],
                                            t["stats"][:, S + 44:S + 48], 127.0, ALU.mult))
    b.v_op(lambda e: e.tensor_scalar(t["stats"][:, S + 32:S + 36], t["stats"][:, S + 20:S + 24],
                                     1.0 / 127.0, None, ALU.mult))
    b.v_op(lambda e: e.tensor_scalar(t["stats"][:, S + 36:S + 40], t["stats"][:, S + 32:S + 36],
                                     t["mws"][:, 0:1], None, ALU.mult))
    b.v_op(lambda e: e.tensor_scalar(t["stats"][:, S + 40:S + 44], t["stats"][:, S + 32:S + 36],
                                     t["mws"][:, 2:3], None, ALU.mult))
    evs = b.val("ev")
    for tc in range(4):
        src = t["x"][:, tc, :]
        w = [("ev", evs)] if tc == 0 else []
        if out_guard is not None and tc < 2:
            w = w + [out_guard]
        if len(tps) >= 2:
            w.append(("tp", tps[-2]))
        b.v_op(lambda e, src=src, tc=tc: e.tensor_scalar(
            t["xs"][:], src, t["stats"][:, S + 4 + tc:S + 5 + tc],
            t["stats"][:, S + 28 + tc:S + 29 + tc], ALU.subtract, ALU.mult), waits=w,
            drain=False)
        b.v_op(lambda e, tc=tc: e.tensor_single_scalar(
            t["scrx"][:, tc % 2, :], t["xs"][:], C_RND, ALU.add), drain=False)
        evq = b.v_op(lambda e, tc=tc: e.tensor_single_scalar(
            t["q1xn"][:, tc % 2, :], t["scrx"][:, tc % 2, :], C_RND, ALU.subtract))
        tp = b.dma(t["actfm"][:, :, tc * 128:(tc + 1) * 128], t["q1xn"][:, tc % 2, :],
                   waits=[("ev", evq)], sem="tp", transpose=True)
        tps.append(tp)
    return tps

def ffn_preload(b):
    t = b.t
    pre = {}
    pg = getattr(b, "prev_a1g", None)
    for gi in range(2):
        w = [("mma1", pg[14 + gi])] if pg is not None else \
            [("st2", getattr(b, "st2_gate", 0))]
        b.dma(t["wgb"][:, gi, :, :].rearrange("p k t -> p (k t)"),
              b.d["wgT_s"][gi].rearrange("p k t -> p (k t)"), waits=w)
        pre[gi] = b.dma(t["wub"][:, gi, :, :].rearrange("p k t -> p (k t)"),
                        b.d["wuT_s"][gi].rearrange("p k t -> p (k t)"))
    return pre

def ffn_phase(b, S, q1_tps, pre=None, after_pr0=None):
    t = b.t
    wgT_s, wuT_s, wdT_s = b.d["wgT_s"], b.d["wuT_s"], b.d["wdT_s"]
    q2tp = {}
    evr_tc = {}
    a1_groups = {}
    for pair in range(2):
        for n in range(8):
            par = n % 2
            gi = pair * 8 + n
            if pre is not None and gi < 2:
                ldu = pre[gi]
            else:
                w_ld = [("mma1", a1_groups[gi - 2])] if gi >= 2 else \
                    [("st2", getattr(b, "st2_gate", 0))]
                b.dma(t["wgb"][:, par, :, :].rearrange("p k t -> p (k t)"),
                      wgT_s[n].rearrange("p k t -> p (k t)"), waits=w_ld)
                ldu = b.dma(t["wub"][:, par, :, :].rearrange("p k t -> p (k t)"),
                            wuT_s[n].rearrange("p k t -> p (k t)"))
            for tcl in range(2):
                tc = pair * 2 + tcl
                pb = 2 * (gi * 2 + tcl) % 6
                w = [("ld", ldu)]
                if gi == 0:
                    w += [("tp", q1_tps[3])] + b.ejoin()
                prev_ev = b.val("eva1") - 2
                if prev_ev > 0:
                    w.append(("eva1", prev_ev))
                for k in range(8):
                    b.mm(b.ps[:, pb, :], t["actfm"][:, k, tc * 128:(tc + 1) * 128],
                         t["wgb"][:, par, k, :], k == 0, k == 7,
                         waits=w if k == 0 else None, sem="mma1")
                for k in range(8):
                    b.mm(b.ps[:, pb + 1, :], t["actfm"][:, k, tc * 128:(tc + 1) * 128],
                         t["wub"][:, par, k, :], k == 0, k == 7, sem="mma1")
                mmv = b.val("mma1")
                scv = b.s_op(lambda e, tcl=tcl, tc=tc, pb=pb: e.activation(
                    t["silu"][:, tcl, :], b.ps[:, pb, :], AF.Silu,
                    scale=t["stats"][:, S + 36 + tc:S + 37 + tc]),
                    waits=[("mma1", mmv - 1)], drain=False)
                b.v_op(lambda e, tcl=tcl, tc=tc, n=n, pb=pb: e.scalar_tensor_tensor(
                    t["g"][:, tcl, n * 512:(n + 1) * 512], b.ps[:, pb + 1, :],
                    t["stats"][:, S + 40 + tc:S + 41 + tc], t["silu"][:, tcl, :],
                    ALU.mult, ALU.mult),
                    waits=[("mma1", mmv), ("sc", scv)], sem="eva1", drain=False)
                a1_groups[gi] = mmv
        for tcl in range(2):
            tc = pair * 2 + tcl
            b.v_op(lambda e, tcl=tcl, tc=tc: e.reduce_max(
                t["stats"][:, S + 56 + tc:S + 57 + tc], t["g"][:, tcl, :], AX.X,
                apply_absolute_value=True))
            b.v_op(lambda e, tc=tc: e.tensor_single_scalar(
                t["stats"][:, S + 64 + tc:S + 65 + tc], t["stats"][:, S + 56 + tc:S + 57 + tc],
                1e-5, ALU.max))
            b.v_recip(t["stats"][:, S + 60 + tc:S + 61 + tc],
                      t["stats"][:, S + 64 + tc:S + 65 + tc])
            b.v_op(lambda e, tc=tc: e.tensor_single_scalar(
                t["stats"][:, S + 68 + tc:S + 69 + tc], t["stats"][:, S + 60 + tc:S + 61 + tc],
                127.0, ALU.mult))
            evsm = b.v_op(lambda e, tc=tc: e.tensor_scalar(
                t["stats"][:, S + 72 + tc:S + 73 + tc], t["stats"][:, S + 64 + tc:S + 65 + tc],
                t["mws"][:, 4:5], 0.5 / 127.0, ALU.mult, ALU.mult))
            evr = None
            src2 = t["q2tm"][:] if pair == 0 else t["q2t2"][:, tcl, :]
            for qq in range(4):
                sl = slice(qq * 1024, (qq + 1) * 1024)
                w = None
                if qq == 0:
                    w = [("ev", evsm)] + ([("tp", b.val("tp"))] if pair == 0 else [])
                b.v_op(lambda e, tcl=tcl, tc=tc, sl=sl: e.tensor_scalar(
                    t["xs"][:], t["g"][:, tcl, sl], t["stats"][:, S + 68 + tc:S + 69 + tc],
                    C_RND, ALU.mult, ALU.add), waits=w, drain=False)
                evr = b.v_op(lambda e, sl=sl, src2=src2: e.tensor_single_scalar(
                    src2[:, sl], t["xs"][:], C_RND, ALU.subtract))
            evr_tc[tc] = evr
            if pair == 0:
                q2tp[tc] = b.dma(t["q2fmA"][:, :, tc * 128:(tc + 1) * 128], src2,
                                 waits=[("ev", evr)], sem="tp", transpose=True)
    evA2 = {}
    wd_load0 = b.val("wdk")
    for pr in range(2):
        tc_e, tc_o = 2 * pr, 2 * pr + 1
        if pr == 1:
            for tc in (2, 3):
                q2tp[tc] = b.dma(t["q2fmB"][:, :, (tc - 2) * 128:(tc - 1) * 128],
                                 t["q2t2"][:, tc - 2, :],
                                 waits=[("ev", evr_tc[tc])], sem="tp", transpose=True)
        for ld_i in range(16):
            gl = b.val("wdk") - wd_load0
            w = []
            if pr == 0 and ld_i == 0:
                w = [("st2", getattr(b, "st2_gate", 0))]
            wdk_prev = b.val("wdk") - 2
            if wdk_prev > 0:
                w.append(("wdk", wdk_prev))
            buf = gl % 3
            b.dma(t["wdb"][:, buf, :, :], wdT_s[ld_i * 2:ld_i * 2 + 2].transpose([1, 0, 2]),
                  waits=w, sem="wd")
            if getattr(b, "wd_log", None) is not None:
                b.dma(b.wd_log[gl], t["wdb"][:, buf, :, :], sem="st")
            wdv = b.val("wd")
            for kk in range(2):
                k = ld_i * 2 + kk
                w2 = [("wd", wdv)] if kk == 0 else []
                if k == 0:
                    w2 += [("eva2", b.val("eva2")), ("tp", q2tp[2 * pr + 1])]
                    if pr == 0:
                        w2 += [("eva1", b.val("eva1") - 2), ("evb2", b.val("evb2"))]
                for ci, tc in enumerate((tc_e, tc_o)):
                    pb = 4 + 2 * ci
                    qfm = t["q2fmA"] if tc < 2 else t["q2fmB"]
                    for h in range(2):
                        b.mm(b.ps[:, pb + h, :], qfm[:, k, (tc % 2) * 128:(tc % 2 + 1) * 128],
                             t["wdb"][:, buf, kk, h * 512:(h + 1) * 512],
                             k == 0, k == 31,
                             waits=w2 if (ci == 0 and h == 0) else None, sem="mma2",
                             do_inc=(k == 31 and ci == 1 and h == 1))
            b.mm(b.ps[0:1, 0, 0:1], t["conek"][:], t["conek"][:], True, True,
                 sem="wdk", do_inc=True)
        mmv = b.val("mma2")
        for ci, tc in enumerate((tc_e, tc_o)):
            pb = 4 + 2 * ci
            for h in range(2):
                b.v_op(lambda e, tc=tc, h=h, pb=pb: e.scalar_tensor_tensor(
                    t["x"][:, tc, h * 512:(h + 1) * 512], b.ps[:, pb + h, :],
                    t["stats"][:, S + 72 + tc:S + 73 + tc],
                    t["x"][:, tc, h * 512:(h + 1) * 512], ALU.mult, ALU.add),
                    waits=[("mma2", mmv)], sem="eva2")
            evA2[tc] = b.val("eva2")
        if pr == 0 and after_pr0 is not None:
            after_pr0(evA2)
    b.prev_a1g = a1_groups
    b.last_evr3 = evr_tc[3]
    return evA2

def lnb_front(b, S, tc, evA2_tc, sq_scs, sc0):
    t = b.t
    src = t["x"][:, tc, :]
    xst = t["xs2"] if tc % 2 else t["xs"]
    b.v_op(lambda e, src=src, tc=tc: e.reduce_sum(
        t["stats"][:, S + tc:S + tc + 1], src, AX.X), waits=[("eva2", evA2_tc)])
    b.v_op(lambda e, tc=tc: e.tensor_single_scalar(
        t["stats"][:, S + 4 + tc:S + 5 + tc], t["stats"][:, S + tc:S + tc + 1],
        1.0 / 1024.0, ALU.mult))
    wxs = [("sc", sq_scs[-2])] if len(sq_scs) >= 2 else [("sc", sc0)]
    evx = b.v_op(lambda e, src=src, tc=tc, xst=xst: e.tensor_scalar(
        xst[:], src, t["stats"][:, S + 4 + tc:S + 5 + tc], None, ALU.subtract),
        waits=wxs)
    sq = b.s_op(lambda e, tc=tc, xst=xst: e.activation(
        t["q2d"][:], xst[:], AF.Square,
        accum_out=t["stats"][:, S + 8 + tc:S + 9 + tc]), waits=[("ev", evx)])
    sq_scs.append(sq)
    b.s_op(lambda e, tc=tc: e.activation(t["stats"][:, S + 12 + tc:S + 13 + tc],
                                         t["stats"][:, S + 8 + tc:S + 9 + tc],
                                         AF.Ln, bias=t["ceps"][:, 0:1], scale=1.0 / 1024.0))
    return b.s_op(lambda e, tc=tc: e.activation(t["stats"][:, S + 16 + tc:S + 17 + tc],
                                                t["stats"][:, S + 12 + tc:S + 13 + tc],
                                                AF.Exp, scale=-0.5))

def lnb_backc(b, S, tc, sc_exp, w_tp):
    t = b.t
    src = t["x"][:, tc, :]
    xst = t["xs2"] if tc % 2 else t["xs"]
    ev_st = b.v_op(lambda e, src=src, tc=tc, xst=xst: e.scalar_tensor_tensor(
        src, xst[:], t["stats"][:, S + 16 + tc:S + 17 + tc], src,
        ALU.mult, ALU.add), waits=[("sc", sc_exp)])
    xn1 = t["scrx"][:, tc % 2, :]
    b.v_op(lambda e, xn1=xn1, tc=tc, xst=xst: e.scalar_tensor_tensor(
        xn1, xst[:], t["stats"][:, S + 16 + tc:S + 17 + tc], t["gob"][:],
        ALU.mult, ALU.mult), drain=False)
    evq = b.v_op(lambda e, xn1=xn1, tc=tc: e.tensor_tensor(
        t["q1xn"][:, tc % 2, :], xn1, t["bob"][:], ALU.add), waits=[w_tp])
    return ev_st, evq

def phase_b(b, S, tile, evA2, x1t_dst, yz_dst, lnb=None, mid_hook=None):
    t = b.t
    j = tile % 2
    if not lnb:
        lnb = {"sq_scs": [], "sc0": b.val("sc"), "exps": {}, "backs": {}}
        for tc in (0, 1):
            lnb["exps"][tc] = lnb_front(b, S, tc, evA2[tc], lnb["sq_scs"], lnb["sc0"])
        for tc in (0, 1):
            lnb["backs"][tc] = lnb_backc(b, S, tc, lnb["exps"][tc], ("tp", b.val("tp")))
    sq_scs, exps, backs = lnb["sq_scs"], lnb["exps"], lnb["backs"]
    ldp = b.dma(t["winPb"][:].rearrange("p m k c -> p (m k c)"),
                b.d["winP_s"][:].rearrange("p m k c -> p (m k c)"),
                waits=[("ev", b.val("ev")), ("eva1", b.val("eva1")),
                       ("tp", b.val("tp"))])
    xn_tps = []
    for tc in (0, 1):
        ev_st, evq = backs[tc]
        b.dma(x1t_dst(tc), t["x"][:, tc, :], waits=[("ev", ev_st)], sem="st")
        xn_tps.append(b.dma(t["actfm"][:, :, tc * 128:(tc + 1) * 128],
                            t["q1xn"][:, tc % 2, :],
                            waits=[("ev", evq)], sem="tp", transpose=True))
    ldz = b.dma(t["winZb"][:].rearrange("p k z -> p (k z)"),
                b.d["winZ_s"][:].rearrange("p k z -> p (k z)"))
    for tc in (2, 3):
        exps[tc] = lnb_front(b, S, tc, evA2[tc], sq_scs, lnb["sc0"])
    for tc in (2, 3):
        ev_st, evq = lnb_backc(b, S, tc, exps[tc], ("tp", xn_tps[tc - 2]))
        b.dma(x1t_dst(tc), t["x"][:, tc, :], waits=[("ev", ev_st)], sem="st")
        xn_tps.append(b.dma(t["actfm"][:, :, tc * 128:(tc + 1) * 128],
                            t["q1xn"][:, tc % 2, :],
                            waits=[("ev", evq)], sem="tp", transpose=True))
    if mid_hook is not None:
        mid_hook()
    for m in range(8):
        pb = m % 2
        if m == 0:
            w = [("ld", ldp), ("tp", xn_tps[3])] + b.ejoin()
        else:
            w = [("evb", max(0, b.val("evb") - 1))]
        for k in range(8):
            b.mm(b.ps[:, pb, :], t["winPb"][:, m, k, :], t["actfm"][:, k, :],
                 k == 0, k == 7, waits=w if k == 0 else None, sem="mmb")
        mmv = b.val("mmb")
        b.s_op(lambda e, m=m, pb=pb: e.activation(
            t["xp"][:, m, :], b.ps[:, pb, :], AF.Silu, bias=t["binp"][:, m:m + 1]),
            waits=[("mmb", mmv)], sem="evb")
    evb_xp = b.val("evb")
    for k in range(8):
        b.mm(b.ps[0:64, 2, :], t["wxpT"][:, k, :], t["xp"][:, k, :], k == 0, k == 7,
             waits=[("evb", evb_xp)] if k == 0 else None, sem="mmb")
    mmv = b.val("mmb")
    b.v_op(lambda e: e.tensor_scalar(t["u_s"][:], b.ps[0:64, 2, :], t["bxp"][:, 0:1],
                                     None, ALU.add), waits=[("mmb", mmv)], sem="evb")
    init = 0.0 if tile < 2 else t["hcar"][:, j:j + 1]
    b.v_op(lambda e, init=init: e.tensor_tensor_scan(
        t["H"][:], t["abc"][:], t["u_s"][:], init, ALU.mult, ALU.add), sem="evb")
    b.v_op(lambda e: e.tensor_copy(t["hcar"][:, j:j + 1], t["H"][:, 511:512]), sem="evb")
    ev_hb = b.v_op(lambda e: e.tensor_copy(t["Hb"][:], t["H"][:]), sem="evb")
    th_done = {}
    for tc in range(4):
        pbz = 3 + (tc % 2) * 2
        for k in range(8):
            w = []
            if k == 0:
                w = [("evb2", max(0, b.val("evb2") - 4))]
                if tc == 0:
                    w += [("ld", ldz), ("eva2", evA2[3]), ("eva1", b.val("eva1"))]
            b.mm(b.ps[:, pbz, :], t["actfm"][:, k, tc * 128:(tc + 1) * 128],
                 t["winZb"][:, k, 0:512], k == 0, k == 7,
                 waits=w if k == 0 else None, sem="mmb")
            b.mm(b.ps[:, pbz + 1, :], t["actfm"][:, k, tc * 128:(tc + 1) * 128],
                 t["winZb"][:, k, 512:1024], k == 0, k == 7, sem="mmb")
        mmz = b.val("mmb")
        for h in range(2):
            ev = b.v_op(lambda e, h=h, pbz=pbz: e.scalar_tensor_tensor(
                t["silu"][:, h, :], b.ps[:, pbz + h, :], 0.5,
                t["bzh"][:, h * 512:(h + 1) * 512], ALU.mult, ALU.add),
                waits=[("mmb", mmz)], sem="evb2", drain=False)
            b.s_op(lambda e, tc=tc, h=h: e.activation(
                t["th"][:, tc, h * 512:(h + 1) * 512], t["silu"][:, h, :], AF.Tanh),
                waits=[("evb2", ev)], sem="evb2")
        th_done[tc] = b.val("evb2")
    mm_h0 = {}
    for tc in range(4):
        w = [("evb", ev_hb)]
        if tc == 3:
            w.append(("evb2", th_done[2]))
        b.mm(b.ps[:, tc, :], t["Hb"][:, tc * 128:(tc + 1) * 128], t["dmat"][:, 0:512],
             True, True, waits=w, sem="mmb")
        mm_h0[tc] = b.val("mmb")
    for tc in range(4):
        b.v_op(lambda e, tc=tc: e.scalar_tensor_tensor(
            t["yztm"][:, tc, 0:512], t["th"][:, tc, 0:512], 1.0, b.ps[:, tc, :],
            ALU.add, ALU.mult), waits=[("mmb", mm_h0[tc]), ("evb2", th_done[tc])],
            sem="evb3", drain=False)
        b.mm(b.ps[:, tc, :], t["Hb"][:, tc * 128:(tc + 1) * 128], t["dmat"][:, 512:1024],
             True, True, waits=[("evb3", b.val("evb3"))], sem="mmb")
        mm_h0[tc] = b.val("mmb")
    for tc in range(4):
        b.v_op(lambda e, tc=tc: e.scalar_tensor_tensor(
            t["yztm"][:, tc, 512:1024], t["th"][:, tc, 512:1024], 1.0, b.ps[:, tc, :],
            ALU.add, ALU.mult), waits=[("mmb", mm_h0[tc])], sem="evb3",
            drain=(tc == 3))
    st = b.dma(yz_dst, t["yztm"][:], waits=[("evb3", b.val("evb3"))], sem="st")
    return st

def pass2_wout(b, S, u, yz_srcs, x1t_src, first, tc_hook=None):
    t = b.t
    out_guard = b.val("out")
    if first:
        b.dma(t["woutb"][:].rearrange("p k o -> p (k o)"),
              b.d["woutT_s"][:].rearrange("p k o -> p (k o)"), sem="ld",
              waits=[("tp", b.val("tp")), ("ev", b.val("ev"))])
    w_wout = b.val("ld")
    if isinstance(yz_srcs, int):
        tp_yz = yz_srcs
    elif callable(yz_srcs):
        tp_yz = yz_srcs()
    else:
        tp_yz = None
        for jj in range(2):
            w = [("mmw", b.val("mmw"))]
            tp_yz = b.dma(t["yzfm"][:, :, jj * 256:(jj + 1) * 256], yz_srcs[jj],
                          waits=w, sem="tp", transpose=True)
    evw = {}
    for tc in range(4):
        par = tc % 2
        ld_x1 = b.dma(t["x1ts"][:, par, :], x1t_src(tc),
                      waits=[("evw", max(0, b.val("evw") - 4)), ("st", b.val("st")),
                             ("evb3", b.val("evb3")), ("eva2", b.val("eva2"))], sem="ld")
        pb = par * 2
        for k in range(8):
            w = []
            if k == 0:
                w = [("tp", tp_yz), ("ld", w_wout)]
                if tc == 0:
                    w += b.ejoin()
                prev = b.val("evw") - 4
                if prev > 0:
                    w.append(("evw", prev))
            b.mm(b.ps[:, pb, :], t["yzfm"][:, k, tc * 128:(tc + 1) * 128],
                 t["woutb"][:, k, 0:512], k == 0, k == 7, waits=w if k == 0 else None, sem="mmw")
            b.mm(b.ps[:, pb + 1, :], t["yzfm"][:, k, tc * 128:(tc + 1) * 128],
                 t["woutb"][:, k, 512:1024], k == 0, k == 7, sem="mmw")
        mmv = b.val("mmw")
        for h in range(2):
            b.v_op(lambda e, h=h, pb=pb: e.tensor_tensor(
                t["ebuf"][:, h * 512:(h + 1) * 512], b.ps[:, pb + h, :],
                t["outb"][:, h * 512:(h + 1) * 512], ALU.add),
                waits=[("mmw", mmv), ("ld", ld_x1)], sem="evw", drain=False)
            b.v_op(lambda e, tc=tc, h=h, par=par: e.tensor_tensor(
                t["x"][:, tc, h * 512:(h + 1) * 512], t["x1ts"][:, par, h * 512:(h + 1) * 512],
                t["ebuf"][:, h * 512:(h + 1) * 512], ALU.add),
                waits=[("out", out_guard)], sem="evw")
        evw[tc] = b.val("evw")
        if tc_hook is not None:
            tc_hook(tc)
    return evw

def weight_aps(nc):
    aps = {}
    names = [("w_gate", [4096, 1024]), ("w_up", [4096, 1024]), ("w_down", [1024, 4096]),
             ("W_in", [2048, 1024]), ("W_xp", [64, 1024]), ("W_out", [1024, 1024]),
             ("Dmat", [64, 1024]), ("A_log", [1, 1, 64]), ("b_in", [2048]), ("b_xp", [64]),
             ("b_out", [1024]), ("gamma", [1024]), ("beta", [1024])]
    for nm, shp in names:
        aps[nm] = nc.dram_tensor(nm, shp, F32, kind="ExternalInput").ap()
    return aps

def dram_scratch(b, nc):
    b.d["wgT_s"] = nc.dram_tensor("wgT_s", [8, 128, 8, 512], BF16).ap()
    b.d["wuT_s"] = nc.dram_tensor("wuT_s", [8, 128, 8, 512], BF16).ap()
    b.d["wdT_s"] = nc.dram_tensor("wdT_s", [32, 128, 1024], BF16).ap()
    b.d["winP_s"] = nc.dram_tensor("winP_s", [128, 8, 8, 128], BF16).ap()
    b.d["winZ_s"] = nc.dram_tensor("winZ_s", [128, 8, 1024], BF16).ap()
    b.d["woutT_s"] = nc.dram_tensor("woutT_s", [128, 8, 1024], BF16).ap()

def setup_b(nc):
    smap = build_smap()
    b = B()
    b.t = {nm: nc.alloc_sbuf_tensor_at(nm, shp, dt, offset=off)
           for nm, (shp, dt, off) in smap.items.items()}
    b.ps = nc.alloc_psum_tensor("ps", [128, 8, 512], F32)
    ident = nc.inline_tensor(np.eye(128, dtype=NPBF), name="ident128")
    b.d["ident"] = ident.ap()
    dram_scratch(b, nc)
    return b

def pe_transpose(b, dst, src_ap, nblk, waits=None, sem="ev"):
    t = b.t
    ev = None
    for blk in range(nblk):
        pb = 2 + blk % 2
        w = list(waits or []) if blk == 0 else []
        prev = b.val("ev") - 1
        if blk >= 2:
            w.append(("ev", b.val("ev") - 1))
        b.mm(b.ps[:, pb, 0:128], src_ap[:, blk * 128:(blk + 1) * 128], t["ident128"][:],
             True, True, waits=w, sem="mm") if False else None
        P = src_ap.shape[0]
        def fn(e, pb=pb, blk=blk, P=P):
            return e.transpose(b.ps[:, pb, 0:P // 2].bitcast(BF16),
                               src_ap[:, blk * 128:(blk + 1) * 128],
                               t["ident128"][0:P, 0:P])
        b.emit("tensor", fn, w, ("mm", 1))
        mmv = b.val("mm")
        ev = b.v_op(lambda e, pb=pb, blk=blk, P=P: e.tensor_copy(
            dst(blk), b.ps[:, pb, 0:P // 2].bitcast(BF16)), waits=[("mm", mmv)], drain=False)
    return ev

def emit_pass1(b, x_in, x1t_out, yz_out, quarter_hook=None):
    t = b.t

    def emit_ldx(tile, w):
        q, j = tile // 2, tile % 2
        return b.dma(t["x"][:], x_in[j, q * 512:q * 512 + 512, :]
                     .rearrange("(c p) f -> p c f", p=128), waits=w)

    nxt = {}
    for tile in range(8):
        q, j = tile // 2, tile % 2
        l0 = q * 512
        if tile == 0:
            ldx = emit_ldx(0, [("tp", b.val("tp")), ("ev", b.val("ev")), ("sc", b.val("sc"))])
            pre = ffn_preload(b)
            sq_scs, sc0 = [], b.val("sc")
            for tc in range(4):
                ln_stats_tc(b, 0, tc, sq_scs, sc0, waits=([("ld", ldx)] if tc == 0 else None))
        else:
            pre = nxt["pre"]
        x1v = x1t_out[j].rearrange("(c p) f -> p c f", p=128)
        q1_tps = ln_quant_rest(b, 0)
        lnb = {}

        def after_pr0(evA2, lnb=lnb):
            lnb.update(sq_scs=[], sc0=b.val("sc"), exps={}, backs={})
            for tc in (0, 1):
                lnb["exps"][tc] = lnb_front(b, 0, tc, evA2[tc], lnb["sq_scs"], lnb["sc0"])
            for tc in (0, 1):
                lnb["backs"][tc] = lnb_backc(b, 0, tc, lnb["exps"][tc],
                                             ("tp", b.val("tp")))
        evA2 = ffn_phase(b, 0, q1_tps, pre=pre, after_pr0=after_pr0)

        def mid_hook(tile=tile):
            if tile == 7:
                return
            ldx2 = emit_ldx(tile + 1, [("st", b.val("st"))])
            nxt["pre"] = ffn_preload(b)
            sq_scs, sc0 = [], b.val("sc")
            for tc in range(4):
                ln_stats_tc(b, 0, tc, sq_scs, sc0, waits=([("ld", ldx2)] if tc == 0 else None))
        phase_b(b, 0, tile, evA2,
                x1t_dst=lambda tc, x1v=x1v, l0=l0: x1v[:, (l0 // 128) + tc, :],
                yz_dst=yz_out[q, j].rearrange("(c p) f -> p c f", p=128), lnb=lnb,
                mid_hook=mid_hook)
        if quarter_hook is not None and tile % 2 == 1:
            quarter_hook(tile // 2)
    b.emit("sync", lambda e: e.nop(), [("st", b.val("st")), ("out", b.val("out"))], None)

def emit_pass2(b, yz_in_fn, x1t_src_fn, y_out_fn, prefetch=False):
    t = b.t
    nxt_tp = None
    for u in range(8):
        sq_scs, sc0 = [], b.val("sc")

        def tc_hook(tc, sq_scs=sq_scs, sc0=sc0):
            ln_stats_tc(b, 0, tc, sq_scs, sc0)
        yz = yz_in_fn(u) if nxt_tp is None else nxt_tp
        evw = pass2_wout(b, 0, u, yz, lambda tc, u=u: x1t_src_fn(u, tc),
                         first=(u == 0), tc_hook=tc_hook)
        pre = ffn_preload(b)
        q1_tps = ln_quant_rest(b, 0, out_guard=("out", b.val("out")))
        evA2 = ffn_phase(b, 0, q1_tps, pre=pre)
        if prefetch and u < 7:
            nxt_tp = yz_in_fn(u + 1)()
        for tc in range(4):
            b.dma(y_out_fn(u, tc), t["x"][:, tc, :], waits=[("eva2", evA2[tc])], sem="out")
    b.emit("sync", lambda e: e.nop(), [("out", b.val("out"))], None)

def _build_p1():
    nc = bass.Bass("TRN2", target_bir_lowering=False)
    b = setup_b(nc)
    aps = weight_aps(nc)
    x_in = nc.dram_tensor("x", [2, 2048, 1024], F32, kind="ExternalInput").ap()
    x1t_out = nc.dram_tensor("x1t", [2, 2048, 1024], F32, kind="ExternalOutput").ap()
    yz_out = nc.dram_tensor("yz", [4, 2, 512, 1024], BF16, kind="ExternalOutput").ap()
    prepass_consts(b)
    g1 = prepass_ffn(b, aps["w_gate"], 4096, 1024, b.d["wgT_s"], 0)
    g2 = prepass_ffn(b, aps["w_up"], 4096, 1024, b.d["wuT_s"], 2, gate=g1)
    prepass_ffn(b, aps["w_down"], 1024, 4096, b.d["wdT_s"], 4, gate=g2)
    b.st2_gate = b.val("st2")
    prepass_attn(b, aps, need_producer=True, need_consumer=False)
    emit_pass1(b, x_in, x1t_out, yz_out)
    finalize(b, nc, SEMS)
    return nc

def _prepass_wout(b, aps):
    t = b.t
    for mp in range(0, 8, 2):
        ld = b.dma(t["wstage"][:, 0:2, :],
                   aps["W_out"][mp * 128:(mp + 2) * 128, :].rearrange("(m p) i -> p m i", p=128),
                   waits=[("ev", b.val("ev"))])
        evc = b.v_op(lambda e: e.tensor_copy(
            t["pp_q4"][:].rearrange("p (m f) -> p m f", m=4)[:, 0:2, :],
            t["wstage"][:, 0:2, :]), waits=[("ld", ld), ("tp", b.val("tp"))])
        for mm_ in range(2):
            m = mp + mm_
            evt = pe_transpose(b, lambda blk: t["pp_img"][:, blk, 0:128],
                               t["pp_q4"][:, mm_ * 1024:(mm_ + 1) * 1024], 8,
                               waits=[("ev", evc), ("st", b.val("st"))])
            b.dma(b.d["woutT_s"][:, :, m * 128:(m + 1) * 128], t["pp_img"][:, :, 0:128],
                  waits=[("ev", evt)], sem="st")

def _build_p2():
    nc = bass.Bass("TRN2", target_bir_lowering=False)
    b = setup_b(nc)
    aps = weight_aps(nc)
    x1t_in = nc.dram_tensor("x1t", [2, 2048, 1024], F32, kind="ExternalInput").ap()
    yz_in = nc.dram_tensor("yzin", [8, 2, 256, 1024], BF16, kind="ExternalInput").ap()
    y_out = nc.dram_tensor("y", [2, 2048, 1024], F32, kind="ExternalOutput").ap()
    prepass_consts(b)
    g1 = prepass_ffn(b, aps["w_gate"], 4096, 1024, b.d["wgT_s"], 0)
    g2 = prepass_ffn(b, aps["w_up"], 4096, 1024, b.d["wuT_s"], 2, gate=g1)
    prepass_ffn(b, aps["w_down"], 1024, 4096, b.d["wdT_s"], 4, gate=g2)
    b.st2_gate = b.val("st2")
    prepass_attn(b, aps, need_producer=False, need_consumer=True)
    _prepass_wout(b, aps)

    def yz_src(u):
        return [yz_in[u, jj] for jj in range(2)]

    def x1t_src(u, tc):
        jj, ih = tc // 2, tc % 2
        b2 = 2 * u + jj
        return bass.AP(x1t_in.tensor, (ih * 2048 + b2) * 1024, [[16 * 1024, 128], [1, 1024]])

    def y_dst(u, tc):
        jj, ih = tc // 2, tc % 2
        b2 = 2 * u + jj
        return bass.AP(y_out.tensor, (ih * 2048 + b2) * 1024, [[16 * 1024, 128], [1, 1024]])

    emit_pass2(b, yz_src, x1t_src, y_dst)
    finalize(b, nc, SEMS)
    return nc

def _build_fused():
    nc = bass.Bass("TRN2", target_bir_lowering=False, num_devices=8)
    b = setup_b(nc)
    aps = weight_aps(nc)
    t = b.t
    x_in = nc.dram_tensor("x", [2, 2048, 1024], F32, kind="ExternalInput").ap()
    y_out = nc.dram_tensor("y", [2, 2048, 1024], F32, kind="ExternalOutput").ap()
    x1t_s = nc.dram_tensor("x1t_s", [2, 2048, 1024], F32, kind="Internal").ap()
    yz_send = nc.dram_tensor("yz_send", [4, 2, 512, 1024], BF16, kind="Internal").ap()
    gat = nc.dram_tensor("gat", [4, 8, 2, 512, 1024], BF16, kind="Internal").ap()
    prepass_consts(b)
    g1 = prepass_ffn(b, aps["w_gate"], 4096, 1024, b.d["wgT_s"], 0)
    g2 = prepass_ffn(b, aps["w_up"], 4096, 1024, b.d["wuT_s"], 2, gate=g1)
    prepass_ffn(b, aps["w_down"], 1024, 4096, b.d["wdT_s"], 4, gate=g2)
    b.st2_gate = b.val("st2")
    prepass_attn(b, aps, need_producer=True, need_consumer=True)
    _prepass_wout(b, aps)

    emit_pass1(b, x_in, x1t_s, yz_send, quarter_hook=None)
    for q in range(4):
        def ag_fn(g, q=q):
            return g.collective_compute(
                "AllGather", mybir.AluOpType.bypass,
                replica_groups=[list(range(NCORE))],
                ins=[yz_send[q]], outs=[gat[q]])
        b.emit("gpsimd", ag_fn,
               [("st", b.val("st")), ("tp", b.val("tp")), ("evb3", b.val("evb3"))],
               ("cc", 1))

    b.dyn = {}

    def rank_fn(g):
        rank = g.compute_val(g.partition_id())
        b.dyn["rq"] = nc.s_assert_within(rank // 2, 0, 3)
        b.dyn["rm256"] = nc.s_assert_within((rank % 2) * 256, 0, 256)
        return g.engine_nop()
    b.emit("gpsimd", rank_fn, [("cc", 4)], ("cc", 1))

    def yz_loader(u):
        def load():
            w = [("cc", 5), ("tp", b.val("tp")), ("ev", getattr(b, "last_evr3", 0)),
                 ("mmb", b.val("mmb")), ("st", b.val("st")), ("evb3", b.val("evb3"))]
            for jj in range(2):
                def fn(g, u=u, jj=jj):
                    in_ = gat[ds(b.dyn["rq"], 1), u, jj, ds(b.dyn["rm256"], 256), :]
                    in_ = in_.squeeze(0).rearrange("(c p) f -> p c f", p=128)
                    return g.dma_start(out=t["yztm2"][:, jj * 2:jj * 2 + 2, :], in_=in_)
                fn._is_dma = True
                b.emit("gpsimd", fn, w, ("ld", 16))
                w = []
            ldv = b.ctr["ld"]
            tp = None
            for tc in range(4):
                tp = b.dma(t["yzfm"][:, :, tc * 128:(tc + 1) * 128], t["yztm2"][:, tc, :],
                           waits=[("ld", ldv), ("mmw", b.val("mmw"))], sem="tp",
                           transpose=True)
            return tp
        return load

    def x1t_src(u, tc):
        jj, ih = tc // 2, tc % 2
        b2 = 2 * u + jj
        return bass.AP(x1t_s.tensor, (ih * 2048 + b2) * 1024, [[16 * 1024, 128], [1, 1024]])

    def y_dst(u, tc):
        jj, ih = tc // 2, tc % 2
        b2 = 2 * u + jj
        return bass.AP(y_out.tensor, (ih * 2048 + b2) * 1024, [[16 * 1024, 128], [1, 1024]])

    emit_pass2(b, yz_loader, x1t_src, y_dst, prefetch=PREFETCH)
    finalize(b, nc, SEMS)
    return nc

NCORE = 8

def kernel(L0_attn_x, L0_attn_A_log, L0_attn_Dmat, L0_attn_W_in, L0_attn_b_in,
           L0_attn_W_xp, L0_attn_b_xp, L0_attn_W_out, L0_attn_b_out, L0_attn_gamma,
           L0_attn_beta, L0_ffn_w_gate, L0_ffn_w_up, L0_ffn_w_down):
    wmap = dict(
        w_gate=np.ascontiguousarray(L0_ffn_w_gate, dtype=np.float32),
        w_up=np.ascontiguousarray(L0_ffn_w_up, dtype=np.float32),
        w_down=np.ascontiguousarray(L0_ffn_w_down, dtype=np.float32),
        W_in=np.ascontiguousarray(L0_attn_W_in, dtype=np.float32),
        W_xp=np.ascontiguousarray(L0_attn_W_xp, dtype=np.float32),
        W_out=np.ascontiguousarray(L0_attn_W_out, dtype=np.float32),
        Dmat=np.ascontiguousarray(L0_attn_Dmat, dtype=np.float32),
        A_log=np.ascontiguousarray(L0_attn_A_log, dtype=np.float32),
        b_in=np.ascontiguousarray(L0_attn_b_in, dtype=np.float32),
        b_xp=np.ascontiguousarray(L0_attn_b_xp, dtype=np.float32),
        b_out=np.ascontiguousarray(L0_attn_b_out, dtype=np.float32),
        gamma=np.ascontiguousarray(L0_attn_gamma, dtype=np.float32),
        beta=np.ascontiguousarray(L0_attn_beta, dtype=np.float32),
    )
    x = np.ascontiguousarray(L0_attn_x, dtype=np.float32)
    if FUSED:
        nc = _build_fused()
        in_maps = [dict(wmap, x=np.ascontiguousarray(x[2 * c:2 * c + 2])) for c in range(8)]
        res = run_bass_kernel_spmd(nc, in_maps, core_ids=list(range(8)))
        out = np.stack([np.asarray(res.results[c]["y"]) for c in range(8)])
        return out.reshape(16, 2048, 1024)
    nc1 = _build_p1()
    in_maps = [dict(wmap, x=np.ascontiguousarray(x[2 * c:2 * c + 2])) for c in range(8)]
    res1 = run_bass_kernel_spmd(nc1, in_maps, core_ids=list(range(8)))
    x1t = [res1.results[c]["x1t"] for c in range(8)]
    yz = np.stack([np.asarray(res1.results[c]["yz"]).view(NPBF) if res1.results[c]["yz"].dtype != NPBF
                   else res1.results[c]["yz"] for c in range(8)])
    nc2 = _build_p2()
    in_maps2 = []
    for c in range(8):
        yzin = np.ascontiguousarray(yz[:, c // 2, :, (c % 2) * 256:(c % 2) * 256 + 256, :])
        in_maps2.append(dict(wmap, x1t=np.ascontiguousarray(x1t[c]), yzin=yzin))
    res2 = run_bass_kernel_spmd(nc2, in_maps2, core_ids=list(range(8)))
    out = np.concatenate([np.asarray(res2.results[c]["y"])[None] for c in range(8)], axis=0)
    return out.reshape(16, 2048, 1024)


# version 16
# speedup vs baseline: 1.0625x; 1.0023x over previous
import numpy as np
import ml_dtypes
import concourse.bass as bass
import concourse.mybir as mybir
from concourse.bass_utils import run_bass_kernel_spmd

F32 = mybir.dt.float32
BF16 = mybir.dt.bfloat16
AF = mybir.ActivationFunctionType
ALU = mybir.AluOpType
AX = mybir.AxisListType
NPBF = ml_dtypes.bfloat16
ds = bass.ds

FUSED = True
PREFETCH = False
C_RND = float(np.float32(1.5 * 2 ** 23))
EPS = 1e-5
CLIP_W = 1.4999999

EV_SEMS = ("ev", "eva1", "eva2", "evb", "evb2", "evb3", "evw", "sc")

def _ru(x, a=32):
    return (x + a - 1) // a * a

class SMap:
    def __init__(self):
        self.off = 0
        self.items = {}

    def add(self, name, shape, dt, at=None):
        nbytes = int(np.prod(shape[1:])) * mybir.dt.size(dt)
        o = self.off if at is None else at
        self.items[name] = (shape, dt, o)
        if at is None:
            self.off = _ru(o + nbytes)
        return o + nbytes

SBUF_BASE = 16512

def build_smap():
    m = SMap()
    m.off = SBUF_BASE
    m.add("dmat", [64, 1024], BF16)
    m.add("wxpT", [128, 8, 64], BF16)
    m.add("binp", [128, 8], F32)
    m.add("bxp", [64, 1], F32)
    m.add("gob", [128, 1024], F32)
    m.add("bob", [128, 1024], F32)
    m.add("outb", [128, 1024], F32)
    m.add("bzh", [128, 1024], F32)
    m.add("abc", [64, 512], F32)
    m.add("stats", [128, 96], F32)
    m.add("mws", [128, 8], F32)
    m.add("hcar", [64, 2], F32)
    m.add("onesr", [1, 128], F32)
    m.add("onec", [128, 1], F32)
    m.add("conek", [128, 1], BF16)
    m.add("ident128", [128, 128], BF16)
    m.add("ceps", [128, 1], F32)
    m.add("x", [128, 4, 1024], F32)
    m.add("scrx", [128, 2, 1024], F32)
    m.add("xs", [128, 1024], F32)
    m.add("q1xn", [128, 2, 1024], BF16)
    m.add("actfm", [128, 8, 512], BF16)
    m.add("silu", [128, 2, 512], F32)
    m.add("wgb", [128, 2, 8, 512], BF16)
    m.add("wub", [128, 2, 8, 512], BF16)
    m.add("wdb", [128, 3, 2, 1024], BF16)
    goff = m.off
    m.add("g", [128, 2, 4096], F32)
    q2off = m.add("q2tm", [128, 4096], BF16) - 8192
    fmoff = m.add("q2fmA", [128, 32, 256], BF16) - 16384
    m.add("q2fmB", [128, 32, 256], BF16)
    aend = m.off
    m.add("xs2", [128, 1024], F32, at=q2off)
    m.add("q2d", [128, 1024], BF16, at=q2off + 4096)
    m.add("q2t2", [128, 2, 4096], BF16, at=goff)
    m.add("yztm2", [128, 4, 1024], BF16, at=goff + 16384)
    m.add("woutb", [128, 8, 1024], BF16)
    m.add("yzfm", [128, 8, 512], BF16)
    m.budget = m.off
    m.add("winPb", [128, 8, 8, 128], BF16, at=goff)
    m.add("winZb", [128, 8, 1024], BF16, at=goff + 16384)
    m.add("xp", [128, 8, 512], BF16, at=goff + 32768)
    m.add("th", [128, 4, 1024], F32, at=goff + 40960)
    m.add("x1ts", [128, 2, 1024], F32, at=goff + 40960)
    m.add("ebuf", [128, 1024], F32, at=goff + 49152)
    m.add("u_s", [64, 512], F32, at=goff + 57344)
    m.add("H", [64, 512], F32, at=goff + 59392)
    m.add("Hb", [64, 512], BF16, at=goff + 61440)
    m.add("yztm", [128, 4, 1024], BF16, at=goff + 62464)
    assert goff + 62464 + 8192 <= aend
    xoff = m.items["x"][2]
    goff = m.items["g"][2]
    m.add("wstage", [128, 32, 1024], F32, at=xoff)
    po = xoff + 131072
    assert po <= goff + 45056 + 1
    m.add("pp_t1", [128, 1024], F32, at=po)
    m.add("pp_t2", [128, 1024], F32, at=po + 4096)
    m.add("rowst", [1, 2048], F32, at=po)
    m.add("pp_q4", [128, 4096], BF16, at=po + 8192)
    m.add("pp_abs", [128, 1024], BF16, at=po + 16384)
    m.add("pp_img", [128, 8, 512], BF16, at=po + 18432)
    assert po + 18432 + 8192 <= goff + 73728
    assert m.budget <= SBUF_BASE + 212800, m.budget
    return m

ENGS = ("sync", "tensor", "vector", "scalar", "gpsimd")

class B:
    def __init__(self):
        self.ops = {e: [] for e in ENGS}
        self.ctr = {}
        self.t = {}
        self.d = {}
        self.ps = None
        self.sems = {}

    def inc(self, name, amt=1):
        self.ctr[name] = self.ctr.get(name, 0) + amt
        return self.ctr[name]

    def val(self, name):
        return self.ctr.get(name, 0)

    def ejoin(self):
        return [(s, self.val(s)) for s in EV_SEMS if self.val(s) > 0]

    def emit(self, eng, fn, waits=None, inc=None, drain=True):
        post = []
        if inc is not None:
            incs = inc if isinstance(inc, list) else [inc]
            for (s, amt) in incs:
                post.append((s, amt, self.inc(s, amt)))
        self.ops[eng].append((fn, [(s, v) for (s, v) in (waits or []) if v > 0], post,
                              drain))

    PLAIN_DMA_SEMS = ("ld", "wd", "st", "out")

    def dma(self, out, in_, waits=None, sem="ld", transpose=False, eng="sync", slow=False):
        waits = list(waits or [])

        def fn(e, out=out, in_=in_, transpose=transpose, slow=slow):
            if transpose:
                return e.dma_start_transpose(out=out, in_=in_)
            if slow:
                return e.dma_start(out=out, in_=in_, allow_slow_non_contiguous=True)
            return e.dma_start(out=out, in_=in_)
        fn._is_dma = True
        self.emit(eng, fn, waits, (sem, 16))
        return self.ctr[sem]

    def v_op(self, fn, waits=None, sem="ev", drain=True):
        self.emit("vector", fn, waits, (sem, 1), drain=drain)
        return self.ctr[sem]

    def v_recip(self, out, in_, waits=None, sem="ev"):
        return self.v_op(lambda e, out=out, in_=in_: e.reciprocal(out, in_), waits, sem)

    def s_op(self, fn, waits=None, sem="sc", drain=True):
        self.emit("scalar", fn, waits, (sem, 1), drain=drain)
        return self.ctr[sem]

    def mm(self, out, lhsT, rhs, start, stop, waits=None, sem="mm", do_inc=None,
           extra_inc=None):
        def fn(e, out=out, lhsT=lhsT, rhs=rhs, start=start, stop=stop):
            return e.matmul(out, lhsT, rhs, start=start, stop=stop)
        if do_inc is None:
            do_inc = stop
        incs = [(sem, 1)] if do_inc else []
        if extra_inc is not None:
            incs.append(extra_inc)
        self.emit("tensor", fn, waits, incs if incs else None)
        return self.ctr.get(sem, 0)

def finalize(b, nc, sem_names):
    import contextlib
    stack = contextlib.ExitStack()
    for s in sem_names:
        b.sems[s] = stack.enter_context(nc.semaphore(s))
    block = stack.enter_context(nc.Block())

    drain_engs = ("vector", "scalar")

    def mk(eng_name):
        def body(eng):
            for fn, waits, post, drain_f in b.ops[eng_name]:
                for (s, v) in waits:
                    eng.wait_ge(b.sems[s], v)
                inst = fn(eng)
                if (eng_name in drain_engs and drain_f
                        and not getattr(fn, "_is_dma", False)):
                    tgt = eng.drain() if post else None
                    if tgt is None:
                        eng.drain()
                    for (s, amt, _) in post:
                        tgt.then_inc(b.sems[s], amt)
                else:
                    for (s, amt, _) in post:
                        inst.then_inc(b.sems[s], amt)
        return body
    block.sync(mk("sync"))
    block.tensor(mk("tensor"))
    block.vector(mk("vector"))
    block.scalar(mk("scalar"))
    block.gpsimd(mk("gpsimd"))
    stack.close()

SEMS = ["ld", "ld2", "tp", "st", "st2", "wd", "wdk", "mm", "mma1", "mma2", "mmb", "mmw",
        "ev", "eva1", "eva2", "evb", "evb2", "evb3", "evw", "sc", "cc", "out"]

def prepass_consts(b):
    t = b.t
    b.dma(t["ident128"][:], b.d["ident"])
    b.v_op(lambda e: e.memset(t["onesr"][:], 1.0))
    b.v_op(lambda e: e.memset(t["onec"][:], 1.0))
    b.v_op(lambda e: e.memset(t["conek"][:], 1.0))
    b.v_op(lambda e: e.memset(t["ceps"][:], EPS))
    b.v_op(lambda e: e.memset(t["hcar"][:], 0.0))

def prepass_ffn(b, w_ap, O, I, wt_s, mcol, gate=None):
    t = b.t
    is_gu = (I == 1024)
    if is_gu:
        src = w_ap.rearrange("(m p) i -> p m i", p=128)
        dst = t["wstage"][:]
    else:
        src = w_ap.rearrange("(m p) (c i) -> p m c i", p=128, i=1024)
        dst = t["wstage"][:].rearrange("p (m c) i -> p m c i", c=4)
    ld_sub = []
    for s in range(8):
        w = b.ejoin() if (gate is None and s == 0) else \
            ([("ev", gate[0][4 * s + 3])] if gate is not None else [])
        if is_gu:
            o, i_ = dst[:, 4 * s:4 * s + 4, :], src[:, 4 * s:4 * s + 4, :]
        else:
            o, i_ = dst[:, s, :, :], src[:, s, :, :]
        eng, sem = ("sync", "ld") if s % 2 == 0 else ("gpsimd", "ld2")
        v = b.dma(o, i_, waits=w, sem=sem, eng=eng)
        ld_sub.append((sem, v))
    for c in range(32):
        w = [ld_sub[c // 4]] if c % 4 == 0 else None
        if c == 0 and gate is not None:
            w = w + [("ev", gate[1])]
        b.s_op(lambda e, c=c: e.activation(t["pp_abs"][:], t["wstage"][:, c, :], AF.Abs,
                                           accum_out=t["stats"][:, 64 + c:65 + c]),
               waits=w, drain=(c == 31))
    sc_abs = b.val("sc")
    ev_statsfree = b.v_op(lambda e: e.reduce_sum(t["stats"][:, 60:61], t["stats"][:, 64:96], AX.X),
                          waits=[("sc", sc_abs)])
    ev = ev_statsfree
    mm0 = b.mm(b.ps[0:1, 0, 0:1], t["stats"][:, 60:61], t["onec"][:], True, True,
               waits=[("ev", ev)] + b.ejoin())
    ev = b.v_op(lambda e: e.tensor_scalar(t["rowst"][0:1, 0:1], b.ps[0:1, 0, 0:1],
                                          1.0 / (O * I), 1e-5, ALU.mult, ALU.max),
                waits=[("mm", mm0)])
    ev = b.v_recip(t["rowst"][0:1, 1:2], t["rowst"][0:1, 0:1])
    mm1 = b.mm(b.ps[:, 1, 0:2], t["onesr"][:], t["rowst"][0:1, 0:2], True, True,
               waits=[("ev", ev)])
    evm = b.v_op(lambda e: e.tensor_copy(t["mws"][:, mcol:mcol + 2], b.ps[:, 1, 0:2]),
                 waits=[("mm", mm1)])
    scol = t["mws"][:, mcol + 1:mcol + 2]
    ev = evm
    chunkfree = [None] * 32
    for m in range(O // 128):
        npc = I // 1024
        for pc in range(npc):
            c = m * npc + pc
            w1 = [("ev", evm), ("tp", b.val("tp"))] if (m == 0 and pc == 0) else None
            chunkfree[c] = b.v_op(
                lambda e, c=c: e.tensor_scalar(t["pp_t1"][:], t["wstage"][:, c, :],
                                               scol, CLIP_W, ALU.mult, ALU.min), waits=w1,
                drain=False)
            b.v_op(lambda e: e.tensor_scalar(t["pp_t2"][:], t["pp_t1"][:],
                                             -CLIP_W, C_RND, ALU.max, ALU.add), drain=False)
            ev = b.v_op(lambda e, pc=pc: e.tensor_single_scalar(
                t["pp_q4"][:, pc * 1024:(pc + 1) * 1024], t["pp_t2"][:], C_RND, ALU.subtract),
                waits=[("ev", b.val("ev")), ("st2", b.val("st2"))] if pc == 0 and m > 0 else None)
        if is_gu:
            toff = (m % 4) * 128
            evt = pe_transpose(b, lambda blk, toff=toff: t["pp_img"][:, blk, toff:toff + 128],
                               t["pp_q4"][:, 0:1024], 8, waits=[("ev", ev)])
            if m % 4 == 3:
                b.dma(wt_s[m // 4], t["pp_img"][:], waits=[("ev", evt)], sem="st2",
                      eng="scalar")
        else:
            img = bass.AP(t["pp_img"], 0, [[4096, 128], [128, 32], [1, 128]])
            evt = pe_transpose(b, lambda blk: bass.AP(t["pp_img"], blk * 128,
                                                      [[4096, 128], [1, 128]]),
                               t["pp_q4"][:], 32, waits=[("ev", ev)])
            b.dma(wt_s[:, :, m * 128:(m + 1) * 128].transpose([1, 0, 2]), img,
                  waits=[("ev", evt)], sem="st2", eng="scalar")
    return chunkfree, ev_statsfree

def prepass_attn(b, aps, need_producer, need_consumer):
    t = b.t
    if need_producer:
        for grp in range(2):
            for m0 in range(grp * 8, grp * 8 + 8, 2):
                ld = b.dma(t["wstage"][:, 0:2, :],
                           aps["W_in"][m0 * 128:(m0 + 2) * 128, :].rearrange("(m p) i -> p m i", p=128),
                           waits=[("ev", b.val("ev"))])
                evc = b.v_op(lambda e: e.tensor_copy(
                    t["pp_q4"][:].rearrange("p (m f) -> p m f", m=4)[:, 0:2, :],
                    t["wstage"][:, 0:2, :]), waits=[("ld", ld), ("tp", b.val("tp"))])
                for mm_ in range(2):
                    m = m0 + mm_
                    if grp == 0:
                        outap = b.d["winP_s"][:, m, :, :]
                    else:
                        outap = b.d["winZ_s"][:, :, (m - 8) * 128:(m - 7) * 128]
                    evt = pe_transpose(b, lambda blk: t["pp_img"][:, blk, 0:128],
                                       t["pp_q4"][:, mm_ * 1024:(mm_ + 1) * 1024], 8,
                                       waits=[("ev", evc), ("st", b.val("st"))])
                    b.dma(outap, t["pp_img"][:, :, 0:128], waits=[("ev", evt)], sem="st")
        ld = b.dma(t["wstage"][0:64, 0, :], aps["W_xp"], waits=[("ev", b.val("ev"))])
        evc = b.v_op(lambda e: e.tensor_copy(t["pp_q4"][0:64, 0:1024], t["wstage"][0:64, 0, :]),
                     waits=[("ld", ld), ("tp", b.val("tp"))])
        pe_transpose(b, lambda blk: t["wxpT"][:, blk, :],
                     t["pp_q4"][0:64, 0:1024], 8, waits=[("ev", evc)])
        ld = b.dma(t["wstage"][0:64, 1, :], aps["Dmat"])
        b.v_op(lambda e: e.tensor_single_scalar(t["dmat"][:], t["wstage"][0:64, 1, :],
                                                0.5, ALU.mult), waits=[("ld", ld)])
        b.dma(t["binp"][:], aps["b_in"][0:1024].rearrange("(m p) -> m p", p=128).rearrange("m p -> p m"), slow=True)
        b.dma(t["bxp"][:], aps["b_xp"][:, None], slow=True)
        ld = b.dma(t["stats"][0:64, 32:33], aps["A_log"].rearrange("a b p -> (a b) p").rearrange("a p -> p a"), slow=True)
        sc = b.s_op(lambda e: e.activation(t["stats"][0:64, 33:34], t["stats"][0:64, 32:33], AF.Exp),
                    waits=[("ld", ld)])
        ev = b.v_op(lambda e: e.tensor_single_scalar(t["stats"][0:64, 34:35],
                                                     t["stats"][0:64, 33:34], -1.0, ALU.mult),
                    waits=[("sc", sc)])
        sc = b.s_op(lambda e: e.activation(t["stats"][0:64, 35:36], t["stats"][0:64, 34:35], AF.Exp),
                    waits=[("ev", ev)])
        b.v_op(lambda e: e.memset(t["xs"][0:64, 0:512], 1.0))
        b.v_op(lambda e: e.tensor_scalar(t["abc"][:], t["xs"][0:64, 0:512],
                                         t["stats"][0:64, 35:36], None, ALU.mult),
               waits=[("sc", sc)])
    bcasts = []
    if need_producer:
        bcasts += [("gob", aps["gamma"], None), ("bob", aps["beta"], None),
                   ("bzh", aps["b_in"][1024:2048], 0.5)]
    if need_consumer:
        bcasts += [("outb", aps["b_out"], None)]
    for dstt, srcap, scale in bcasts:
        ld = b.dma(t["rowst"][0:1, 0:1024], srcap[None, :],
                   waits=[("mm", b.val("mm")), ("ev", b.val("ev"))])
        if scale is not None:
            ev = b.v_op(lambda e, scale=scale: e.tensor_single_scalar(
                t["rowst"][0:1, 1024:2048], t["rowst"][0:1, 0:1024], scale, ALU.mult),
                waits=[("ld", ld)])
            row = t["rowst"][0:1, 1024:2048]
        else:
            ev = b.val("ev")
            row = t["rowst"][0:1, 0:1024]
        for h in range(2):
            mmv = b.mm(b.ps[:, 2 + h, :], t["onesr"][:], row[:, h * 512:(h + 1) * 512], True, True,
                       waits=[("ld", ld), ("ev", ev), ("ev", b.val("ev"))])
            b.v_op(lambda e, dstt=dstt, h=h: e.tensor_copy(t[dstt][:, h * 512:(h + 1) * 512],
                                                           b.ps[:, 2 + h, :]),
                   waits=[("mm", mmv)])

def ln_stats_tc(b, S, tc, sq_scs, sc0, waits=None):
    t = b.t
    src = t["x"][:, tc, :]
    xst = t["xs2"] if tc % 2 else t["xs"]
    b.v_op(lambda e, src=src, tc=tc: e.reduce_sum(t["stats"][:, S + tc:S + tc + 1], src, AX.X),
           waits=list(waits or []))
    b.v_op(lambda e, tc=tc: e.tensor_single_scalar(
        t["stats"][:, S + 4 + tc:S + 5 + tc], t["stats"][:, S + tc:S + tc + 1],
        1.0 / 1024.0, ALU.mult))
    wxs = [("sc", sq_scs[-2])] if len(sq_scs) >= 2 else [("sc", sc0)]
    evx = b.v_op(lambda e, src=src, tc=tc, xst=xst: e.tensor_scalar(
        xst[:], src, t["stats"][:, S + 4 + tc:S + 5 + tc], None, ALU.subtract),
        waits=wxs)
    sq = b.s_op(lambda e, tc=tc, xst=xst: e.activation(t["q2d"][:], xst[:], AF.Square,
                                                       accum_out=t["stats"][:, S + 8 + tc:S + 9 + tc]),
                waits=[("ev", evx)])
    sq_scs.append(sq)
    b.v_op(lambda e, tc=tc, xst=xst: e.reduce_max(t["stats"][:, S + 12 + tc:S + 13 + tc], xst[:],
                                                  AX.X, apply_absolute_value=True))

def ln_quant(b, S, extra_w=None, out_guard=None):
    sq_scs = []
    sc0 = b.val("sc")
    for tc in range(4):
        ln_stats_tc(b, S, tc, sq_scs, sc0, waits=(extra_w if tc == 0 else None))
    return ln_quant_rest(b, S, out_guard=out_guard)

def ln_quant_rest(b, S, out_guard=None):
    t = b.t
    tps = []
    sc = b.s_op(lambda e: e.activation(t["stats"][:, S + 44:S + 48], t["stats"][:, S + 8:S + 12],
                                       AF.Ln, bias=t["ceps"][:, 0:1], scale=1.0 / 1024.0),
                waits=[("ev", b.val("ev"))])
    sc = b.s_op(lambda e: e.activation(t["stats"][:, S + 16:S + 20], t["stats"][:, S + 44:S + 48],
                                       AF.Exp, scale=-0.5))
    b.v_op(lambda e: e.tensor_tensor(t["stats"][:, S + 44:S + 48], t["stats"][:, S + 16:S + 20],
                                     t["stats"][:, S + 12:S + 16], ALU.mult), waits=[("sc", sc)])
    b.v_op(lambda e: e.tensor_single_scalar(t["stats"][:, S + 20:S + 24],
                                            t["stats"][:, S + 44:S + 48], 1e-5, ALU.max))
    b.v_recip(t["stats"][:, S + 24:S + 28], t["stats"][:, S + 20:S + 24])
    b.v_op(lambda e: e.tensor_tensor(t["stats"][:, S + 44:S + 48], t["stats"][:, S + 16:S + 20],
                                     t["stats"][:, S + 24:S + 28], ALU.mult))
    b.v_op(lambda e: e.tensor_single_scalar(t["stats"][:, S + 28:S + 32],
                                            t["stats"][:, S + 44:S + 48], 127.0, ALU.mult))
    b.v_op(lambda e: e.tensor_scalar(t["stats"][:, S + 32:S + 36], t["stats"][:, S + 20:S + 24],
                                     1.0 / 127.0, None, ALU.mult))
    b.v_op(lambda e: e.tensor_scalar(t["stats"][:, S + 36:S + 40], t["stats"][:, S + 32:S + 36],
                                     t["mws"][:, 0:1], None, ALU.mult))
    b.v_op(lambda e: e.tensor_scalar(t["stats"][:, S + 40:S + 44], t["stats"][:, S + 32:S + 36],
                                     t["mws"][:, 2:3], None, ALU.mult))
    evs = b.val("ev")
    for tc in range(4):
        src = t["x"][:, tc, :]
        w = [("ev", evs), ("tp", b.val("tp"))] if tc == 0 else []
        if out_guard is not None and tc < 2:
            w = w + [out_guard]
        if len(tps) >= 2:
            w.append(("tp", tps[-2]))
        b.v_op(lambda e, src=src, tc=tc: e.tensor_scalar(
            t["xs"][:], src, t["stats"][:, S + 4 + tc:S + 5 + tc],
            t["stats"][:, S + 28 + tc:S + 29 + tc], ALU.subtract, ALU.mult), waits=w,
            drain=False)
        b.v_op(lambda e, tc=tc: e.tensor_single_scalar(
            t["scrx"][:, tc % 2, :], t["xs"][:], C_RND, ALU.add), drain=False)
        evq = b.v_op(lambda e, tc=tc: e.tensor_single_scalar(
            t["q1xn"][:, tc % 2, :], t["scrx"][:, tc % 2, :], C_RND, ALU.subtract))
        tp = b.dma(t["actfm"][:, :, tc * 128:(tc + 1) * 128], t["q1xn"][:, tc % 2, :],
                   waits=[("ev", evq)], sem="tp", transpose=True)
        tps.append(tp)
    return tps

def ffn_preload(b):
    t = b.t
    pre = {}
    pg = getattr(b, "prev_a1g", None)
    for gi in range(2):
        w = [("mma1", pg[14 + gi])] if pg is not None else \
            [("st2", getattr(b, "st2_gate", 0))]
        b.dma(t["wgb"][:, gi, :, :].rearrange("p k t -> p (k t)"),
              b.d["wgT_s"][gi].rearrange("p k t -> p (k t)"), waits=w)
        pre[gi] = b.dma(t["wub"][:, gi, :, :].rearrange("p k t -> p (k t)"),
                        b.d["wuT_s"][gi].rearrange("p k t -> p (k t)"))
    return pre

def ffn_phase(b, S, q1_tps, pre=None, after_pr0=None):
    t = b.t
    wgT_s, wuT_s, wdT_s = b.d["wgT_s"], b.d["wuT_s"], b.d["wdT_s"]
    q2tp = {}
    evr_tc = {}
    a1_groups = {}
    for pair in range(2):
        for n in range(8):
            par = n % 2
            gi = pair * 8 + n
            if pre is not None and gi < 2:
                ldu = pre[gi]
            else:
                w_ld = [("mma1", a1_groups[gi - 2])] if gi >= 2 else \
                    [("st2", getattr(b, "st2_gate", 0))]
                b.dma(t["wgb"][:, par, :, :].rearrange("p k t -> p (k t)"),
                      wgT_s[n].rearrange("p k t -> p (k t)"), waits=w_ld)
                ldu = b.dma(t["wub"][:, par, :, :].rearrange("p k t -> p (k t)"),
                            wuT_s[n].rearrange("p k t -> p (k t)"))
            for tcl in range(2):
                tc = pair * 2 + tcl
                pb = 2 * (gi * 2 + tcl) % 6
                w = [("ld", ldu)]
                if gi == 0:
                    w += [("tp", q1_tps[3])] + b.ejoin()
                prev_ev = b.val("eva1") - 2
                if prev_ev > 0:
                    w.append(("eva1", prev_ev))
                for k in range(8):
                    b.mm(b.ps[:, pb, :], t["actfm"][:, k, tc * 128:(tc + 1) * 128],
                         t["wgb"][:, par, k, :], k == 0, k == 7,
                         waits=w if k == 0 else None, sem="mma1")
                for k in range(8):
                    b.mm(b.ps[:, pb + 1, :], t["actfm"][:, k, tc * 128:(tc + 1) * 128],
                         t["wub"][:, par, k, :], k == 0, k == 7, sem="mma1")
                mmv = b.val("mma1")
                scv = b.s_op(lambda e, tcl=tcl, tc=tc, pb=pb: e.activation(
                    t["silu"][:, tcl, :], b.ps[:, pb, :], AF.Silu,
                    scale=t["stats"][:, S + 36 + tc:S + 37 + tc]),
                    waits=[("mma1", mmv - 1)])
                b.v_op(lambda e, tcl=tcl, tc=tc, n=n, pb=pb: e.scalar_tensor_tensor(
                    t["g"][:, tcl, n * 512:(n + 1) * 512], b.ps[:, pb + 1, :],
                    t["stats"][:, S + 40 + tc:S + 41 + tc], t["silu"][:, tcl, :],
                    ALU.mult, ALU.mult),
                    waits=[("mma1", mmv), ("sc", scv)], sem="eva1", drain=False)
                a1_groups[gi] = mmv
        for tcl in range(2):
            tc = pair * 2 + tcl
            b.v_op(lambda e, tcl=tcl, tc=tc: e.reduce_max(
                t["stats"][:, S + 56 + tc:S + 57 + tc], t["g"][:, tcl, :], AX.X,
                apply_absolute_value=True))
            b.v_op(lambda e, tc=tc: e.tensor_single_scalar(
                t["stats"][:, S + 64 + tc:S + 65 + tc], t["stats"][:, S + 56 + tc:S + 57 + tc],
                1e-5, ALU.max))
            b.v_recip(t["stats"][:, S + 60 + tc:S + 61 + tc],
                      t["stats"][:, S + 64 + tc:S + 65 + tc])
            b.v_op(lambda e, tc=tc: e.tensor_single_scalar(
                t["stats"][:, S + 68 + tc:S + 69 + tc], t["stats"][:, S + 60 + tc:S + 61 + tc],
                127.0, ALU.mult))
            evsm = b.v_op(lambda e, tc=tc: e.tensor_scalar(
                t["stats"][:, S + 72 + tc:S + 73 + tc], t["stats"][:, S + 64 + tc:S + 65 + tc],
                t["mws"][:, 4:5], 0.5 / 127.0, ALU.mult, ALU.mult))
            evr = None
            src2 = t["q2tm"][:] if pair == 0 else t["q2t2"][:, tcl, :]
            for qq in range(4):
                sl = slice(qq * 1024, (qq + 1) * 1024)
                w = None
                if qq == 0:
                    w = [("ev", evsm)] + ([("tp", b.val("tp"))] if pair == 0 else [])
                b.v_op(lambda e, tcl=tcl, tc=tc, sl=sl: e.tensor_scalar(
                    t["xs"][:], t["g"][:, tcl, sl], t["stats"][:, S + 68 + tc:S + 69 + tc],
                    C_RND, ALU.mult, ALU.add), waits=w, drain=False)
                evr = b.v_op(lambda e, sl=sl, src2=src2: e.tensor_single_scalar(
                    src2[:, sl], t["xs"][:], C_RND, ALU.subtract))
            evr_tc[tc] = evr
            if pair == 0:
                q2tp[tc] = b.dma(t["q2fmA"][:, :, tc * 128:(tc + 1) * 128], src2,
                                 waits=[("ev", evr)], sem="tp", transpose=True)
    evA2 = {}
    wd_load0 = b.val("wdk")
    for pr in range(2):
        tc_e, tc_o = 2 * pr, 2 * pr + 1
        if pr == 1:
            for tc in (2, 3):
                q2tp[tc] = b.dma(t["q2fmB"][:, :, (tc - 2) * 128:(tc - 1) * 128],
                                 t["q2t2"][:, tc - 2, :],
                                 waits=[("ev", evr_tc[tc])], sem="tp", transpose=True)
        for ld_i in range(16):
            gl = b.val("wdk") - wd_load0
            w = []
            if pr == 0 and ld_i == 0:
                w = [("st2", getattr(b, "st2_gate", 0))]
            wdk_prev = b.val("wdk") - 2
            if wdk_prev > 0:
                w.append(("wdk", wdk_prev))
            buf = gl % 3
            b.dma(t["wdb"][:, buf, :, :], wdT_s[ld_i * 2:ld_i * 2 + 2].transpose([1, 0, 2]),
                  waits=w, sem="wd")
            if getattr(b, "wd_log", None) is not None:
                b.dma(b.wd_log[gl], t["wdb"][:, buf, :, :], sem="st")
            wdv = b.val("wd")
            for kk in range(2):
                k = ld_i * 2 + kk
                w2 = [("wd", wdv)] if kk == 0 else []
                if k == 0:
                    w2 += [("eva2", b.val("eva2")), ("tp", q2tp[2 * pr + 1])]
                    if pr == 0:
                        w2 += [("eva1", b.val("eva1") - 2), ("evb2", b.val("evb2"))]
                for ci, tc in enumerate((tc_e, tc_o)):
                    pb = 4 + 2 * ci
                    qfm = t["q2fmA"] if tc < 2 else t["q2fmB"]
                    for h in range(2):
                        b.mm(b.ps[:, pb + h, :], qfm[:, k, (tc % 2) * 128:(tc % 2 + 1) * 128],
                             t["wdb"][:, buf, kk, h * 512:(h + 1) * 512],
                             k == 0, k == 31,
                             waits=w2 if (ci == 0 and h == 0) else None, sem="mma2",
                             do_inc=(k == 31 and ci == 1 and h == 1))
            b.mm(b.ps[0:1, 0, 0:1], t["conek"][:], t["conek"][:], True, True,
                 sem="wdk", do_inc=True)
        mmv = b.val("mma2")
        for ci, tc in enumerate((tc_e, tc_o)):
            pb = 4 + 2 * ci
            for h in range(2):
                b.v_op(lambda e, tc=tc, h=h, pb=pb: e.scalar_tensor_tensor(
                    t["x"][:, tc, h * 512:(h + 1) * 512], b.ps[:, pb + h, :],
                    t["stats"][:, S + 72 + tc:S + 73 + tc],
                    t["x"][:, tc, h * 512:(h + 1) * 512], ALU.mult, ALU.add),
                    waits=[("mma2", mmv)], sem="eva2")
            evA2[tc] = b.val("eva2")
        if pr == 0 and after_pr0 is not None:
            after_pr0(evA2)
    b.prev_a1g = a1_groups
    b.last_evr3 = evr_tc[3]
    return evA2

def lnb_front(b, S, tc, evA2_tc, sq_scs, sc0):
    t = b.t
    src = t["x"][:, tc, :]
    xst = t["xs2"] if tc % 2 else t["xs"]
    b.v_op(lambda e, src=src, tc=tc: e.reduce_sum(
        t["stats"][:, S + tc:S + tc + 1], src, AX.X), waits=[("eva2", evA2_tc)])
    b.v_op(lambda e, tc=tc: e.tensor_single_scalar(
        t["stats"][:, S + 4 + tc:S + 5 + tc], t["stats"][:, S + tc:S + tc + 1],
        1.0 / 1024.0, ALU.mult))
    wxs = [("sc", sq_scs[-2])] if len(sq_scs) >= 2 else [("sc", sc0)]
    evx = b.v_op(lambda e, src=src, tc=tc, xst=xst: e.tensor_scalar(
        xst[:], src, t["stats"][:, S + 4 + tc:S + 5 + tc], None, ALU.subtract),
        waits=wxs)
    sq = b.s_op(lambda e, tc=tc, xst=xst: e.activation(
        t["q2d"][:], xst[:], AF.Square,
        accum_out=t["stats"][:, S + 8 + tc:S + 9 + tc]), waits=[("ev", evx)])
    sq_scs.append(sq)
    b.s_op(lambda e, tc=tc: e.activation(t["stats"][:, S + 12 + tc:S + 13 + tc],
                                         t["stats"][:, S + 8 + tc:S + 9 + tc],
                                         AF.Ln, bias=t["ceps"][:, 0:1], scale=1.0 / 1024.0))
    return b.s_op(lambda e, tc=tc: e.activation(t["stats"][:, S + 16 + tc:S + 17 + tc],
                                                t["stats"][:, S + 12 + tc:S + 13 + tc],
                                                AF.Exp, scale=-0.5))

def lnb_backc(b, S, tc, sc_exp, w_tp):
    t = b.t
    src = t["x"][:, tc, :]
    xst = t["xs2"] if tc % 2 else t["xs"]
    ev_st = b.v_op(lambda e, src=src, tc=tc, xst=xst: e.scalar_tensor_tensor(
        src, xst[:], t["stats"][:, S + 16 + tc:S + 17 + tc], src,
        ALU.mult, ALU.add), waits=[("sc", sc_exp)])
    xn1 = t["scrx"][:, tc % 2, :]
    b.v_op(lambda e, xn1=xn1, tc=tc, xst=xst: e.scalar_tensor_tensor(
        xn1, xst[:], t["stats"][:, S + 16 + tc:S + 17 + tc], t["gob"][:],
        ALU.mult, ALU.mult), drain=False)
    evq = b.v_op(lambda e, xn1=xn1, tc=tc: e.tensor_tensor(
        t["q1xn"][:, tc % 2, :], xn1, t["bob"][:], ALU.add), waits=[w_tp])
    return ev_st, evq

def phase_b(b, S, tile, evA2, x1t_dst, yz_dst, lnb=None, mid_hook=None):
    t = b.t
    j = tile % 2
    if not lnb:
        lnb = {"sq_scs": [], "sc0": b.val("sc"), "exps": {}, "backs": {}}
        for tc in (0, 1):
            lnb["exps"][tc] = lnb_front(b, S, tc, evA2[tc], lnb["sq_scs"], lnb["sc0"])
        for tc in (0, 1):
            lnb["backs"][tc] = lnb_backc(b, S, tc, lnb["exps"][tc], ("tp", b.val("tp")))
    sq_scs, exps, backs = lnb["sq_scs"], lnb["exps"], lnb["backs"]
    ldp = b.dma(t["winPb"][:].rearrange("p m k c -> p (m k c)"),
                b.d["winP_s"][:].rearrange("p m k c -> p (m k c)"),
                waits=[("ev", b.val("ev")), ("eva1", b.val("eva1")),
                       ("tp", b.val("tp"))])
    xn_tps = []
    for tc in (0, 1):
        ev_st, evq = backs[tc]
        b.dma(x1t_dst(tc), t["x"][:, tc, :], waits=[("ev", ev_st)], sem="st")
        xn_tps.append(b.dma(t["actfm"][:, :, tc * 128:(tc + 1) * 128],
                            t["q1xn"][:, tc % 2, :],
                            waits=[("ev", evq)], sem="tp", transpose=True))
    ldz = b.dma(t["winZb"][:].rearrange("p k z -> p (k z)"),
                b.d["winZ_s"][:].rearrange("p k z -> p (k z)"))
    for tc in (2, 3):
        exps[tc] = lnb_front(b, S, tc, evA2[tc], sq_scs, lnb["sc0"])
    for tc in (2, 3):
        ev_st, evq = lnb_backc(b, S, tc, exps[tc], ("tp", xn_tps[tc - 2]))
        b.dma(x1t_dst(tc), t["x"][:, tc, :], waits=[("ev", ev_st)], sem="st")
        xn_tps.append(b.dma(t["actfm"][:, :, tc * 128:(tc + 1) * 128],
                            t["q1xn"][:, tc % 2, :],
                            waits=[("ev", evq)], sem="tp", transpose=True))
    if mid_hook is not None:
        mid_hook()
    for m in range(8):
        pb = m % 2
        if m == 0:
            w = [("ld", ldp), ("tp", xn_tps[3])] + b.ejoin()
        else:
            w = [("evb", max(0, b.val("evb") - 1))]
        for k in range(8):
            b.mm(b.ps[:, pb, :], t["winPb"][:, m, k, :], t["actfm"][:, k, :],
                 k == 0, k == 7, waits=w if k == 0 else None, sem="mmb")
        mmv = b.val("mmb")
        b.s_op(lambda e, m=m, pb=pb: e.activation(
            t["xp"][:, m, :], b.ps[:, pb, :], AF.Silu, bias=t["binp"][:, m:m + 1]),
            waits=[("mmb", mmv)], sem="evb")
    evb_xp = b.val("evb")
    for k in range(8):
        b.mm(b.ps[0:64, 2, :], t["wxpT"][:, k, :], t["xp"][:, k, :], k == 0, k == 7,
             waits=[("evb", evb_xp)] if k == 0 else None, sem="mmb")
    mmv = b.val("mmb")
    b.v_op(lambda e: e.tensor_scalar(t["u_s"][:], b.ps[0:64, 2, :], t["bxp"][:, 0:1],
                                     None, ALU.add), waits=[("mmb", mmv)], sem="evb")
    init = 0.0 if tile < 2 else t["hcar"][:, j:j + 1]
    b.v_op(lambda e, init=init: e.tensor_tensor_scan(
        t["H"][:], t["abc"][:], t["u_s"][:], init, ALU.mult, ALU.add), sem="evb")
    b.v_op(lambda e: e.tensor_copy(t["hcar"][:, j:j + 1], t["H"][:, 511:512]), sem="evb")
    ev_hb = b.v_op(lambda e: e.tensor_copy(t["Hb"][:], t["H"][:]), sem="evb")
    th_done = {}
    for tc in range(4):
        pbz = 3 + (tc % 2) * 2
        for k in range(8):
            w = []
            if k == 0:
                w = [("evb2", max(0, b.val("evb2") - 4))]
                if tc == 0:
                    w += [("ld", ldz), ("eva2", evA2[3]), ("eva1", b.val("eva1"))]
            b.mm(b.ps[:, pbz, :], t["actfm"][:, k, tc * 128:(tc + 1) * 128],
                 t["winZb"][:, k, 0:512], k == 0, k == 7,
                 waits=w if k == 0 else None, sem="mmb")
            b.mm(b.ps[:, pbz + 1, :], t["actfm"][:, k, tc * 128:(tc + 1) * 128],
                 t["winZb"][:, k, 512:1024], k == 0, k == 7, sem="mmb")
        mmz = b.val("mmb")
        for h in range(2):
            ev = b.v_op(lambda e, h=h, pbz=pbz: e.scalar_tensor_tensor(
                t["silu"][:, h, :], b.ps[:, pbz + h, :], 0.5,
                t["bzh"][:, h * 512:(h + 1) * 512], ALU.mult, ALU.add),
                waits=[("mmb", mmz)], sem="evb2")
            b.s_op(lambda e, tc=tc, h=h: e.activation(
                t["th"][:, tc, h * 512:(h + 1) * 512], t["silu"][:, h, :], AF.Tanh),
                waits=[("evb2", ev)], sem="evb2")
        th_done[tc] = b.val("evb2")
    mm_h0 = {}
    for tc in range(4):
        w = [("evb", ev_hb)]
        if tc == 3:
            w.append(("evb2", th_done[2]))
        b.mm(b.ps[:, tc, :], t["Hb"][:, tc * 128:(tc + 1) * 128], t["dmat"][:, 0:512],
             True, True, waits=w, sem="mmb")
        mm_h0[tc] = b.val("mmb")
    for tc in range(4):
        b.v_op(lambda e, tc=tc: e.scalar_tensor_tensor(
            t["yztm"][:, tc, 0:512], t["th"][:, tc, 0:512], 1.0, b.ps[:, tc, :],
            ALU.add, ALU.mult), waits=[("mmb", mm_h0[tc]), ("evb2", th_done[tc])],
            sem="evb3", drain=False)
        b.mm(b.ps[:, tc, :], t["Hb"][:, tc * 128:(tc + 1) * 128], t["dmat"][:, 512:1024],
             True, True, waits=[("evb3", b.val("evb3"))], sem="mmb")
        mm_h0[tc] = b.val("mmb")
    for tc in range(4):
        b.v_op(lambda e, tc=tc: e.scalar_tensor_tensor(
            t["yztm"][:, tc, 512:1024], t["th"][:, tc, 512:1024], 1.0, b.ps[:, tc, :],
            ALU.add, ALU.mult), waits=[("mmb", mm_h0[tc])], sem="evb3",
            drain=(tc == 3))
    st = b.dma(yz_dst, t["yztm"][:], waits=[("evb3", b.val("evb3"))], sem="st")
    return st

def pass2_wout(b, S, u, yz_srcs, x1t_src, first, tc_hook=None):
    t = b.t
    out_guard = b.val("out")
    if first:
        b.dma(t["woutb"][:].rearrange("p k o -> p (k o)"),
              b.d["woutT_s"][:].rearrange("p k o -> p (k o)"), sem="ld",
              waits=[("tp", b.val("tp")), ("ev", b.val("ev"))])
    w_wout = b.val("ld")
    if isinstance(yz_srcs, int):
        tp_yz = yz_srcs
    elif callable(yz_srcs):
        tp_yz = yz_srcs()
    else:
        tp_yz = None
        for jj in range(2):
            w = [("mmw", b.val("mmw"))]
            tp_yz = b.dma(t["yzfm"][:, :, jj * 256:(jj + 1) * 256], yz_srcs[jj],
                          waits=w, sem="tp", transpose=True)
    evw = {}
    for tc in range(4):
        par = tc % 2
        ld_x1 = b.dma(t["x1ts"][:, par, :], x1t_src(tc),
                      waits=[("evw", max(0, b.val("evw") - 4)), ("st", b.val("st")),
                             ("evb3", b.val("evb3")), ("eva2", b.val("eva2"))], sem="ld")
        pb = par * 2
        for k in range(8):
            w = []
            if k == 0:
                w = [("tp", tp_yz), ("ld", w_wout)]
                if tc == 0:
                    w += b.ejoin()
                prev = b.val("evw") - 4
                if prev > 0:
                    w.append(("evw", prev))
            b.mm(b.ps[:, pb, :], t["yzfm"][:, k, tc * 128:(tc + 1) * 128],
                 t["woutb"][:, k, 0:512], k == 0, k == 7, waits=w if k == 0 else None, sem="mmw")
            b.mm(b.ps[:, pb + 1, :], t["yzfm"][:, k, tc * 128:(tc + 1) * 128],
                 t["woutb"][:, k, 512:1024], k == 0, k == 7, sem="mmw")
        mmv = b.val("mmw")
        for h in range(2):
            b.v_op(lambda e, h=h, pb=pb: e.tensor_tensor(
                t["ebuf"][:, h * 512:(h + 1) * 512], b.ps[:, pb + h, :],
                t["outb"][:, h * 512:(h + 1) * 512], ALU.add),
                waits=[("mmw", mmv), ("ld", ld_x1)], sem="evw", drain=False)
            b.v_op(lambda e, tc=tc, h=h, par=par: e.tensor_tensor(
                t["x"][:, tc, h * 512:(h + 1) * 512], t["x1ts"][:, par, h * 512:(h + 1) * 512],
                t["ebuf"][:, h * 512:(h + 1) * 512], ALU.add),
                waits=[("out", out_guard)], sem="evw")
        evw[tc] = b.val("evw")
        if tc_hook is not None:
            tc_hook(tc)
    return evw

def weight_aps(nc):
    aps = {}
    names = [("w_gate", [4096, 1024]), ("w_up", [4096, 1024]), ("w_down", [1024, 4096]),
             ("W_in", [2048, 1024]), ("W_xp", [64, 1024]), ("W_out", [1024, 1024]),
             ("Dmat", [64, 1024]), ("A_log", [1, 1, 64]), ("b_in", [2048]), ("b_xp", [64]),
             ("b_out", [1024]), ("gamma", [1024]), ("beta", [1024])]
    for nm, shp in names:
        aps[nm] = nc.dram_tensor(nm, shp, F32, kind="ExternalInput").ap()
    return aps

def dram_scratch(b, nc):
    b.d["wgT_s"] = nc.dram_tensor("wgT_s", [8, 128, 8, 512], BF16).ap()
    b.d["wuT_s"] = nc.dram_tensor("wuT_s", [8, 128, 8, 512], BF16).ap()
    b.d["wdT_s"] = nc.dram_tensor("wdT_s", [32, 128, 1024], BF16).ap()
    b.d["winP_s"] = nc.dram_tensor("winP_s", [128, 8, 8, 128], BF16).ap()
    b.d["winZ_s"] = nc.dram_tensor("winZ_s", [128, 8, 1024], BF16).ap()
    b.d["woutT_s"] = nc.dram_tensor("woutT_s", [128, 8, 1024], BF16).ap()

def setup_b(nc):
    smap = build_smap()
    b = B()
    b.t = {nm: nc.alloc_sbuf_tensor_at(nm, shp, dt, offset=off)
           for nm, (shp, dt, off) in smap.items.items()}
    b.ps = nc.alloc_psum_tensor("ps", [128, 8, 512], F32)
    ident = nc.inline_tensor(np.eye(128, dtype=NPBF), name="ident128")
    b.d["ident"] = ident.ap()
    dram_scratch(b, nc)
    return b

def pe_transpose(b, dst, src_ap, nblk, waits=None, sem="ev"):
    t = b.t
    ev = None
    for blk in range(nblk):
        pb = 2 + blk % 2
        w = list(waits or []) if blk == 0 else []
        prev = b.val("ev") - 1
        if blk >= 2:
            w.append(("ev", b.val("ev") - 1))
        b.mm(b.ps[:, pb, 0:128], src_ap[:, blk * 128:(blk + 1) * 128], t["ident128"][:],
             True, True, waits=w, sem="mm") if False else None
        P = src_ap.shape[0]
        def fn(e, pb=pb, blk=blk, P=P):
            return e.transpose(b.ps[:, pb, 0:P // 2].bitcast(BF16),
                               src_ap[:, blk * 128:(blk + 1) * 128],
                               t["ident128"][0:P, 0:P])
        b.emit("tensor", fn, w, ("mm", 1))
        mmv = b.val("mm")
        ev = b.v_op(lambda e, pb=pb, blk=blk, P=P: e.tensor_copy(
            dst(blk), b.ps[:, pb, 0:P // 2].bitcast(BF16)), waits=[("mm", mmv)],
            drain=(blk == nblk - 1))
    return ev

def emit_pass1(b, x_in, x1t_out, yz_out, quarter_hook=None):
    t = b.t

    def emit_ldx(tile, w):
        q, j = tile // 2, tile % 2
        return b.dma(t["x"][:], x_in[j, q * 512:q * 512 + 512, :]
                     .rearrange("(c p) f -> p c f", p=128), waits=w)

    nxt = {}
    for tile in range(8):
        q, j = tile // 2, tile % 2
        l0 = q * 512
        if tile == 0:
            ldx = emit_ldx(0, [("tp", b.val("tp")), ("ev", b.val("ev")), ("sc", b.val("sc"))])
            pre = ffn_preload(b)
            sq_scs, sc0 = [], b.val("sc")
            for tc in range(4):
                ln_stats_tc(b, 0, tc, sq_scs, sc0, waits=([("ld", ldx)] if tc == 0 else None))
        else:
            pre = nxt["pre"]
        x1v = x1t_out[j].rearrange("(c p) f -> p c f", p=128)
        q1_tps = ln_quant_rest(b, 0)
        lnb = {}

        def after_pr0(evA2, lnb=lnb):
            lnb.update(sq_scs=[], sc0=b.val("sc"), exps={}, backs={})
            for tc in (0, 1):
                lnb["exps"][tc] = lnb_front(b, 0, tc, evA2[tc], lnb["sq_scs"], lnb["sc0"])
            for tc in (0, 1):
                lnb["backs"][tc] = lnb_backc(b, 0, tc, lnb["exps"][tc],
                                             ("tp", b.val("tp")))
        evA2 = ffn_phase(b, 0, q1_tps, pre=pre, after_pr0=after_pr0)

        def mid_hook(tile=tile):
            if tile == 7:
                return
            ldx2 = emit_ldx(tile + 1, [("st", b.val("st"))])
            nxt["pre"] = ffn_preload(b)
            sq_scs, sc0 = [], b.val("sc")
            for tc in range(4):
                ln_stats_tc(b, 0, tc, sq_scs, sc0, waits=([("ld", ldx2)] if tc == 0 else None))
        phase_b(b, 0, tile, evA2,
                x1t_dst=lambda tc, x1v=x1v, l0=l0: x1v[:, (l0 // 128) + tc, :],
                yz_dst=yz_out[q, j].rearrange("(c p) f -> p c f", p=128), lnb=lnb,
                mid_hook=mid_hook)
        if quarter_hook is not None and tile % 2 == 1:
            quarter_hook(tile // 2)
    b.emit("sync", lambda e: e.nop(), [("st", b.val("st")), ("out", b.val("out"))], None)

def emit_pass2(b, yz_in_fn, x1t_src_fn, y_out_fn, prefetch=False):
    t = b.t
    nxt_tp = None
    for u in range(8):
        sq_scs, sc0 = [], b.val("sc")

        def tc_hook(tc, sq_scs=sq_scs, sc0=sc0):
            ln_stats_tc(b, 0, tc, sq_scs, sc0)
        yz = yz_in_fn(u) if nxt_tp is None else nxt_tp
        evw = pass2_wout(b, 0, u, yz, lambda tc, u=u: x1t_src_fn(u, tc),
                         first=(u == 0), tc_hook=tc_hook)
        pre = ffn_preload(b)
        q1_tps = ln_quant_rest(b, 0, out_guard=("out", b.val("out")))
        evA2 = ffn_phase(b, 0, q1_tps, pre=pre)
        if prefetch and u < 7:
            nxt_tp = yz_in_fn(u + 1)()
        for tc in range(4):
            b.dma(y_out_fn(u, tc), t["x"][:, tc, :], waits=[("eva2", evA2[tc])], sem="out")
    b.emit("sync", lambda e: e.nop(), [("out", b.val("out"))], None)

def _build_p1():
    nc = bass.Bass("TRN2", target_bir_lowering=False)
    b = setup_b(nc)
    aps = weight_aps(nc)
    x_in = nc.dram_tensor("x", [2, 2048, 1024], F32, kind="ExternalInput").ap()
    x1t_out = nc.dram_tensor("x1t", [2, 2048, 1024], F32, kind="ExternalOutput").ap()
    yz_out = nc.dram_tensor("yz", [4, 2, 512, 1024], BF16, kind="ExternalOutput").ap()
    prepass_consts(b)
    g1 = prepass_ffn(b, aps["w_gate"], 4096, 1024, b.d["wgT_s"], 0)
    g2 = prepass_ffn(b, aps["w_up"], 4096, 1024, b.d["wuT_s"], 2, gate=g1)
    prepass_ffn(b, aps["w_down"], 1024, 4096, b.d["wdT_s"], 4, gate=g2)
    b.st2_gate = b.val("st2")
    prepass_attn(b, aps, need_producer=True, need_consumer=False)
    emit_pass1(b, x_in, x1t_out, yz_out)
    finalize(b, nc, SEMS)
    return nc

def _prepass_wout(b, aps):
    t = b.t
    for mp in range(0, 8, 2):
        ld = b.dma(t["wstage"][:, 0:2, :],
                   aps["W_out"][mp * 128:(mp + 2) * 128, :].rearrange("(m p) i -> p m i", p=128),
                   waits=[("ev", b.val("ev"))])
        evc = b.v_op(lambda e: e.tensor_copy(
            t["pp_q4"][:].rearrange("p (m f) -> p m f", m=4)[:, 0:2, :],
            t["wstage"][:, 0:2, :]), waits=[("ld", ld), ("tp", b.val("tp"))])
        for mm_ in range(2):
            m = mp + mm_
            evt = pe_transpose(b, lambda blk: t["pp_img"][:, blk, 0:128],
                               t["pp_q4"][:, mm_ * 1024:(mm_ + 1) * 1024], 8,
                               waits=[("ev", evc), ("st", b.val("st"))])
            b.dma(b.d["woutT_s"][:, :, m * 128:(m + 1) * 128], t["pp_img"][:, :, 0:128],
                  waits=[("ev", evt)], sem="st")

def _build_p2():
    nc = bass.Bass("TRN2", target_bir_lowering=False)
    b = setup_b(nc)
    aps = weight_aps(nc)
    x1t_in = nc.dram_tensor("x1t", [2, 2048, 1024], F32, kind="ExternalInput").ap()
    yz_in = nc.dram_tensor("yzin", [8, 2, 256, 1024], BF16, kind="ExternalInput").ap()
    y_out = nc.dram_tensor("y", [2, 2048, 1024], F32, kind="ExternalOutput").ap()
    prepass_consts(b)
    g1 = prepass_ffn(b, aps["w_gate"], 4096, 1024, b.d["wgT_s"], 0)
    g2 = prepass_ffn(b, aps["w_up"], 4096, 1024, b.d["wuT_s"], 2, gate=g1)
    prepass_ffn(b, aps["w_down"], 1024, 4096, b.d["wdT_s"], 4, gate=g2)
    b.st2_gate = b.val("st2")
    prepass_attn(b, aps, need_producer=False, need_consumer=True)
    _prepass_wout(b, aps)

    def yz_src(u):
        return [yz_in[u, jj] for jj in range(2)]

    def x1t_src(u, tc):
        jj, ih = tc // 2, tc % 2
        b2 = 2 * u + jj
        return bass.AP(x1t_in.tensor, (ih * 2048 + b2) * 1024, [[16 * 1024, 128], [1, 1024]])

    def y_dst(u, tc):
        jj, ih = tc // 2, tc % 2
        b2 = 2 * u + jj
        return bass.AP(y_out.tensor, (ih * 2048 + b2) * 1024, [[16 * 1024, 128], [1, 1024]])

    emit_pass2(b, yz_src, x1t_src, y_dst)
    finalize(b, nc, SEMS)
    return nc

def _build_fused():
    nc = bass.Bass("TRN2", target_bir_lowering=False, num_devices=8)
    b = setup_b(nc)
    aps = weight_aps(nc)
    t = b.t
    x_in = nc.dram_tensor("x", [2, 2048, 1024], F32, kind="ExternalInput").ap()
    y_out = nc.dram_tensor("y", [2, 2048, 1024], F32, kind="ExternalOutput").ap()
    x1t_s = nc.dram_tensor("x1t_s", [2, 2048, 1024], F32, kind="Internal").ap()
    yz_send = nc.dram_tensor("yz_send", [4, 2, 512, 1024], BF16, kind="Internal").ap()
    gat = nc.dram_tensor("gat", [4, 8, 2, 512, 1024], BF16, kind="Internal").ap()
    prepass_consts(b)
    g1 = prepass_ffn(b, aps["w_gate"], 4096, 1024, b.d["wgT_s"], 0)
    g2 = prepass_ffn(b, aps["w_up"], 4096, 1024, b.d["wuT_s"], 2, gate=g1)
    prepass_ffn(b, aps["w_down"], 1024, 4096, b.d["wdT_s"], 4, gate=g2)
    b.st2_gate = b.val("st2")
    prepass_attn(b, aps, need_producer=True, need_consumer=True)
    _prepass_wout(b, aps)

    emit_pass1(b, x_in, x1t_s, yz_send, quarter_hook=None)
    for q in range(4):
        def ag_fn(g, q=q):
            return g.collective_compute(
                "AllGather", mybir.AluOpType.bypass,
                replica_groups=[list(range(NCORE))],
                ins=[yz_send[q]], outs=[gat[q]])
        b.emit("gpsimd", ag_fn,
               [("st", b.val("st")), ("tp", b.val("tp")), ("evb3", b.val("evb3"))],
               ("cc", 1))

    b.dyn = {}

    def rank_fn(g):
        rank = g.compute_val(g.partition_id())
        b.dyn["rq"] = nc.s_assert_within(rank // 2, 0, 3)
        b.dyn["rm256"] = nc.s_assert_within((rank % 2) * 256, 0, 256)
        return g.engine_nop()
    b.emit("gpsimd", rank_fn, [("cc", 4)], ("cc", 1))

    def yz_loader(u):
        def load():
            w = [("cc", 5), ("tp", b.val("tp")), ("ev", getattr(b, "last_evr3", 0)),
                 ("mmb", b.val("mmb")), ("st", b.val("st")), ("evb3", b.val("evb3"))]
            for jj in range(2):
                def fn(g, u=u, jj=jj):
                    in_ = gat[ds(b.dyn["rq"], 1), u, jj, ds(b.dyn["rm256"], 256), :]
                    in_ = in_.squeeze(0).rearrange("(c p) f -> p c f", p=128)
                    return g.dma_start(out=t["yztm2"][:, jj * 2:jj * 2 + 2, :], in_=in_)
                fn._is_dma = True
                b.emit("gpsimd", fn, w, ("ld", 16))
                w = []
            ldv = b.ctr["ld"]
            tp = None
            for tc in range(4):
                tp = b.dma(t["yzfm"][:, :, tc * 128:(tc + 1) * 128], t["yztm2"][:, tc, :],
                           waits=[("ld", ldv), ("mmw", b.val("mmw"))], sem="tp",
                           transpose=True)
            return tp
        return load

    def x1t_src(u, tc):
        jj, ih = tc // 2, tc % 2
        b2 = 2 * u + jj
        return bass.AP(x1t_s.tensor, (ih * 2048 + b2) * 1024, [[16 * 1024, 128], [1, 1024]])

    def y_dst(u, tc):
        jj, ih = tc // 2, tc % 2
        b2 = 2 * u + jj
        return bass.AP(y_out.tensor, (ih * 2048 + b2) * 1024, [[16 * 1024, 128], [1, 1024]])

    emit_pass2(b, yz_loader, x1t_src, y_dst, prefetch=PREFETCH)
    finalize(b, nc, SEMS)
    return nc

NCORE = 8

def kernel(L0_attn_x, L0_attn_A_log, L0_attn_Dmat, L0_attn_W_in, L0_attn_b_in,
           L0_attn_W_xp, L0_attn_b_xp, L0_attn_W_out, L0_attn_b_out, L0_attn_gamma,
           L0_attn_beta, L0_ffn_w_gate, L0_ffn_w_up, L0_ffn_w_down):
    wmap = dict(
        w_gate=np.ascontiguousarray(L0_ffn_w_gate, dtype=np.float32),
        w_up=np.ascontiguousarray(L0_ffn_w_up, dtype=np.float32),
        w_down=np.ascontiguousarray(L0_ffn_w_down, dtype=np.float32),
        W_in=np.ascontiguousarray(L0_attn_W_in, dtype=np.float32),
        W_xp=np.ascontiguousarray(L0_attn_W_xp, dtype=np.float32),
        W_out=np.ascontiguousarray(L0_attn_W_out, dtype=np.float32),
        Dmat=np.ascontiguousarray(L0_attn_Dmat, dtype=np.float32),
        A_log=np.ascontiguousarray(L0_attn_A_log, dtype=np.float32),
        b_in=np.ascontiguousarray(L0_attn_b_in, dtype=np.float32),
        b_xp=np.ascontiguousarray(L0_attn_b_xp, dtype=np.float32),
        b_out=np.ascontiguousarray(L0_attn_b_out, dtype=np.float32),
        gamma=np.ascontiguousarray(L0_attn_gamma, dtype=np.float32),
        beta=np.ascontiguousarray(L0_attn_beta, dtype=np.float32),
    )
    x = np.ascontiguousarray(L0_attn_x, dtype=np.float32)
    if FUSED:
        nc = _build_fused()
        in_maps = [dict(wmap, x=np.ascontiguousarray(x[2 * c:2 * c + 2])) for c in range(8)]
        res = run_bass_kernel_spmd(nc, in_maps, core_ids=list(range(8)))
        out = np.stack([np.asarray(res.results[c]["y"]) for c in range(8)])
        return out.reshape(16, 2048, 1024)
    nc1 = _build_p1()
    in_maps = [dict(wmap, x=np.ascontiguousarray(x[2 * c:2 * c + 2])) for c in range(8)]
    res1 = run_bass_kernel_spmd(nc1, in_maps, core_ids=list(range(8)))
    x1t = [res1.results[c]["x1t"] for c in range(8)]
    yz = np.stack([np.asarray(res1.results[c]["yz"]).view(NPBF) if res1.results[c]["yz"].dtype != NPBF
                   else res1.results[c]["yz"] for c in range(8)])
    nc2 = _build_p2()
    in_maps2 = []
    for c in range(8):
        yzin = np.ascontiguousarray(yz[:, c // 2, :, (c % 2) * 256:(c % 2) * 256 + 256, :])
        in_maps2.append(dict(wmap, x1t=np.ascontiguousarray(x1t[c]), yzin=yzin))
    res2 = run_bass_kernel_spmd(nc2, in_maps2, core_ids=list(range(8)))
    out = np.concatenate([np.asarray(res2.results[c]["y"])[None] for c in range(8)], axis=0)
    return out.reshape(16, 2048, 1024)


# version 18
# speedup vs baseline: 1.0798x; 1.0162x over previous
import numpy as np
import ml_dtypes
import concourse.bass as bass
import concourse.mybir as mybir
from concourse.bass_utils import run_bass_kernel_spmd

F32 = mybir.dt.float32
BF16 = mybir.dt.bfloat16
AF = mybir.ActivationFunctionType
ALU = mybir.AluOpType
AX = mybir.AxisListType
NPBF = ml_dtypes.bfloat16
ds = bass.ds

FUSED = True
PREFETCH = True
LAZY_TPS = True
C_RND = float(np.float32(1.5 * 2 ** 23))
EPS = 1e-5
CLIP_W = 1.4999999

EV_SEMS = ("ev", "eva1", "eva2", "evb", "evb2", "evb3", "evw", "sc")

def _ru(x, a=32):
    return (x + a - 1) // a * a

class SMap:
    def __init__(self):
        self.off = 0
        self.items = {}

    def add(self, name, shape, dt, at=None):
        nbytes = int(np.prod(shape[1:])) * mybir.dt.size(dt)
        o = self.off if at is None else at
        self.items[name] = (shape, dt, o)
        if at is None:
            self.off = _ru(o + nbytes)
        return o + nbytes

SBUF_BASE = 16512

def build_smap():
    m = SMap()
    m.off = SBUF_BASE
    m.add("dmat", [64, 1024], BF16)
    m.add("wxpT", [128, 8, 64], BF16)
    m.add("binp", [128, 8], F32)
    m.add("bxp", [64, 1], F32)
    m.add("gob", [128, 1024], F32)
    m.add("bob", [128, 1024], F32)
    m.add("outb", [128, 1024], F32)
    m.add("bzh", [128, 1024], F32)
    m.add("abc", [64, 512], F32)
    m.add("stats", [128, 96], F32)
    m.add("mws", [128, 8], F32)
    m.add("hcar", [64, 2], F32)
    m.add("onesr", [1, 128], F32)
    m.add("onec", [128, 1], F32)
    m.add("conek", [128, 1], BF16)
    m.add("ident128", [128, 128], BF16)
    m.add("ceps", [128, 1], F32)
    m.add("x", [128, 4, 1024], F32)
    m.add("scrx", [128, 2, 1024], F32)
    m.add("xs", [128, 1024], F32)
    m.add("q1xn", [128, 2, 1024], BF16)
    m.add("actfm", [128, 8, 512], BF16)
    m.add("silu", [128, 2, 512], F32)
    m.add("wgb", [128, 2, 8, 512], BF16)
    m.add("wub", [128, 2, 8, 512], BF16)
    m.add("wdb", [128, 3, 2, 1024], BF16)
    goff = m.off
    m.add("g", [128, 2, 4096], F32)
    q2off = m.add("q2tm", [128, 4096], BF16) - 8192
    fmoff = m.add("q2fmA", [128, 32, 256], BF16) - 16384
    m.add("q2fmB", [128, 32, 256], BF16)
    aend = m.off
    m.add("xs2", [128, 1024], F32, at=q2off)
    m.add("q2d", [128, 1024], BF16, at=q2off + 4096)
    m.add("q2t2", [128, 2, 4096], BF16, at=goff)
    m.add("yztm2", [128, 4, 1024], BF16, at=goff + 16384)
    m.add("woutb", [128, 8, 1024], BF16)
    m.add("yzfm", [128, 8, 512], BF16)
    m.budget = m.off
    m.add("winPb", [128, 8, 8, 128], BF16, at=goff)
    m.add("winZb", [128, 8, 1024], BF16, at=goff + 16384)
    m.add("xp", [128, 8, 512], BF16, at=goff + 32768)
    m.add("th", [128, 4, 1024], F32, at=goff + 40960)
    m.add("x1ts", [128, 2, 1024], F32, at=goff + 40960)
    m.add("ebuf", [128, 1024], F32, at=goff + 49152)
    m.add("u_s", [64, 512], F32, at=goff + 57344)
    m.add("H", [64, 512], F32, at=goff + 59392)
    m.add("Hb", [64, 512], BF16, at=goff + 61440)
    m.add("yztm", [128, 4, 1024], BF16, at=goff + 62464)
    assert goff + 62464 + 8192 <= aend
    xoff = m.items["x"][2]
    goff = m.items["g"][2]
    m.add("wstage", [128, 32, 1024], F32, at=xoff)
    po = xoff + 131072
    assert po <= goff + 45056 + 1
    m.add("pp_t1", [128, 1024], F32, at=po)
    m.add("pp_t2", [128, 1024], F32, at=po + 4096)
    m.add("rowst", [1, 2048], F32, at=po)
    m.add("pp_q4", [128, 4096], BF16, at=po + 8192)
    m.add("pp_abs", [128, 1024], BF16, at=po + 16384)
    m.add("pp_img", [128, 8, 512], BF16, at=po + 18432)
    assert po + 18432 + 8192 <= goff + 73728
    assert m.budget <= SBUF_BASE + 212800, m.budget
    return m

ENGS = ("sync", "tensor", "vector", "scalar", "gpsimd")

class B:
    def __init__(self):
        self.ops = {e: [] for e in ENGS}
        self.ctr = {}
        self.t = {}
        self.d = {}
        self.ps = None
        self.sems = {}

    def inc(self, name, amt=1):
        self.ctr[name] = self.ctr.get(name, 0) + amt
        return self.ctr[name]

    def val(self, name):
        return self.ctr.get(name, 0)

    def ejoin(self):
        return [(s, self.val(s)) for s in EV_SEMS if self.val(s) > 0]

    def emit(self, eng, fn, waits=None, inc=None, drain=True):
        post = []
        if inc is not None:
            incs = inc if isinstance(inc, list) else [inc]
            for (s, amt) in incs:
                post.append((s, amt, self.inc(s, amt)))
        self.ops[eng].append((fn, [(s, v) for (s, v) in (waits or []) if v > 0], post,
                              drain))

    PLAIN_DMA_SEMS = ("ld", "wd", "st", "out")

    def dma(self, out, in_, waits=None, sem="ld", transpose=False, eng="sync", slow=False):
        waits = list(waits or [])

        def fn(e, out=out, in_=in_, transpose=transpose, slow=slow):
            if transpose:
                return e.dma_start_transpose(out=out, in_=in_)
            if slow:
                return e.dma_start(out=out, in_=in_, allow_slow_non_contiguous=True)
            return e.dma_start(out=out, in_=in_)
        fn._is_dma = True
        self.emit(eng, fn, waits, (sem, 16))
        return self.ctr[sem]

    def v_op(self, fn, waits=None, sem="ev", drain=True):
        self.emit("vector", fn, waits, (sem, 1), drain=drain)
        return self.ctr[sem]

    def v_recip(self, out, in_, waits=None, sem="ev"):
        return self.v_op(lambda e, out=out, in_=in_: e.reciprocal(out, in_), waits, sem)

    def s_op(self, fn, waits=None, sem="sc", drain=True):
        self.emit("scalar", fn, waits, (sem, 1), drain=drain)
        return self.ctr[sem]

    def mm(self, out, lhsT, rhs, start, stop, waits=None, sem="mm", do_inc=None,
           extra_inc=None):
        def fn(e, out=out, lhsT=lhsT, rhs=rhs, start=start, stop=stop):
            return e.matmul(out, lhsT, rhs, start=start, stop=stop)
        if do_inc is None:
            do_inc = stop
        incs = [(sem, 1)] if do_inc else []
        if extra_inc is not None:
            incs.append(extra_inc)
        self.emit("tensor", fn, waits, incs if incs else None)
        return self.ctr.get(sem, 0)

def finalize(b, nc, sem_names):
    import contextlib
    stack = contextlib.ExitStack()
    for s in sem_names:
        b.sems[s] = stack.enter_context(nc.semaphore(s))
    block = stack.enter_context(nc.Block())

    drain_engs = ("vector", "scalar")

    def mk(eng_name):
        def body(eng):
            for fn, waits, post, drain_f in b.ops[eng_name]:
                for (s, v) in waits:
                    eng.wait_ge(b.sems[s], v)
                inst = fn(eng)
                if (eng_name in drain_engs and drain_f
                        and not getattr(fn, "_is_dma", False)):
                    tgt = eng.drain() if post else None
                    if tgt is None:
                        eng.drain()
                    for (s, amt, _) in post:
                        tgt.then_inc(b.sems[s], amt)
                else:
                    for (s, amt, _) in post:
                        inst.then_inc(b.sems[s], amt)
        return body
    block.sync(mk("sync"))
    block.tensor(mk("tensor"))
    block.vector(mk("vector"))
    block.scalar(mk("scalar"))
    block.gpsimd(mk("gpsimd"))
    stack.close()

SEMS = ["ld", "ld2", "tp", "st", "st2", "wd", "wdk", "mm", "mma1", "mma2", "mmb", "mmw",
        "ev", "eva1", "eva2", "evb", "evb2", "evb3", "evw", "sc", "cc", "out"]

def prepass_consts(b):
    t = b.t
    b.dma(t["ident128"][:], b.d["ident"])
    b.v_op(lambda e: e.memset(t["onesr"][:], 1.0))
    b.v_op(lambda e: e.memset(t["onec"][:], 1.0))
    b.v_op(lambda e: e.memset(t["conek"][:], 1.0))
    b.v_op(lambda e: e.memset(t["ceps"][:], EPS))
    b.v_op(lambda e: e.memset(t["hcar"][:], 0.0))

def prepass_ffn(b, w_ap, O, I, wt_s, mcol, gate=None):
    t = b.t
    is_gu = (I == 1024)
    if is_gu:
        src = w_ap.rearrange("(m p) i -> p m i", p=128)
        dst = t["wstage"][:]
    else:
        src = w_ap.rearrange("(m p) (c i) -> p m c i", p=128, i=1024)
        dst = t["wstage"][:].rearrange("p (m c) i -> p m c i", c=4)
    ld_sub = []
    for s in range(8):
        w = b.ejoin() if (gate is None and s == 0) else \
            ([("ev", gate[0][4 * s + 3])] if gate is not None else [])
        if is_gu:
            o, i_ = dst[:, 4 * s:4 * s + 4, :], src[:, 4 * s:4 * s + 4, :]
        else:
            o, i_ = dst[:, s, :, :], src[:, s, :, :]
        eng, sem = ("sync", "ld") if s % 2 == 0 else ("gpsimd", "ld2")
        v = b.dma(o, i_, waits=w, sem=sem, eng=eng)
        ld_sub.append((sem, v))
    for c in range(32):
        w = [ld_sub[c // 4]] if c % 4 == 0 else None
        if c == 0 and gate is not None:
            w = w + [("ev", gate[1])]
        b.s_op(lambda e, c=c: e.activation(t["pp_abs"][:], t["wstage"][:, c, :], AF.Abs,
                                           accum_out=t["stats"][:, 64 + c:65 + c]),
               waits=w, drain=(c == 31))
    sc_abs = b.val("sc")
    ev_statsfree = b.v_op(lambda e: e.reduce_sum(t["stats"][:, 60:61], t["stats"][:, 64:96], AX.X),
                          waits=[("sc", sc_abs)])
    ev = ev_statsfree
    mm0 = b.mm(b.ps[0:1, 0, 0:1], t["stats"][:, 60:61], t["onec"][:], True, True,
               waits=[("ev", ev)] + b.ejoin())
    ev = b.v_op(lambda e: e.tensor_scalar(t["rowst"][0:1, 0:1], b.ps[0:1, 0, 0:1],
                                          1.0 / (O * I), 1e-5, ALU.mult, ALU.max),
                waits=[("mm", mm0)])
    ev = b.v_recip(t["rowst"][0:1, 1:2], t["rowst"][0:1, 0:1])
    mm1 = b.mm(b.ps[:, 1, 0:2], t["onesr"][:], t["rowst"][0:1, 0:2], True, True,
               waits=[("ev", ev)])
    evm = b.v_op(lambda e: e.tensor_copy(t["mws"][:, mcol:mcol + 2], b.ps[:, 1, 0:2]),
                 waits=[("mm", mm1)])
    scol = t["mws"][:, mcol + 1:mcol + 2]
    ev = evm
    chunkfree = [None] * 32
    for m in range(O // 128):
        npc = I // 1024
        for pc in range(npc):
            c = m * npc + pc
            w1 = [("ev", evm), ("tp", b.val("tp"))] if (m == 0 and pc == 0) else None
            chunkfree[c] = b.v_op(
                lambda e, c=c: e.tensor_scalar(t["pp_t1"][:], t["wstage"][:, c, :],
                                               scol, CLIP_W, ALU.mult, ALU.min), waits=w1,
                drain=False)
            b.v_op(lambda e: e.tensor_scalar(t["pp_t2"][:], t["pp_t1"][:],
                                             -CLIP_W, C_RND, ALU.max, ALU.add), drain=False)
            ev = b.v_op(lambda e, pc=pc: e.tensor_single_scalar(
                t["pp_q4"][:, pc * 1024:(pc + 1) * 1024], t["pp_t2"][:], C_RND, ALU.subtract),
                waits=[("ev", b.val("ev")), ("st2", b.val("st2"))] if pc == 0 and m > 0 else None)
        if is_gu:
            toff = (m % 4) * 128
            evt = pe_transpose(b, lambda blk, toff=toff: t["pp_img"][:, blk, toff:toff + 128],
                               t["pp_q4"][:, 0:1024], 8, waits=[("ev", ev)])
            if m % 4 == 3:
                b.dma(wt_s[m // 4], t["pp_img"][:], waits=[("ev", evt)], sem="st2",
                      eng="scalar")
        else:
            img = bass.AP(t["pp_img"], 0, [[4096, 128], [128, 32], [1, 128]])
            evt = pe_transpose(b, lambda blk: bass.AP(t["pp_img"], blk * 128,
                                                      [[4096, 128], [1, 128]]),
                               t["pp_q4"][:], 32, waits=[("ev", ev)])
            b.dma(wt_s[:, :, m * 128:(m + 1) * 128].transpose([1, 0, 2]), img,
                  waits=[("ev", evt)], sem="st2", eng="scalar")
    return chunkfree, ev_statsfree

def prepass_attn(b, aps, need_producer, need_consumer):
    t = b.t
    if need_producer:
        for grp in range(2):
            for m0 in range(grp * 8, grp * 8 + 8, 2):
                ld = b.dma(t["wstage"][:, 0:2, :],
                           aps["W_in"][m0 * 128:(m0 + 2) * 128, :].rearrange("(m p) i -> p m i", p=128),
                           waits=[("ev", b.val("ev"))])
                evc = b.v_op(lambda e: e.tensor_copy(
                    t["pp_q4"][:].rearrange("p (m f) -> p m f", m=4)[:, 0:2, :],
                    t["wstage"][:, 0:2, :]), waits=[("ld", ld), ("tp", b.val("tp"))])
                for mm_ in range(2):
                    m = m0 + mm_
                    if grp == 0:
                        outap = b.d["winP_s"][:, m, :, :]
                    else:
                        outap = b.d["winZ_s"][:, :, (m - 8) * 128:(m - 7) * 128]
                    evt = pe_transpose(b, lambda blk: t["pp_img"][:, blk, 0:128],
                                       t["pp_q4"][:, mm_ * 1024:(mm_ + 1) * 1024], 8,
                                       waits=[("ev", evc), ("st", b.val("st"))])
                    b.dma(outap, t["pp_img"][:, :, 0:128], waits=[("ev", evt)], sem="st")
        ld = b.dma(t["wstage"][0:64, 0, :], aps["W_xp"], waits=[("ev", b.val("ev"))])
        evc = b.v_op(lambda e: e.tensor_copy(t["pp_q4"][0:64, 0:1024], t["wstage"][0:64, 0, :]),
                     waits=[("ld", ld), ("tp", b.val("tp"))])
        pe_transpose(b, lambda blk: t["wxpT"][:, blk, :],
                     t["pp_q4"][0:64, 0:1024], 8, waits=[("ev", evc)])
        ld = b.dma(t["wstage"][0:64, 1, :], aps["Dmat"])
        b.v_op(lambda e: e.tensor_single_scalar(t["dmat"][:], t["wstage"][0:64, 1, :],
                                                0.5, ALU.mult), waits=[("ld", ld)])
        b.dma(t["binp"][:], aps["b_in"][0:1024].rearrange("(m p) -> m p", p=128).rearrange("m p -> p m"), slow=True)
        b.dma(t["bxp"][:], aps["b_xp"][:, None], slow=True)
        ld = b.dma(t["stats"][0:64, 32:33], aps["A_log"].rearrange("a b p -> (a b) p").rearrange("a p -> p a"), slow=True)
        sc = b.s_op(lambda e: e.activation(t["stats"][0:64, 33:34], t["stats"][0:64, 32:33], AF.Exp),
                    waits=[("ld", ld)])
        ev = b.v_op(lambda e: e.tensor_single_scalar(t["stats"][0:64, 34:35],
                                                     t["stats"][0:64, 33:34], -1.0, ALU.mult),
                    waits=[("sc", sc)])
        sc = b.s_op(lambda e: e.activation(t["stats"][0:64, 35:36], t["stats"][0:64, 34:35], AF.Exp),
                    waits=[("ev", ev)])
        b.v_op(lambda e: e.memset(t["xs"][0:64, 0:512], 1.0))
        b.v_op(lambda e: e.tensor_scalar(t["abc"][:], t["xs"][0:64, 0:512],
                                         t["stats"][0:64, 35:36], None, ALU.mult),
               waits=[("sc", sc)])
    bcasts = []
    if need_producer:
        bcasts += [("gob", aps["gamma"], None), ("bob", aps["beta"], None),
                   ("bzh", aps["b_in"][1024:2048], 0.5)]
    if need_consumer:
        bcasts += [("outb", aps["b_out"], None)]
    for dstt, srcap, scale in bcasts:
        ld = b.dma(t["rowst"][0:1, 0:1024], srcap[None, :],
                   waits=[("mm", b.val("mm")), ("ev", b.val("ev"))])
        if scale is not None:
            ev = b.v_op(lambda e, scale=scale: e.tensor_single_scalar(
                t["rowst"][0:1, 1024:2048], t["rowst"][0:1, 0:1024], scale, ALU.mult),
                waits=[("ld", ld)])
            row = t["rowst"][0:1, 1024:2048]
        else:
            ev = b.val("ev")
            row = t["rowst"][0:1, 0:1024]
        for h in range(2):
            mmv = b.mm(b.ps[:, 2 + h, :], t["onesr"][:], row[:, h * 512:(h + 1) * 512], True, True,
                       waits=[("ld", ld), ("ev", ev), ("ev", b.val("ev"))])
            b.v_op(lambda e, dstt=dstt, h=h: e.tensor_copy(t[dstt][:, h * 512:(h + 1) * 512],
                                                           b.ps[:, 2 + h, :]),
                   waits=[("mm", mmv)])

def ln_stats_tc(b, S, tc, sq_scs, sc0, waits=None):
    t = b.t
    src = t["x"][:, tc, :]
    xst = t["xs2"] if tc % 2 else t["xs"]
    b.v_op(lambda e, src=src, tc=tc: e.reduce_sum(t["stats"][:, S + tc:S + tc + 1], src, AX.X),
           waits=list(waits or []))
    b.v_op(lambda e, tc=tc: e.tensor_single_scalar(
        t["stats"][:, S + 4 + tc:S + 5 + tc], t["stats"][:, S + tc:S + tc + 1],
        1.0 / 1024.0, ALU.mult))
    wxs = [("sc", sq_scs[-2])] if len(sq_scs) >= 2 else [("sc", sc0)]
    evx = b.v_op(lambda e, src=src, tc=tc, xst=xst: e.tensor_scalar(
        xst[:], src, t["stats"][:, S + 4 + tc:S + 5 + tc], None, ALU.subtract),
        waits=wxs)
    sq = b.s_op(lambda e, tc=tc, xst=xst: e.activation(t["q2d"][:], xst[:], AF.Square,
                                                       accum_out=t["stats"][:, S + 8 + tc:S + 9 + tc]),
                waits=[("ev", evx)])
    sq_scs.append(sq)
    b.v_op(lambda e, tc=tc, xst=xst: e.reduce_max(t["stats"][:, S + 12 + tc:S + 13 + tc], xst[:],
                                                  AX.X, apply_absolute_value=True))

def ln_quant(b, S, extra_w=None, out_guard=None):
    sq_scs = []
    sc0 = b.val("sc")
    for tc in range(4):
        ln_stats_tc(b, S, tc, sq_scs, sc0, waits=(extra_w if tc == 0 else None))
    return ln_quant_rest(b, S, out_guard=out_guard)

def ln_quant_rest(b, S, out_guard=None):
    t = b.t
    tps = []
    sc = b.s_op(lambda e: e.activation(t["stats"][:, S + 44:S + 48], t["stats"][:, S + 8:S + 12],
                                       AF.Ln, bias=t["ceps"][:, 0:1], scale=1.0 / 1024.0),
                waits=[("ev", b.val("ev"))])
    sc = b.s_op(lambda e: e.activation(t["stats"][:, S + 16:S + 20], t["stats"][:, S + 44:S + 48],
                                       AF.Exp, scale=-0.5))
    b.v_op(lambda e: e.tensor_tensor(t["stats"][:, S + 44:S + 48], t["stats"][:, S + 16:S + 20],
                                     t["stats"][:, S + 12:S + 16], ALU.mult), waits=[("sc", sc)])
    b.v_op(lambda e: e.tensor_single_scalar(t["stats"][:, S + 20:S + 24],
                                            t["stats"][:, S + 44:S + 48], 1e-5, ALU.max))
    b.v_recip(t["stats"][:, S + 24:S + 28], t["stats"][:, S + 20:S + 24])
    b.v_op(lambda e: e.tensor_tensor(t["stats"][:, S + 44:S + 48], t["stats"][:, S + 16:S + 20],
                                     t["stats"][:, S + 24:S + 28], ALU.mult))
    b.v_op(lambda e: e.tensor_single_scalar(t["stats"][:, S + 28:S + 32],
                                            t["stats"][:, S + 44:S + 48], 127.0, ALU.mult))
    b.v_op(lambda e: e.tensor_scalar(t["stats"][:, S + 32:S + 36], t["stats"][:, S + 20:S + 24],
                                     1.0 / 127.0, None, ALU.mult))
    b.v_op(lambda e: e.tensor_scalar(t["stats"][:, S + 36:S + 40], t["stats"][:, S + 32:S + 36],
                                     t["mws"][:, 0:1], None, ALU.mult))
    b.v_op(lambda e: e.tensor_scalar(t["stats"][:, S + 40:S + 44], t["stats"][:, S + 32:S + 36],
                                     t["mws"][:, 2:3], None, ALU.mult))
    evs = b.val("ev")
    for tc in range(4):
        src = t["x"][:, tc, :]
        w = [("ev", evs), ("tp", b.val("tp"))] if tc == 0 else []
        if out_guard is not None and tc < 2:
            w = w + [out_guard]
        if len(tps) >= 2:
            w.append(("tp", tps[-2]))
        b.v_op(lambda e, src=src, tc=tc: e.tensor_scalar(
            t["xs"][:], src, t["stats"][:, S + 4 + tc:S + 5 + tc],
            t["stats"][:, S + 28 + tc:S + 29 + tc], ALU.subtract, ALU.mult), waits=w,
            drain=False)
        b.v_op(lambda e, tc=tc: e.tensor_single_scalar(
            t["scrx"][:, tc % 2, :], t["xs"][:], C_RND, ALU.add), drain=False)
        evq = b.v_op(lambda e, tc=tc: e.tensor_single_scalar(
            t["q1xn"][:, tc % 2, :], t["scrx"][:, tc % 2, :], C_RND, ALU.subtract))
        tp = b.dma(t["actfm"][:, :, tc * 128:(tc + 1) * 128], t["q1xn"][:, tc % 2, :],
                   waits=[("ev", evq)], sem="tp", transpose=True)
        tps.append(tp)
    return tps

def ffn_preload(b):
    t = b.t
    pre = {}
    pg = getattr(b, "prev_a1g", None)
    for gi in range(2):
        w = [("mma1", pg[14 + gi])] if pg is not None else \
            [("st2", getattr(b, "st2_gate", 0))]
        b.dma(t["wgb"][:, gi, :, :].rearrange("p k t -> p (k t)"),
              b.d["wgT_s"][gi].rearrange("p k t -> p (k t)"), waits=w)
        pre[gi] = b.dma(t["wub"][:, gi, :, :].rearrange("p k t -> p (k t)"),
                        b.d["wuT_s"][gi].rearrange("p k t -> p (k t)"))
    return pre

def ffn_phase(b, S, q1_tps, pre=None, after_pr0=None):
    t = b.t
    wgT_s, wuT_s, wdT_s = b.d["wgT_s"], b.d["wuT_s"], b.d["wdT_s"]
    q2tp = {}
    evr_tc = {}
    a1_groups = {}
    for pair in range(2):
        for n in range(8):
            par = n % 2
            gi = pair * 8 + n
            if pre is not None and gi < 2:
                ldu = pre[gi]
            else:
                w_ld = [("mma1", a1_groups[gi - 2])] if gi >= 2 else \
                    [("st2", getattr(b, "st2_gate", 0))]
                b.dma(t["wgb"][:, par, :, :].rearrange("p k t -> p (k t)"),
                      wgT_s[n].rearrange("p k t -> p (k t)"), waits=w_ld)
                ldu = b.dma(t["wub"][:, par, :, :].rearrange("p k t -> p (k t)"),
                            wuT_s[n].rearrange("p k t -> p (k t)"))
            for tcl in range(2):
                tc = pair * 2 + tcl
                pb = 2 * (gi * 2 + tcl) % 6
                w = [("ld", ldu)]
                if gi == 0:
                    w += [("tp", q1_tps[3])] + b.ejoin()
                prev_ev = b.val("eva1") - 2
                if prev_ev > 0:
                    w.append(("eva1", prev_ev))
                for k in range(8):
                    b.mm(b.ps[:, pb, :], t["actfm"][:, k, tc * 128:(tc + 1) * 128],
                         t["wgb"][:, par, k, :], k == 0, k == 7,
                         waits=w if k == 0 else None, sem="mma1")
                for k in range(8):
                    b.mm(b.ps[:, pb + 1, :], t["actfm"][:, k, tc * 128:(tc + 1) * 128],
                         t["wub"][:, par, k, :], k == 0, k == 7, sem="mma1")
                mmv = b.val("mma1")
                scv = b.s_op(lambda e, tcl=tcl, tc=tc, pb=pb: e.activation(
                    t["silu"][:, tcl, :], b.ps[:, pb, :], AF.Silu,
                    scale=t["stats"][:, S + 36 + tc:S + 37 + tc]),
                    waits=[("mma1", mmv - 1)])
                b.v_op(lambda e, tcl=tcl, tc=tc, n=n, pb=pb: e.scalar_tensor_tensor(
                    t["g"][:, tcl, n * 512:(n + 1) * 512], b.ps[:, pb + 1, :],
                    t["stats"][:, S + 40 + tc:S + 41 + tc], t["silu"][:, tcl, :],
                    ALU.mult, ALU.mult),
                    waits=[("mma1", mmv), ("sc", scv)], sem="eva1", drain=False)
                a1_groups[gi] = mmv
        for tcl in range(2):
            tc = pair * 2 + tcl
            b.v_op(lambda e, tcl=tcl, tc=tc: e.reduce_max(
                t["stats"][:, S + 56 + tc:S + 57 + tc], t["g"][:, tcl, :], AX.X,
                apply_absolute_value=True))
            b.v_op(lambda e, tc=tc: e.tensor_single_scalar(
                t["stats"][:, S + 64 + tc:S + 65 + tc], t["stats"][:, S + 56 + tc:S + 57 + tc],
                1e-5, ALU.max))
            b.v_recip(t["stats"][:, S + 60 + tc:S + 61 + tc],
                      t["stats"][:, S + 64 + tc:S + 65 + tc])
            b.v_op(lambda e, tc=tc: e.tensor_single_scalar(
                t["stats"][:, S + 68 + tc:S + 69 + tc], t["stats"][:, S + 60 + tc:S + 61 + tc],
                127.0, ALU.mult))
            evsm = b.v_op(lambda e, tc=tc: e.tensor_scalar(
                t["stats"][:, S + 72 + tc:S + 73 + tc], t["stats"][:, S + 64 + tc:S + 65 + tc],
                t["mws"][:, 4:5], 0.5 / 127.0, ALU.mult, ALU.mult))
            evr = None
            src2 = t["q2tm"][:] if pair == 0 else t["q2t2"][:, tcl, :]
            for qq in range(4):
                sl = slice(qq * 1024, (qq + 1) * 1024)
                w = None
                if qq == 0:
                    w = [("ev", evsm)] + ([("tp", b.val("tp"))] if pair == 0 else [])
                b.v_op(lambda e, tcl=tcl, tc=tc, sl=sl: e.tensor_scalar(
                    t["xs"][:], t["g"][:, tcl, sl], t["stats"][:, S + 68 + tc:S + 69 + tc],
                    C_RND, ALU.mult, ALU.add), waits=w, drain=False)
                evr = b.v_op(lambda e, sl=sl, src2=src2: e.tensor_single_scalar(
                    src2[:, sl], t["xs"][:], C_RND, ALU.subtract))
            evr_tc[tc] = evr
            if pair == 0:
                q2tp[tc] = b.dma(t["q2fmA"][:, :, tc * 128:(tc + 1) * 128], src2,
                                 waits=[("ev", evr)], sem="tp", transpose=True)
    evA2 = {}
    wd_load0 = b.val("wdk")
    for pr in range(2):
        tc_e, tc_o = 2 * pr, 2 * pr + 1
        if pr == 1:
            for tc in (2, 3):
                q2tp[tc] = b.dma(t["q2fmB"][:, :, (tc - 2) * 128:(tc - 1) * 128],
                                 t["q2t2"][:, tc - 2, :],
                                 waits=[("ev", evr_tc[tc])], sem="tp", transpose=True)
        for ld_i in range(16):
            gl = b.val("wdk") - wd_load0
            w = []
            if pr == 0 and ld_i == 0:
                w = [("st2", getattr(b, "st2_gate", 0))]
            wdk_prev = b.val("wdk") - 2
            if wdk_prev > 0:
                w.append(("wdk", wdk_prev))
            buf = gl % 3
            b.dma(t["wdb"][:, buf, :, :], wdT_s[ld_i * 2:ld_i * 2 + 2].transpose([1, 0, 2]),
                  waits=w, sem="wd")
            if getattr(b, "wd_log", None) is not None:
                b.dma(b.wd_log[gl], t["wdb"][:, buf, :, :], sem="st")
            wdv = b.val("wd")
            for kk in range(2):
                k = ld_i * 2 + kk
                w2 = [("wd", wdv)] if kk == 0 else []
                if k == 0:
                    w2 += [("eva2", b.val("eva2")), ("tp", q2tp[2 * pr + 1])]
                    if pr == 0:
                        w2 += [("eva1", b.val("eva1") - 2), ("evb2", b.val("evb2"))]
                for ci, tc in enumerate((tc_e, tc_o)):
                    pb = 4 + 2 * ci
                    qfm = t["q2fmA"] if tc < 2 else t["q2fmB"]
                    for h in range(2):
                        b.mm(b.ps[:, pb + h, :], qfm[:, k, (tc % 2) * 128:(tc % 2 + 1) * 128],
                             t["wdb"][:, buf, kk, h * 512:(h + 1) * 512],
                             k == 0, k == 31,
                             waits=w2 if (ci == 0 and h == 0) else None, sem="mma2",
                             do_inc=(k == 31 and ci == 1 and h == 1))
            b.mm(b.ps[0:1, 0, 0:1], t["conek"][:], t["conek"][:], True, True,
                 sem="wdk", do_inc=True)
        mmv = b.val("mma2")
        for ci, tc in enumerate((tc_e, tc_o)):
            pb = 4 + 2 * ci
            for h in range(2):
                b.v_op(lambda e, tc=tc, h=h, pb=pb: e.scalar_tensor_tensor(
                    t["x"][:, tc, h * 512:(h + 1) * 512], b.ps[:, pb + h, :],
                    t["stats"][:, S + 72 + tc:S + 73 + tc],
                    t["x"][:, tc, h * 512:(h + 1) * 512], ALU.mult, ALU.add),
                    waits=[("mma2", mmv)], sem="eva2")
            evA2[tc] = b.val("eva2")
        if pr == 0 and after_pr0 is not None:
            after_pr0(evA2)
    b.prev_a1g = a1_groups
    b.last_evr3 = evr_tc[3]
    return evA2

def lnb_front(b, S, tc, evA2_tc, sq_scs, sc0):
    t = b.t
    src = t["x"][:, tc, :]
    xst = t["xs2"] if tc % 2 else t["xs"]
    b.v_op(lambda e, src=src, tc=tc: e.reduce_sum(
        t["stats"][:, S + tc:S + tc + 1], src, AX.X), waits=[("eva2", evA2_tc)])
    b.v_op(lambda e, tc=tc: e.tensor_single_scalar(
        t["stats"][:, S + 4 + tc:S + 5 + tc], t["stats"][:, S + tc:S + tc + 1],
        1.0 / 1024.0, ALU.mult))
    wxs = [("sc", sq_scs[-2])] if len(sq_scs) >= 2 else [("sc", sc0)]
    evx = b.v_op(lambda e, src=src, tc=tc, xst=xst: e.tensor_scalar(
        xst[:], src, t["stats"][:, S + 4 + tc:S + 5 + tc], None, ALU.subtract),
        waits=wxs)
    sq = b.s_op(lambda e, tc=tc, xst=xst: e.activation(
        t["q2d"][:], xst[:], AF.Square,
        accum_out=t["stats"][:, S + 8 + tc:S + 9 + tc]), waits=[("ev", evx)])
    sq_scs.append(sq)
    b.s_op(lambda e, tc=tc: e.activation(t["stats"][:, S + 12 + tc:S + 13 + tc],
                                         t["stats"][:, S + 8 + tc:S + 9 + tc],
                                         AF.Ln, bias=t["ceps"][:, 0:1], scale=1.0 / 1024.0))
    return b.s_op(lambda e, tc=tc: e.activation(t["stats"][:, S + 16 + tc:S + 17 + tc],
                                                t["stats"][:, S + 12 + tc:S + 13 + tc],
                                                AF.Exp, scale=-0.5))

def lnb_backc(b, S, tc, sc_exp, w_tp):
    t = b.t
    src = t["x"][:, tc, :]
    xst = t["xs2"] if tc % 2 else t["xs"]
    ev_st = b.v_op(lambda e, src=src, tc=tc, xst=xst: e.scalar_tensor_tensor(
        src, xst[:], t["stats"][:, S + 16 + tc:S + 17 + tc], src,
        ALU.mult, ALU.add), waits=[("sc", sc_exp)])
    xn1 = t["scrx"][:, tc % 2, :]
    b.v_op(lambda e, xn1=xn1, tc=tc, xst=xst: e.scalar_tensor_tensor(
        xn1, xst[:], t["stats"][:, S + 16 + tc:S + 17 + tc], t["gob"][:],
        ALU.mult, ALU.mult), drain=False)
    evq = b.v_op(lambda e, xn1=xn1, tc=tc: e.tensor_tensor(
        t["q1xn"][:, tc % 2, :], xn1, t["bob"][:], ALU.add), waits=[w_tp])
    return ev_st, evq

def phase_b(b, S, tile, evA2, x1t_dst, yz_dst, lnb=None, mid_hook=None):
    t = b.t
    j = tile % 2
    if not lnb:
        lnb = {"sq_scs": [], "sc0": b.val("sc"), "exps": {}, "backs": {}}
        for tc in (0, 1):
            lnb["exps"][tc] = lnb_front(b, S, tc, evA2[tc], lnb["sq_scs"], lnb["sc0"])
        for tc in (0, 1):
            lnb["backs"][tc] = lnb_backc(b, S, tc, lnb["exps"][tc], ("tp", b.val("tp")))
    sq_scs, exps, backs = lnb["sq_scs"], lnb["exps"], lnb["backs"]
    ldp = b.dma(t["winPb"][:].rearrange("p m k c -> p (m k c)"),
                b.d["winP_s"][:].rearrange("p m k c -> p (m k c)"),
                waits=[("ev", b.val("ev")), ("eva1", b.val("eva1")),
                       ("tp", b.val("tp"))])
    xn_tps = []
    for tc in (0, 1):
        ev_st, evq = backs[tc]
        b.dma(x1t_dst(tc), t["x"][:, tc, :], waits=[("ev", ev_st)], sem="st")
        xn_tps.append(b.dma(t["actfm"][:, :, tc * 128:(tc + 1) * 128],
                            t["q1xn"][:, tc % 2, :],
                            waits=[("ev", evq)], sem="tp", transpose=True))
    ldz = b.dma(t["winZb"][:].rearrange("p k z -> p (k z)"),
                b.d["winZ_s"][:].rearrange("p k z -> p (k z)"))
    for tc in (2, 3):
        exps[tc] = lnb_front(b, S, tc, evA2[tc], sq_scs, lnb["sc0"])
    for tc in (2, 3):
        ev_st, evq = lnb_backc(b, S, tc, exps[tc], ("tp", xn_tps[tc - 2]))
        b.dma(x1t_dst(tc), t["x"][:, tc, :], waits=[("ev", ev_st)], sem="st")
        xn_tps.append(b.dma(t["actfm"][:, :, tc * 128:(tc + 1) * 128],
                            t["q1xn"][:, tc % 2, :],
                            waits=[("ev", evq)], sem="tp", transpose=True))
    if mid_hook is not None:
        mid_hook()
    for m in range(8):
        pb = m % 2
        if m == 0:
            w = [("ld", ldp), ("tp", xn_tps[3])] + b.ejoin()
        else:
            w = [("evb", max(0, b.val("evb") - 1))]
        for k in range(8):
            b.mm(b.ps[:, pb, :], t["winPb"][:, m, k, :], t["actfm"][:, k, :],
                 k == 0, k == 7, waits=w if k == 0 else None, sem="mmb")
        mmv = b.val("mmb")
        b.s_op(lambda e, m=m, pb=pb: e.activation(
            t["xp"][:, m, :], b.ps[:, pb, :], AF.Silu, bias=t["binp"][:, m:m + 1]),
            waits=[("mmb", mmv)], sem="evb")
    evb_xp = b.val("evb")
    for k in range(8):
        b.mm(b.ps[0:64, 2, :], t["wxpT"][:, k, :], t["xp"][:, k, :], k == 0, k == 7,
             waits=[("evb", evb_xp)] if k == 0 else None, sem="mmb")
    mmv = b.val("mmb")
    b.v_op(lambda e: e.tensor_scalar(t["u_s"][:], b.ps[0:64, 2, :], t["bxp"][:, 0:1],
                                     None, ALU.add), waits=[("mmb", mmv)], sem="evb")
    init = 0.0 if tile < 2 else t["hcar"][:, j:j + 1]
    b.v_op(lambda e, init=init: e.tensor_tensor_scan(
        t["H"][:], t["abc"][:], t["u_s"][:], init, ALU.mult, ALU.add), sem="evb")
    b.v_op(lambda e: e.tensor_copy(t["hcar"][:, j:j + 1], t["H"][:, 511:512]), sem="evb")
    ev_hb = b.v_op(lambda e: e.tensor_copy(t["Hb"][:], t["H"][:]), sem="evb")
    th_done = {}
    for tc in range(4):
        pbz = 3 + (tc % 2) * 2
        for k in range(8):
            w = []
            if k == 0:
                w = [("evb2", max(0, b.val("evb2") - 4))]
                if tc == 0:
                    w += [("ld", ldz), ("eva2", evA2[3]), ("eva1", b.val("eva1"))]
            b.mm(b.ps[:, pbz, :], t["actfm"][:, k, tc * 128:(tc + 1) * 128],
                 t["winZb"][:, k, 0:512], k == 0, k == 7,
                 waits=w if k == 0 else None, sem="mmb")
            b.mm(b.ps[:, pbz + 1, :], t["actfm"][:, k, tc * 128:(tc + 1) * 128],
                 t["winZb"][:, k, 512:1024], k == 0, k == 7, sem="mmb")
        mmz = b.val("mmb")
        for h in range(2):
            ev = b.v_op(lambda e, h=h, pbz=pbz: e.scalar_tensor_tensor(
                t["silu"][:, h, :], b.ps[:, pbz + h, :], 0.5,
                t["bzh"][:, h * 512:(h + 1) * 512], ALU.mult, ALU.add),
                waits=[("mmb", mmz)], sem="evb2")
            b.s_op(lambda e, tc=tc, h=h: e.activation(
                t["th"][:, tc, h * 512:(h + 1) * 512], t["silu"][:, h, :], AF.Tanh),
                waits=[("evb2", ev)], sem="evb2")
        th_done[tc] = b.val("evb2")
    mm_h0 = {}
    for tc in range(4):
        w = [("evb", ev_hb)]
        if tc == 3:
            w.append(("evb2", th_done[2]))
        b.mm(b.ps[:, tc, :], t["Hb"][:, tc * 128:(tc + 1) * 128], t["dmat"][:, 0:512],
             True, True, waits=w, sem="mmb")
        mm_h0[tc] = b.val("mmb")
    for tc in range(4):
        b.v_op(lambda e, tc=tc: e.scalar_tensor_tensor(
            t["yztm"][:, tc, 0:512], t["th"][:, tc, 0:512], 1.0, b.ps[:, tc, :],
            ALU.add, ALU.mult), waits=[("mmb", mm_h0[tc]), ("evb2", th_done[tc])],
            sem="evb3", drain=False)
        b.mm(b.ps[:, tc, :], t["Hb"][:, tc * 128:(tc + 1) * 128], t["dmat"][:, 512:1024],
             True, True, waits=[("evb3", b.val("evb3"))], sem="mmb")
        mm_h0[tc] = b.val("mmb")
    for tc in range(4):
        b.v_op(lambda e, tc=tc: e.scalar_tensor_tensor(
            t["yztm"][:, tc, 512:1024], t["th"][:, tc, 512:1024], 1.0, b.ps[:, tc, :],
            ALU.add, ALU.mult), waits=[("mmb", mm_h0[tc])], sem="evb3",
            drain=(tc == 3))
    st = b.dma(yz_dst, t["yztm"][:], waits=[("evb3", b.val("evb3"))], sem="st")
    return st

def pass2_wout(b, S, u, yz_srcs, x1t_src, first, tc_hook=None):
    t = b.t
    out_guard = b.val("out")
    if first:
        b.dma(t["woutb"][:].rearrange("p k o -> p (k o)"),
              b.d["woutT_s"][:].rearrange("p k o -> p (k o)"), sem="ld",
              waits=[("tp", b.val("tp")), ("ev", b.val("ev"))])
    w_wout = b.val("ld")
    if callable(yz_srcs):
        yz_srcs = yz_srcs()
    if isinstance(yz_srcs, tuple) and yz_srcs[0] == "reads":
        ldv = yz_srcs[1]
        tp_yz = None
        for tc in range(4):
            tp_yz = b.dma(t["yzfm"][:, :, tc * 128:(tc + 1) * 128], t["yztm2"][:, tc, :],
                          waits=[("ld", ldv), ("mmw", b.val("mmw"))], sem="tp",
                          transpose=True)
    elif isinstance(yz_srcs, int):
        tp_yz = yz_srcs
    else:
        tp_yz = None
        for jj in range(2):
            w = [("mmw", b.val("mmw"))]
            tp_yz = b.dma(t["yzfm"][:, :, jj * 256:(jj + 1) * 256], yz_srcs[jj],
                          waits=w, sem="tp", transpose=True)
    evw = {}
    for tc in range(4):
        par = tc % 2
        ld_x1 = b.dma(t["x1ts"][:, par, :], x1t_src(tc),
                      waits=[("evw", max(0, b.val("evw") - 4)), ("st", b.val("st")),
                             ("evb3", b.val("evb3")), ("eva2", b.val("eva2"))], sem="ld")
        pb = par * 2
        for k in range(8):
            w = []
            if k == 0:
                w = [("tp", tp_yz), ("ld", w_wout)]
                if tc == 0:
                    w += b.ejoin()
                prev = b.val("evw") - 4
                if prev > 0:
                    w.append(("evw", prev))
            b.mm(b.ps[:, pb, :], t["yzfm"][:, k, tc * 128:(tc + 1) * 128],
                 t["woutb"][:, k, 0:512], k == 0, k == 7, waits=w if k == 0 else None, sem="mmw")
            b.mm(b.ps[:, pb + 1, :], t["yzfm"][:, k, tc * 128:(tc + 1) * 128],
                 t["woutb"][:, k, 512:1024], k == 0, k == 7, sem="mmw")
        mmv = b.val("mmw")
        for h in range(2):
            b.v_op(lambda e, h=h, pb=pb: e.tensor_tensor(
                t["ebuf"][:, h * 512:(h + 1) * 512], b.ps[:, pb + h, :],
                t["outb"][:, h * 512:(h + 1) * 512], ALU.add),
                waits=[("mmw", mmv), ("ld", ld_x1)], sem="evw", drain=False)
            b.v_op(lambda e, tc=tc, h=h, par=par: e.tensor_tensor(
                t["x"][:, tc, h * 512:(h + 1) * 512], t["x1ts"][:, par, h * 512:(h + 1) * 512],
                t["ebuf"][:, h * 512:(h + 1) * 512], ALU.add),
                waits=[("out", out_guard)], sem="evw")
        evw[tc] = b.val("evw")
        if tc_hook is not None:
            tc_hook(tc)
    return evw

def weight_aps(nc):
    aps = {}
    names = [("w_gate", [4096, 1024]), ("w_up", [4096, 1024]), ("w_down", [1024, 4096]),
             ("W_in", [2048, 1024]), ("W_xp", [64, 1024]), ("W_out", [1024, 1024]),
             ("Dmat", [64, 1024]), ("A_log", [1, 1, 64]), ("b_in", [2048]), ("b_xp", [64]),
             ("b_out", [1024]), ("gamma", [1024]), ("beta", [1024])]
    for nm, shp in names:
        aps[nm] = nc.dram_tensor(nm, shp, F32, kind="ExternalInput").ap()
    return aps

def dram_scratch(b, nc):
    b.d["wgT_s"] = nc.dram_tensor("wgT_s", [8, 128, 8, 512], BF16).ap()
    b.d["wuT_s"] = nc.dram_tensor("wuT_s", [8, 128, 8, 512], BF16).ap()
    b.d["wdT_s"] = nc.dram_tensor("wdT_s", [32, 128, 1024], BF16).ap()
    b.d["winP_s"] = nc.dram_tensor("winP_s", [128, 8, 8, 128], BF16).ap()
    b.d["winZ_s"] = nc.dram_tensor("winZ_s", [128, 8, 1024], BF16).ap()
    b.d["woutT_s"] = nc.dram_tensor("woutT_s", [128, 8, 1024], BF16).ap()

def setup_b(nc):
    smap = build_smap()
    b = B()
    b.t = {nm: nc.alloc_sbuf_tensor_at(nm, shp, dt, offset=off)
           for nm, (shp, dt, off) in smap.items.items()}
    b.ps = nc.alloc_psum_tensor("ps", [128, 8, 512], F32)
    ident = nc.inline_tensor(np.eye(128, dtype=NPBF), name="ident128")
    b.d["ident"] = ident.ap()
    dram_scratch(b, nc)
    return b

def pe_transpose(b, dst, src_ap, nblk, waits=None, sem="ev"):
    t = b.t
    ev = None
    for blk in range(nblk):
        pb = 2 + blk % 2
        w = list(waits or []) if blk == 0 else []
        prev = b.val("ev") - 1
        if blk >= 2:
            w.append(("ev", b.val("ev") - 1))
        b.mm(b.ps[:, pb, 0:128], src_ap[:, blk * 128:(blk + 1) * 128], t["ident128"][:],
             True, True, waits=w, sem="mm") if False else None
        P = src_ap.shape[0]
        def fn(e, pb=pb, blk=blk, P=P):
            return e.transpose(b.ps[:, pb, 0:P // 2].bitcast(BF16),
                               src_ap[:, blk * 128:(blk + 1) * 128],
                               t["ident128"][0:P, 0:P])
        b.emit("tensor", fn, w, ("mm", 1))
        mmv = b.val("mm")
        ev = b.v_op(lambda e, pb=pb, blk=blk, P=P: e.tensor_copy(
            dst(blk), b.ps[:, pb, 0:P // 2].bitcast(BF16)), waits=[("mm", mmv)],
            drain=(blk == nblk - 1))
    return ev

def emit_pass1(b, x_in, x1t_out, yz_out, quarter_hook=None):
    t = b.t

    def emit_ldx(tile, w):
        q, j = tile // 2, tile % 2
        return b.dma(t["x"][:], x_in[j, q * 512:q * 512 + 512, :]
                     .rearrange("(c p) f -> p c f", p=128), waits=w)

    nxt = {}
    for tile in range(8):
        q, j = tile // 2, tile % 2
        l0 = q * 512
        if tile == 0:
            ldx = emit_ldx(0, [("tp", b.val("tp")), ("ev", b.val("ev")), ("sc", b.val("sc"))])
            pre = ffn_preload(b)
            sq_scs, sc0 = [], b.val("sc")
            for tc in range(4):
                ln_stats_tc(b, 0, tc, sq_scs, sc0, waits=([("ld", ldx)] if tc == 0 else None))
        else:
            pre = nxt["pre"]
        x1v = x1t_out[j].rearrange("(c p) f -> p c f", p=128)
        q1_tps = ln_quant_rest(b, 0)
        lnb = {}

        def after_pr0(evA2, lnb=lnb):
            lnb.update(sq_scs=[], sc0=b.val("sc"), exps={}, backs={})
            for tc in (0, 1):
                lnb["exps"][tc] = lnb_front(b, 0, tc, evA2[tc], lnb["sq_scs"], lnb["sc0"])
            for tc in (0, 1):
                lnb["backs"][tc] = lnb_backc(b, 0, tc, lnb["exps"][tc],
                                             ("tp", b.val("tp")))
        evA2 = ffn_phase(b, 0, q1_tps, pre=pre, after_pr0=after_pr0)

        def mid_hook(tile=tile):
            if tile == 7:
                return
            ldx2 = emit_ldx(tile + 1, [("st", b.val("st"))])
            nxt["pre"] = ffn_preload(b)
            sq_scs, sc0 = [], b.val("sc")
            for tc in range(4):
                ln_stats_tc(b, 0, tc, sq_scs, sc0, waits=([("ld", ldx2)] if tc == 0 else None))
        phase_b(b, 0, tile, evA2,
                x1t_dst=lambda tc, x1v=x1v, l0=l0: x1v[:, (l0 // 128) + tc, :],
                yz_dst=yz_out[q, j].rearrange("(c p) f -> p c f", p=128), lnb=lnb,
                mid_hook=mid_hook)
        if quarter_hook is not None and tile % 2 == 1:
            quarter_hook(tile // 2)
    b.emit("sync", lambda e: e.nop(), [("st", b.val("st")), ("out", b.val("out"))], None)

def emit_pass2(b, yz_in_fn, x1t_src_fn, y_out_fn, prefetch=False):
    t = b.t
    nxt_tp = None
    for u in range(8):
        sq_scs, sc0 = [], b.val("sc")

        def tc_hook(tc, sq_scs=sq_scs, sc0=sc0):
            ln_stats_tc(b, 0, tc, sq_scs, sc0)
        yz = yz_in_fn(u) if nxt_tp is None else nxt_tp
        evw = pass2_wout(b, 0, u, yz, lambda tc, u=u: x1t_src_fn(u, tc),
                         first=(u == 0), tc_hook=tc_hook)
        pre = ffn_preload(b)
        q1_tps = ln_quant_rest(b, 0, out_guard=("out", b.val("out")))
        evA2 = ffn_phase(b, 0, q1_tps, pre=pre)
        if prefetch and u < 7:
            nxt_tp = yz_in_fn(u + 1)()
        for tc in range(4):
            b.dma(y_out_fn(u, tc), t["x"][:, tc, :], waits=[("eva2", evA2[tc])], sem="out")
    b.emit("sync", lambda e: e.nop(), [("out", b.val("out"))], None)

def _build_p1():
    nc = bass.Bass("TRN2", target_bir_lowering=False)
    b = setup_b(nc)
    aps = weight_aps(nc)
    x_in = nc.dram_tensor("x", [2, 2048, 1024], F32, kind="ExternalInput").ap()
    x1t_out = nc.dram_tensor("x1t", [2, 2048, 1024], F32, kind="ExternalOutput").ap()
    yz_out = nc.dram_tensor("yz", [4, 2, 512, 1024], BF16, kind="ExternalOutput").ap()
    prepass_consts(b)
    g1 = prepass_ffn(b, aps["w_gate"], 4096, 1024, b.d["wgT_s"], 0)
    g2 = prepass_ffn(b, aps["w_up"], 4096, 1024, b.d["wuT_s"], 2, gate=g1)
    prepass_ffn(b, aps["w_down"], 1024, 4096, b.d["wdT_s"], 4, gate=g2)
    b.st2_gate = b.val("st2")
    prepass_attn(b, aps, need_producer=True, need_consumer=False)
    emit_pass1(b, x_in, x1t_out, yz_out)
    finalize(b, nc, SEMS)
    return nc

def _prepass_wout(b, aps):
    t = b.t
    for mp in range(0, 8, 2):
        ld = b.dma(t["wstage"][:, 0:2, :],
                   aps["W_out"][mp * 128:(mp + 2) * 128, :].rearrange("(m p) i -> p m i", p=128),
                   waits=[("ev", b.val("ev"))])
        evc = b.v_op(lambda e: e.tensor_copy(
            t["pp_q4"][:].rearrange("p (m f) -> p m f", m=4)[:, 0:2, :],
            t["wstage"][:, 0:2, :]), waits=[("ld", ld), ("tp", b.val("tp"))])
        for mm_ in range(2):
            m = mp + mm_
            evt = pe_transpose(b, lambda blk: t["pp_img"][:, blk, 0:128],
                               t["pp_q4"][:, mm_ * 1024:(mm_ + 1) * 1024], 8,
                               waits=[("ev", evc), ("st", b.val("st"))])
            b.dma(b.d["woutT_s"][:, :, m * 128:(m + 1) * 128], t["pp_img"][:, :, 0:128],
                  waits=[("ev", evt)], sem="st")

def _build_p2():
    nc = bass.Bass("TRN2", target_bir_lowering=False)
    b = setup_b(nc)
    aps = weight_aps(nc)
    x1t_in = nc.dram_tensor("x1t", [2, 2048, 1024], F32, kind="ExternalInput").ap()
    yz_in = nc.dram_tensor("yzin", [8, 2, 256, 1024], BF16, kind="ExternalInput").ap()
    y_out = nc.dram_tensor("y", [2, 2048, 1024], F32, kind="ExternalOutput").ap()
    prepass_consts(b)
    g1 = prepass_ffn(b, aps["w_gate"], 4096, 1024, b.d["wgT_s"], 0)
    g2 = prepass_ffn(b, aps["w_up"], 4096, 1024, b.d["wuT_s"], 2, gate=g1)
    prepass_ffn(b, aps["w_down"], 1024, 4096, b.d["wdT_s"], 4, gate=g2)
    b.st2_gate = b.val("st2")
    prepass_attn(b, aps, need_producer=False, need_consumer=True)
    _prepass_wout(b, aps)

    def yz_src(u):
        return [yz_in[u, jj] for jj in range(2)]

    def x1t_src(u, tc):
        jj, ih = tc // 2, tc % 2
        b2 = 2 * u + jj
        return bass.AP(x1t_in.tensor, (ih * 2048 + b2) * 1024, [[16 * 1024, 128], [1, 1024]])

    def y_dst(u, tc):
        jj, ih = tc // 2, tc % 2
        b2 = 2 * u + jj
        return bass.AP(y_out.tensor, (ih * 2048 + b2) * 1024, [[16 * 1024, 128], [1, 1024]])

    emit_pass2(b, yz_src, x1t_src, y_dst)
    finalize(b, nc, SEMS)
    return nc

def _build_fused():
    nc = bass.Bass("TRN2", target_bir_lowering=False, num_devices=8)
    b = setup_b(nc)
    aps = weight_aps(nc)
    t = b.t
    x_in = nc.dram_tensor("x", [2, 2048, 1024], F32, kind="ExternalInput").ap()
    y_out = nc.dram_tensor("y", [2, 2048, 1024], F32, kind="ExternalOutput").ap()
    x1t_s = nc.dram_tensor("x1t_s", [2, 2048, 1024], F32, kind="Internal").ap()
    yz_send = nc.dram_tensor("yz_send", [4, 2, 512, 1024], BF16, kind="Internal").ap()
    gat = nc.dram_tensor("gat", [4, 8, 2, 512, 1024], BF16, kind="Internal").ap()
    prepass_consts(b)
    g1 = prepass_ffn(b, aps["w_gate"], 4096, 1024, b.d["wgT_s"], 0)
    g2 = prepass_ffn(b, aps["w_up"], 4096, 1024, b.d["wuT_s"], 2, gate=g1)
    prepass_ffn(b, aps["w_down"], 1024, 4096, b.d["wdT_s"], 4, gate=g2)
    b.st2_gate = b.val("st2")
    prepass_attn(b, aps, need_producer=True, need_consumer=True)
    _prepass_wout(b, aps)

    emit_pass1(b, x_in, x1t_s, yz_send, quarter_hook=None)
    for q in range(4):
        def ag_fn(g, q=q):
            return g.collective_compute(
                "AllGather", mybir.AluOpType.bypass,
                replica_groups=[list(range(NCORE))],
                ins=[yz_send[q]], outs=[gat[q]])
        b.emit("gpsimd", ag_fn,
               [("st", b.val("st")), ("tp", b.val("tp")), ("evb3", b.val("evb3"))],
               ("cc", 1))

    b.dyn = {}

    def rank_fn(g):
        rank = g.compute_val(g.partition_id())
        b.dyn["rq"] = nc.s_assert_within(rank // 2, 0, 3)
        b.dyn["rm256"] = nc.s_assert_within((rank % 2) * 256, 0, 256)
        return g.engine_nop()
    b.emit("gpsimd", rank_fn, [("cc", 4)], ("cc", 1))

    def yz_loader(u):
        def load():
            w = [("cc", 5), ("tp", b.val("tp")), ("ev", getattr(b, "last_evr3", 0)),
                 ("mmb", b.val("mmb")), ("st", b.val("st")), ("evb3", b.val("evb3"))]
            for jj in range(2):
                def fn(g, u=u, jj=jj):
                    in_ = gat[ds(b.dyn["rq"], 1), u, jj, ds(b.dyn["rm256"], 256), :]
                    in_ = in_.squeeze(0).rearrange("(c p) f -> p c f", p=128)
                    return g.dma_start(out=t["yztm2"][:, jj * 2:jj * 2 + 2, :], in_=in_)
                fn._is_dma = True
                b.emit("gpsimd", fn, w, ("ld", 16))
                w = []
            ldv = b.ctr["ld"]
            if LAZY_TPS:
                return ("reads", ldv)
            tp = None
            for tc in range(4):
                tp = b.dma(t["yzfm"][:, :, tc * 128:(tc + 1) * 128], t["yztm2"][:, tc, :],
                           waits=[("ld", ldv), ("mmw", b.val("mmw"))], sem="tp",
                           transpose=True)
            return tp
        return load

    def x1t_src(u, tc):
        jj, ih = tc // 2, tc % 2
        b2 = 2 * u + jj
        return bass.AP(x1t_s.tensor, (ih * 2048 + b2) * 1024, [[16 * 1024, 128], [1, 1024]])

    def y_dst(u, tc):
        jj, ih = tc // 2, tc % 2
        b2 = 2 * u + jj
        return bass.AP(y_out.tensor, (ih * 2048 + b2) * 1024, [[16 * 1024, 128], [1, 1024]])

    emit_pass2(b, yz_loader, x1t_src, y_dst, prefetch=PREFETCH)
    finalize(b, nc, SEMS)
    return nc

NCORE = 8

def kernel(L0_attn_x, L0_attn_A_log, L0_attn_Dmat, L0_attn_W_in, L0_attn_b_in,
           L0_attn_W_xp, L0_attn_b_xp, L0_attn_W_out, L0_attn_b_out, L0_attn_gamma,
           L0_attn_beta, L0_ffn_w_gate, L0_ffn_w_up, L0_ffn_w_down):
    wmap = dict(
        w_gate=np.ascontiguousarray(L0_ffn_w_gate, dtype=np.float32),
        w_up=np.ascontiguousarray(L0_ffn_w_up, dtype=np.float32),
        w_down=np.ascontiguousarray(L0_ffn_w_down, dtype=np.float32),
        W_in=np.ascontiguousarray(L0_attn_W_in, dtype=np.float32),
        W_xp=np.ascontiguousarray(L0_attn_W_xp, dtype=np.float32),
        W_out=np.ascontiguousarray(L0_attn_W_out, dtype=np.float32),
        Dmat=np.ascontiguousarray(L0_attn_Dmat, dtype=np.float32),
        A_log=np.ascontiguousarray(L0_attn_A_log, dtype=np.float32),
        b_in=np.ascontiguousarray(L0_attn_b_in, dtype=np.float32),
        b_xp=np.ascontiguousarray(L0_attn_b_xp, dtype=np.float32),
        b_out=np.ascontiguousarray(L0_attn_b_out, dtype=np.float32),
        gamma=np.ascontiguousarray(L0_attn_gamma, dtype=np.float32),
        beta=np.ascontiguousarray(L0_attn_beta, dtype=np.float32),
    )
    x = np.ascontiguousarray(L0_attn_x, dtype=np.float32)
    if FUSED:
        nc = _build_fused()
        in_maps = [dict(wmap, x=np.ascontiguousarray(x[2 * c:2 * c + 2])) for c in range(8)]
        res = run_bass_kernel_spmd(nc, in_maps, core_ids=list(range(8)))
        out = np.stack([np.asarray(res.results[c]["y"]) for c in range(8)])
        return out.reshape(16, 2048, 1024)
    nc1 = _build_p1()
    in_maps = [dict(wmap, x=np.ascontiguousarray(x[2 * c:2 * c + 2])) for c in range(8)]
    res1 = run_bass_kernel_spmd(nc1, in_maps, core_ids=list(range(8)))
    x1t = [res1.results[c]["x1t"] for c in range(8)]
    yz = np.stack([np.asarray(res1.results[c]["yz"]).view(NPBF) if res1.results[c]["yz"].dtype != NPBF
                   else res1.results[c]["yz"] for c in range(8)])
    nc2 = _build_p2()
    in_maps2 = []
    for c in range(8):
        yzin = np.ascontiguousarray(yz[:, c // 2, :, (c % 2) * 256:(c % 2) * 256 + 256, :])
        in_maps2.append(dict(wmap, x1t=np.ascontiguousarray(x1t[c]), yzin=yzin))
    res2 = run_bass_kernel_spmd(nc2, in_maps2, core_ids=list(range(8)))
    out = np.concatenate([np.asarray(res2.results[c]["y"])[None] for c in range(8)], axis=0)
    return out.reshape(16, 2048, 1024)
